# version 02
# baseline (speedup 1.0000x reference)
import contextlib

import numpy as np
import ml_dtypes
import concourse.bass as bass
import concourse.mybir as mybir
from concourse.bass_utils import run_bass_kernel_spmd

F32 = mybir.dt.float32
BF16 = mybir.dt.bfloat16
AF = mybir.ActivationFunctionType
OP = mybir.AluOpType
AX = mybir.AxisListType

D = 2048
LQ = 4096
NB = 4
DIN = 4096
NHC = 32
HP = 64
DS = 128
CH = 256
NCHUNK = LQ // CH
CONVC = NHC * HP + 2 * DS
NCT = CONVC // 128
WAC = CONVC + NHC
NKT = D // 128
TOK = 2048
NE = 8
FF = 1024
SF = 1024
LN_EPS = 1e-5
MRMS_EPS = 1e-5
FRMS_EPS = 1e-6

ENGS = ("sync", "scalar", "vector", "tensor", "gpsimd")
CHAIN = {"scalar": "sC", "vector": "vC", "gpsimd": "gC"}


class Prog:
    def __init__(self, nc):
        self.nc = nc
        self.streams = {e: [] for e in ENGS}
        self.sems = {}
        self.cnt = {}

    def sem(self, name):
        if name not in self.sems:
            self.sems[name] = self.nc.alloc_semaphore(name)
            self.cnt[name] = 0
        return self.sems[name]

    def emit(self, eng, fn):
        self.streams[eng].append(fn)

    def wait(self, eng, sname, val=None):
        h = self.sem(sname)
        v = self.cnt[sname] if val is None else val
        if v > 0:
            self.emit(eng, lambda q, h=h, v=v: q.wait_ge(h, v))
        return v

    def cop(self, eng, fn, serial=True):
        ch = CHAIN[eng]
        h = self.sem(ch)
        prev = self.cnt[ch]
        if serial and prev > 0:
            self.emit(eng, lambda q, h=h, v=prev: q.wait_ge(h, v))
        self.cnt[ch] = prev + 1
        self.emit(eng, lambda q, fn=fn, h=h: fn(q).then_inc(h))
        return prev + 1

    def top(self, fn, mark=False):
        if not mark:
            self.emit("tensor", fn)
            return self.cnt.get("tC", 0)
        h = self.sem("tC")
        self.cnt["tC"] += 1
        self.emit("tensor", lambda q, fn=fn, h=h: fn(q).then_inc(h))
        return self.cnt["tC"]

    def dma(self, eng, dst, src, inc, slow=False):
        h = self.sem(inc)
        self.cnt[inc] += 16
        self.emit(eng, lambda q, d=dst, s=src, sl=slow, h=h: q.dma_start(
            out=d, in_=s, allow_slow_non_contiguous=sl).then_inc(h, 16))
        return self.cnt[inc]

    def val(self, sname):
        return self.cnt.get(sname, 0)


def host_layout(inputs):
    u = np.asarray(inputs["L0_attn_u"], dtype=np.float32)
    W_in = np.asarray(inputs["L0_attn_W_in"], dtype=np.float32)
    conv_w = np.asarray(inputs["L0_attn_conv_w"], dtype=np.float32)
    conv_b = np.asarray(inputs["L0_attn_conv_b"], dtype=np.float32)
    dt_bias = np.asarray(inputs["L0_attn_dt_bias"], dtype=np.float32)
    A_log = np.asarray(inputs["L0_attn_A_log"], dtype=np.float32)
    Dp = np.asarray(inputs["L0_attn_Dp"], dtype=np.float32)
    norm_w = np.asarray(inputs["L0_attn_norm_w"], dtype=np.float32)
    W_out = np.asarray(inputs["L0_attn_W_out"], dtype=np.float32)
    rms_w = np.asarray(inputs["L0_ffn_rms_weight"], dtype=np.float32)
    router_w = np.asarray(inputs["L0_ffn_router_w"], dtype=np.float32)
    w_gate = np.asarray(inputs["L0_ffn_w_gate"], dtype=np.float32)
    w_up = np.asarray(inputs["L0_ffn_w_up"], dtype=np.float32)
    w_down = np.asarray(inputs["L0_ffn_w_down"], dtype=np.float32)
    sw_gate = np.asarray(inputs["L0_ffn_sw_gate"], dtype=np.float32)
    sw_up = np.asarray(inputs["L0_ffn_sw_up"], dtype=np.float32)
    sw_down = np.asarray(inputs["L0_ffn_sw_down"], dtype=np.float32)

    in_maps = []
    for c in range(8):
        b, s = c // 2, c % 2
        hh = slice(32 * s, 32 * s + 32)
        xch = slice(DIN + 2048 * s, DIN + 2048 * s + 2048)
        WA = np.concatenate([W_in[xch], W_in[8192:8448],
                             W_in[8448 + 32 * s: 8448 + 32 * s + 32]], axis=0)
        cw = np.concatenate([conv_w[2048 * s:2048 * s + 2048], conv_w[4096:4352]], axis=0)
        cb = np.concatenate([conv_b[2048 * s:2048 * s + 2048], conv_b[4096:4352]], axis=0)
        m = {
            "uTf": np.ascontiguousarray(u[b].T),
            "uTh": np.ascontiguousarray(u[b, 2048 * s:2048 * s + 2048].T),
            "WA": np.ascontiguousarray(WA.T),
            "Wz": np.ascontiguousarray(W_in[0:DIN].T),
            "Wo": np.ascontiguousarray(W_out.T),
            "convw": np.ascontiguousarray(cw.reshape(NCT, 128, 4).transpose(1, 0, 2).reshape(128, NCT * 4)),
            "convb": np.ascontiguousarray(cb.reshape(NCT, 128).T),
            "dtb": np.ascontiguousarray(dt_bias[hh].reshape(32, 1)),
            "alog": np.ascontiguousarray(A_log[hh].reshape(32, 1)),
            "dpv": np.ascontiguousarray(Dp[hh].reshape(32, 1)),
            "nrmw": np.ascontiguousarray(norm_w.reshape(32, 128).T),
            "rmsw": np.ascontiguousarray(rms_w.reshape(16, 128).T),
            "rw": router_w,
            "wg": w_gate, "wu": w_up, "wd": w_down,
            "swg": sw_gate, "swu": sw_up, "swd": sw_down,
            "selm": np.ascontiguousarray(
                np.repeat(np.array([[1.0 - s, float(s)]], dtype=np.float32), 128, axis=0)),
        }
        in_maps.append(m)
    return in_maps


def build_program(debug=(), stop_after=None):
    nc = bass.Bass("TRN2", target_bir_lowering=False)
    P = Prog(nc)
    dbg = set(debug)
    dbg_aps = {}

    def mkdbg(name, shape, dtype=F32):
        if name in dbg:
            ap = nc.dram_tensor("dbg_" + name, list(shape), dtype, kind="ExternalOutput").ap()
            dbg_aps[name] = ap
            return ap
        return None

    uTf = nc.dram_tensor("uTf", [D, LQ], F32, kind="ExternalInput").ap()
    uTh = nc.dram_tensor("uTh", [D, TOK], F32, kind="ExternalInput").ap()
    WA_d = nc.dram_tensor("WA", [D, WAC], F32, kind="ExternalInput").ap()
    Wz_d = nc.dram_tensor("Wz", [D, DIN], F32, kind="ExternalInput").ap()
    Wo_d = nc.dram_tensor("Wo", [DIN, D], F32, kind="ExternalInput").ap()
    convw_d = nc.dram_tensor("convw", [128, NCT * 4], F32, kind="ExternalInput").ap()
    convb_d = nc.dram_tensor("convb", [128, NCT], F32, kind="ExternalInput").ap()
    dtb_d = nc.dram_tensor("dtb", [32, 1], F32, kind="ExternalInput").ap()
    alog_d = nc.dram_tensor("alog", [32, 1], F32, kind="ExternalInput").ap()
    dp_d = nc.dram_tensor("dpv", [32, 1], F32, kind="ExternalInput").ap()
    nrmw_d = nc.dram_tensor("nrmw", [128, 32], F32, kind="ExternalInput").ap()
    rmsw_d = nc.dram_tensor("rmsw", [128, 16], F32, kind="ExternalInput").ap()
    rw_d = nc.dram_tensor("rw", [D, NE], F32, kind="ExternalInput").ap()
    wg_d = nc.dram_tensor("wg", [NE, D, FF], F32, kind="ExternalInput").ap()
    wu_d = nc.dram_tensor("wu", [NE, D, FF], F32, kind="ExternalInput").ap()
    wd_d = nc.dram_tensor("wd", [NE, FF, D], F32, kind="ExternalInput").ap()
    swg_d = nc.dram_tensor("swg", [D, SF], F32, kind="ExternalInput").ap()
    swu_d = nc.dram_tensor("swu", [D, SF], F32, kind="ExternalInput").ap()
    swd_d = nc.dram_tensor("swd", [SF, D], F32, kind="ExternalInput").ap()
    selm_d = nc.dram_tensor("selm", [128, 2], F32, kind="ExternalInput").ap()
    out_d = nc.dram_tensor("out", [D, TOK], F32, kind="ExternalOutput").ap()

    S_d = nc.dram_tensor("S", [2, D, TOK], BF16, kind="Internal")
    G_d = nc.dram_tensor("G", [2, 2, D, TOK], BF16, kind="Internal")
    gat_d = nc.dram_tensor("gat", [DIN, TOK], BF16, kind="Internal")
    ffn_d = nc.dram_tensor("ffn", [D, TOK], BF16, kind="Internal")

    id_bf_t = nc.inline_tensor(np.eye(128, dtype=ml_dtypes.bfloat16), name="t_id_bf")
    id_f_t = nc.inline_tensor(np.eye(128, dtype=np.float32), name="t_id_f")
    ones_cb_t = nc.inline_tensor(np.ones((128, 1), dtype=ml_dtypes.bfloat16), name="t_ones_cb")
    ones_rf_t = nc.inline_tensor(np.ones((1, 128), dtype=np.float32), name="t_ones_rf")
    psel = np.zeros((2, 128), dtype=np.float32)
    psel[0, 0:64] = 1.0
    psel[1, 64:128] = 1.0
    psel_t = nc.inline_tensor(psel, name="t_psel")
    mj = (np.arange(128)[:, None] <= np.arange(256)[None, :]).astype(ml_dtypes.bfloat16)
    mask_t = nc.inline_tensor(np.ascontiguousarray(mj), name="t_mask")
    ind = (np.arange(32)[:, None] == (np.arange(2048)[None, :] // 64)).astype(ml_dtypes.bfloat16)
    ind_t = nc.inline_tensor(np.ascontiguousarray(ind), name="t_ind")
    ones32_t = nc.inline_tensor(np.ones((32, 128), dtype=np.float32), name="t_ones32")
    SCALAR_CONSTS = [LN_EPS, 1.0, MRMS_EPS, FRMS_EPS, 0.5, 2.0]
    sconst_t = nc.inline_tensor(
        np.ascontiguousarray(np.repeat(np.array([SCALAR_CONSTS], dtype=np.float32), 128, axis=0)),
        name="t_sconst")

    A = nc.alloc_sbuf_tensor
    id_bf = A("id_bf", [128, 128], BF16)
    id_f = A("id_f", [128, 128], F32)
    ones_cb = A("ones_cb", [128, 1], BF16)
    ones_rf = A("ones_rf", [1, 128], F32)
    mask_sb = A("mask_sb", [128, 256], BF16)
    convw_sb = A("convw_sb", [128, NCT * 4], F32)
    convb_sb = A("convb_sb", [128, NCT], F32)
    dtb_sb = A("dtb_sb", [32, 1], F32)
    alog_sb = A("alog_sb", [32, 1], F32)
    nA_sb = A("nA_sb", [32, 1], F32)
    ones32_sb = A("ones32_sb", [32, 128], F32)
    dpB_sb = A("dpB_sb", [128, 16], F32)
    selm_sb = A("selm_sb", [128, 2], F32)
    nrmw_sb = A("nrmw_sb", [128, 32], F32)
    rmsw_sb = A("rmsw_sb", [128, 16], F32)
    row1 = A("row1", [1, 512], F32)
    row2 = A("row2", [1, 512], F32)
    row3 = A("row3", [1, 512], F32)
    sconst_sb = A("sconst_sb", [128, len(SCALAR_CONSTS)], F32)
    for i, v in enumerate(SCALAR_CONSTS):
        nc.const_aps.register(F32, v, sconst_sb.ap()[:, i:i + 1])

    pb = [nc.alloc_psum_tensor(f"pb{i}", [128, 512], F32) for i in range(7)]
    pbT = nc.alloc_psum_tensor("pbT", [128, 1024], BF16)

    for dst, src in [(id_bf, id_bf_t), (id_f, id_f_t), (ones_cb, ones_cb_t),
                     (ones_rf, ones_rf_t), (mask_sb, mask_t),
                     (sconst_sb, sconst_t), (ones32_sb, ones32_t)]:
        P.dma("sync", dst.ap(), src.ap(), "dIn")
    P.dma("sync", convw_sb.ap(), convw_d, "dIn")
    P.dma("sync", convb_sb.ap(), convb_d, "dIn")
    P.dma("sync", dtb_sb.ap(), dtb_d, "dIn")
    P.dma("sync", alog_sb.ap(), alog_d, "dIn")
    P.dma("sync", selm_sb.ap(), selm_d, "dIn")
    P.dma("sync", nrmw_sb.ap(), nrmw_d, "dIn")
    P.dma("sync", rmsw_sb.ap(), rmsw_d, "dIn")
    P.dma("gpsimd", dpB_sb.ap()[0:64, :],
          dp_d[0:32:2, 0:1].rearrange("a b -> b a").partition_broadcast(64).squeeze(1), "dGp", slow=True)
    P.dma("gpsimd", dpB_sb.ap()[64:128, :],
          dp_d[1:32:2, 0:1].rearrange("a b -> b a").partition_broadcast(64).squeeze(1), "dGp", slow=True)
    cload = P.val("dIn")

    P.wait("scalar", "dIn", cload)
    P.cop("scalar", lambda q: q.activation(nA_sb.ap(), alog_sb.ap(), AF.Exp))
    P.wait("vector", "sC")
    P.cop("vector", lambda q: q.tensor_scalar_mul(nA_sb.ap(), nA_sb.ap(), -1.0))

    st_ab = contextlib.ExitStack()
    SB = lambda *a: st_ab.enter_context(nc.sbuf_tensor(*a))
    WA_sb = SB("WA_sb", [128, NKT * WAC], BF16)
    xn_sb = [SB(f"xn_sb{i}", [128, NKT * 256], BF16) for i in range(2)]
    xBC_sb = SB("xBC_sb", [128, NCT * 259], BF16)
    convacc = SB("convacc", [128, 512], F32)
    xT_sb = SB("xT_sb", [128, 2 * 2048], BF16)
    xw_sb = SB("xw_sb", [128, 2 * 2048], BF16)
    usq_sb = xw_sb
    Btok_sb = SB("Btok_sb", [128, 2 * 128], BF16)
    wdtT_sb = SB("wdtT_sb", [128, 2 * 32], F32)
    h_sb = SB("h_sb", [128, NHC * HP], F32)
    hT_bf = SB("hT_bf", [128, NHC * HP], BF16)
    smalls = SB("smalls", [32, 2816], F32)
    flats = SB("flats", [128, 4096], F32)
    u_sb = SB("u_sb", [128, NKT * 256], F32)
    ind_sb = SB("ind_sb", [32, 2048], BF16)
    cdr_sb = SB("cdr_sb", [32, 512], F32)

    sm = {}
    off = 0
    for nm, w in [("dtraw", 256), ("dt", 256), ("cs", 256), ("negcs", 256),
                  ("expf", 256), ("decay", 256), ("wdt", 256), ("lndt", 256),
                  ("tmp1", 256), ("tmp2", 256), ("cd", 1), ("cse", 1)]:
        sm[nm] = off
        off += w

    def SM(name, w=256):
        return smalls.ap()[:, sm[name]:sm[name] + w]

    tap_xn = mkdbg("xn0", [128, NKT * 256], BF16)
    tap_xbc = mkdbg("xbc0", [128, NCT * 256], BF16)
    tap_dt = mkdbg("dt0", [32, 256])
    tap_cs = mkdbg("cs0", [32, 256])
    tap_st = mkdbg("st0", [128, 384], BF16)
    tap_h = mkdbg("h0", [128, NHC * HP])
    tap_yp = mkdbg("yp0", [128, 16 * 256], BF16)

    P.wait("vector", "dGp", P.val("dGp"))
    P.cop("vector", lambda q: q.memset(h_sb.ap(), 0.0))
    for t in range(NCT):
        P.cop("vector", lambda q, t=t: q.memset(xBC_sb.ap()[:, 259 * t:259 * t + 3], 0.0), serial=False)
    P.cop("vector", lambda q: q.memset(flats.ap()[0:1, 0:2048], 1.0), serial=False)
    ab_init_v = P.cop("vector", lambda q: q.memset(flats.ap()[0:3, 2048:4096], 1.0), serial=False)
    P.wait("sync", "dIn", cload)
    P.dma("sync", flats.ap()[32:34, 0:128], psel_t.ap(), "dIn")
    P.dma("sync", ind_sb.ap(), ind_t.ap(), "dIn")
    P.wait("tensor", "dIn", P.val("dIn"))

    with nc.sbuf_tensor("wstage0", [128, WAC], F32) as ws0:
        for kt in range(NKT):
            if kt >= 1:
                P.wait("sync", "vC", P.val("vC"))
            P.dma("sync", ws0.ap(), WA_d[128 * kt:128 * kt + 128, :], "dWA")
            P.wait("vector", "dWA", P.val("dWA"))
            P.cop("vector", lambda q, kt=kt: q.tensor_copy(
                WA_sb.ap()[:, WAC * kt: WAC * kt + WAC], ws0.ap()), serial=False)
    wa_ready_v = P.val("vC")

    e_sb = SB("e_sb", [128, 256], F32)
    ycomb = SB("ycomb", [128, 256], F32)
    ypre_sb = [SB(f"ypre{i}", [128, 16 * 256], BF16) for i in range(2)]
    ST_sb = [SB(f"ST{i}", [128, 384], BF16) for i in range(2)]
    CBTm_sb = SB("CBTm", [128, 384], BF16)

    uTf3 = uTf.rearrange("(kt p) T -> p kt T", p=128)
    last_xn_v = 0
    prev_te_done = 0
    prev_ych_v = 0

    if stop_after == "s_init":
        chunk_range = []
    else:
        chunk_range = list(range(NCHUNK))
    for c in chunk_range:
        xb = c % 2
        half = c // 8
        yb = c % 2

        P.wait("sync", "vC", last_xn_v if c > 0 else wa_ready_v)
        P.dma("sync", u_sb.ap().rearrange("p (kt T) -> p kt T", T=256),
              uTf3[:, :, CH * c: CH * c + CH], "dU")
        P.wait("vector", "dU", P.val("dU"))
        P.wait("vector", "tC", prev_te_done)
        ubf_v = P.cop("vector", lambda q, xb=xb, u=u_sb: q.tensor_copy(xn_sb[xb].ap(), u.ap()))
        P.wait("scalar", "vC", ubf_v)
        P.wait("scalar", "tC", prev_te_done)
        usq_v = P.cop("scalar", lambda q, xb=xb: q.activation(usq_sb.ap(), xn_sb[xb].ap(), AF.Square))
        if c == 0 and stop_after == "s_p0a":
            break
        P.wait("tensor", "vC", ubf_v)
        P.wait("tensor", "sC", usq_v)
        P.wait("tensor", "vC", P.val("vC"))
        for kt in range(NKT):
            P.top(lambda q, kt=kt, xb=xb: q.matmul(
                pb[6].ap()[0:1, 0:256], ones_cb.ap(),
                xn_sb[xb].ap()[:, 256 * kt:256 * kt + 256],
                start=(kt == 0), stop=(kt == NKT - 1)))
        for kt in range(NKT):
            sums_mm = P.top(lambda q, kt=kt: q.matmul(
                pb[6].ap()[0:1, 256:512], ones_cb.ap(),
                usq_sb.ap()[:, 256 * kt:256 * kt + 256],
                start=(kt == 0), stop=(kt == NKT - 1)), mark=(kt == NKT - 1))
        if c == 0 and stop_after == "s_p0b":
            break
        P.wait("vector", "tC", sums_mm)
        P.cop("vector", lambda q: q.tensor_scalar_mul(row1.ap()[0:1, 0:256], pb[6].ap()[0:1, 0:256], 1.0 / D))
        P.cop("vector", lambda q: q.tensor_scalar_mul(row2.ap()[0:1, 0:256], pb[6].ap()[0:1, 256:512], 1.0 / D))
        P.cop("vector", lambda q: q.tensor_mul(row3.ap()[0:1, 0:256], row1.ap()[0:1, 0:256], row1.ap()[0:1, 0:256]))
        stats_v = P.cop("vector", lambda q: q.tensor_sub(row2.ap()[0:1, 0:256], row2.ap()[0:1, 0:256], row3.ap()[0:1, 0:256]))
        P.wait("scalar", "vC", stats_v)
        sq_v = P.cop("scalar", lambda q: q.activation(row3.ap()[0:1, 0:256], row2.ap()[0:1, 0:256], AF.Sqrt, bias=LN_EPS))
        P.wait("vector", "sC", sq_v)
        rstd_v = P.cop("vector", lambda q: q.reciprocal(row3.ap()[0:1, 0:256], row3.ap()[0:1, 0:256]))
        if c == 0 and stop_after == "s_p0c":
            break
        P.wait("tensor", "vC", rstd_v)
        P.wait("tensor", "sC", P.val("sC"))
        P.top(lambda q: q.matmul(pb[5].ap()[:, 0:256], ones_rf.ap(), row1.ap()[0:1, 0:256], start=True, stop=True))
        bc_mm = P.top(lambda q: q.matmul(pb[5].ap()[:, 256:512], ones_rf.ap(), row3.ap()[0:1, 0:256], start=True, stop=True), mark=True)
        P.wait("vector", "tC", bc_mm)
        if c == 0 and stop_after == "s_p0d":
            break
        for kt in range(NKT):
            P.cop("vector", lambda q, kt=kt, u=u_sb: q.tensor_sub(
                u.ap()[:, 256 * kt:256 * kt + 256], u.ap()[:, 256 * kt:256 * kt + 256],
                pb[5].ap()[:, 0:256]), serial=(kt == 0))
        if c == 0 and stop_after == "s_p0e":
            break
        for kt in range(NKT):
            xn_v = P.cop("vector", lambda q, kt=kt, xb=xb, u=u_sb: q.tensor_tensor(
                xn_sb[xb].ap()[:, 256 * kt:256 * kt + 256], u.ap()[:, 256 * kt:256 * kt + 256],
                pb[5].ap()[:, 256:512], OP.mult), serial=(kt == 0))
        last_xn_v = xn_v
        if c == 0 and tap_xn is not None:
            P.wait("sync", "vC", xn_v)
            P.dma("sync", tap_xn, xn_sb[0].ap(), "dTap")
            P.wait("sync", "dTap", P.val("dTap"))
            P.wait("vector", "dTap", P.val("dTap"))
            P.wait("tensor", "dTap", P.val("dTap"))


        P.wait("tensor", "vC", xn_v)
        if c == 0:
            P.wait("tensor", "vC", ab_init_v)
        evac_vals = []
        for mt in range(NCT + 1):
            bank = pb[mt % 2]
            mw = 128 if mt < NCT else 32
            if len(evac_vals) >= 2:
                P.wait("tensor", "sC", evac_vals[-2])
            for kt in range(NKT):
                mm_v = P.top(lambda q, mt=mt, kt=kt, bank=bank, mw=mw, xb=xb: q.matmul(
                    bank.ap()[0:mw, 0:256],
                    WA_sb.ap()[:, WAC * kt + 128 * mt: WAC * kt + 128 * mt + mw],
                    xn_sb[xb].ap()[:, 256 * kt:256 * kt + 256],
                    start=(kt == 0), stop=(kt == NKT - 1)), mark=(kt == NKT - 1))
            P.wait("scalar", "tC", mm_v)
            if mt == 0:
                P.wait("scalar", "tC", prev_te_done)
                P.wait("scalar", "vC", prev_ych_v)
            if mt < NCT:
                ev = P.cop("scalar", lambda q, mt=mt, bank=bank: q.copy(
                    xBC_sb.ap()[:, 259 * mt + 3:259 * mt + 259], bank.ap()[:, 0:256]))
            else:
                ev = P.cop("scalar", lambda q, bank=bank: q.copy(
                    SM("dtraw"), bank.ap()[0:32, 0:256]))
            evac_vals.append(ev)
        inproj_evac = evac_vals[-1]

        if c == 0 and stop_after == "s_inproj":
            break
        P.wait("vector", "sC", inproj_evac)
        silu_vals = []
        for t in range(NCT):
            ca = convacc.ap()[:, 256 * (t % 2):256 * (t % 2) + 256]
            if t >= 2:
                P.wait("vector", "sC", silu_vals[t - 2])
            P.cop("vector", lambda q, t=t, ca=ca: q.tensor_scalar(
                ca, xBC_sb.ap()[:, 259 * t + 0:259 * t + 256],
                convw_sb.ap()[:, 4 * t + 0:4 * t + 1], None, OP.mult))
            for k in (1, 2, 3):
                P.cop("vector", lambda q, t=t, k=k, ca=ca: q.scalar_tensor_tensor(
                    ca, xBC_sb.ap()[:, 259 * t + k:259 * t + k + 256],
                    convw_sb.ap()[:, 4 * t + k:4 * t + k + 1], ca, OP.mult, OP.add))
            P.cop("vector", lambda q, t=t: q.tensor_copy(
                xBC_sb.ap()[:, 259 * t + 0:259 * t + 3], xBC_sb.ap()[:, 259 * t + 256:259 * t + 259]))
            vv = P.val("vC")
            P.wait("scalar", "vC", vv)
            sv = P.cop("scalar", lambda q, t=t, ca=ca: q.activation(
                xBC_sb.ap()[:, 259 * t + 3:259 * t + 259], ca,
                AF.Silu, bias=convb_sb.ap()[:, t:t + 1]))
            silu_vals.append(sv)
        conv_done = P.val("sC")
        if c == 0 and tap_xbc is not None:
            P.wait("sync", "sC", conv_done)
            P.dma("sync", tap_xbc.rearrange("p (t T) -> p t T", T=256),
                  xBC_sb.ap().rearrange("p (t T) -> p t T", T=259)[:, :, 3:259], "dTap")
            P.wait("sync", "dTap", P.val("dTap"))
            P.wait("scalar", "dTap", P.val("dTap"))
            P.wait("vector", "dTap", P.val("dTap"))

        if c == 0 and stop_after == "s_conv":
            break
        P.wait("vector", "sC", inproj_evac)
        P.cop("vector", lambda q: q.tensor_scalar(SM("tmp1"), SM("dtraw"), dtb_sb.ap()[0:32, 0:1], None, OP.add))
        P.cop("vector", lambda q: q.tensor_scalar(SM("tmp2"), SM("tmp1"), 0.0, None, OP.max))
        relu_v = P.val("vC")
        P.wait("scalar", "vC", relu_v)
        P.cop("scalar", lambda q: q.activation(SM("dt"), SM("tmp1"), AF.Abs))
        P.cop("scalar", lambda q: q.activation(SM("dt"), SM("dt"), AF.Exp, scale=-1.0))
        ln_v = P.cop("scalar", lambda q: q.activation(SM("dt"), SM("dt"), AF.Ln, bias=1.0))
        P.wait("vector", "sC", ln_v)
        dt_v = P.cop("vector", lambda q: q.tensor_add(SM("dt"), SM("dt"), SM("tmp2")))
        P.wait("scalar", "vC", dt_v)
        P.cop("scalar", lambda q: q.activation(SM("lndt"), SM("dt"), AF.Ln))
        P.cop("vector", lambda q: q.tensor_scalar(SM("tmp1"), SM("dt"), nA_sb.ap()[0:32, 0:1], None, OP.mult))
        P.cop("vector", lambda q: q.tensor_tensor_scan(SM("cs"), SM("tmp1"), SM("tmp1"), 0.0, OP.add, OP.bypass))
        P.cop("vector", lambda q: q.tensor_scalar_mul(SM("negcs"), SM("cs"), -1.0))
        cse_v = P.cop("vector", lambda q: q.tensor_copy(SM("cse", 1), smalls.ap()[:, sm["cs"] + 255:sm["cs"] + 256]))
        P.wait("scalar", "vC", cse_v)
        P.cop("scalar", lambda q: q.activation(SM("expf"), SM("cs"), AF.Exp))
        P.cop("scalar", lambda q: q.activation(SM("decay"), SM("cs"), AF.Exp, scale=-1.0, bias=SM("cse", 1)))
        dexp_v = P.cop("scalar", lambda q: q.activation(SM("cd", 1), SM("cse", 1), AF.Exp))
        P.wait("vector", "sC", dexp_v)
        dtpath_v = P.cop("vector", lambda q: q.tensor_mul(SM("wdt"), SM("decay"), SM("dt")))
        if c == 0 and tap_dt is not None:
            P.wait("sync", "vC", dtpath_v)
            P.dma("sync", tap_dt, SM("dt"), "dTap")
            P.wait("sync", "dTap", P.val("dTap"))
            P.dma("sync", tap_cs, SM("cs"), "dTap")
            P.wait("sync", "dTap", P.val("dTap"))
            P.wait("scalar", "dTap", P.val("dTap"))
            P.wait("vector", "dTap", P.val("dTap"))

        if c == 0 and stop_after == "s_dt":
            break
        P.wait("tensor", "vC", dtpath_v)
        for jt in range(2):
            wdtT_mm = P.top(lambda q, jt=jt: q.matmul(
                pb[2].ap()[0:128, 32 * jt:32 * jt + 32], SM("wdt")[0:32, 128 * jt:128 * jt + 128],
                id_f.ap()[0:32, 0:32], start=True, stop=True, is_transpose=True), mark=(jt == 1))
        P.wait("vector", "tC", wdtT_mm)
        wdtT_v = P.cop("vector", lambda q: q.tensor_copy(wdtT_sb.ap()[:, 0:64], pb[2].ap()[:, 0:64]))
        if c == 0 and stop_after == "s_tr1":
            break
        P.wait("tensor", "sC", conv_done)
        tp_banks = [pbT.ap()[:, 0:128], pb[3].ap().bitcast(BF16)[:, 0:128]]
        tevac = []
        for t in range(17):
            for jt in range(2):
                tpos = 2 * t + jt
                bk = tp_banks[tpos % 2]
                if tpos >= 2:
                    P.wait("tensor", "vC", tevac[tpos - 2])
                tr_mm = P.top(lambda q, t=t, jt=jt, bk=bk: q.matmul(
                    bk, xBC_sb.ap()[:, 259 * t + 3 + 128 * jt: 259 * t + 3 + 128 * jt + 128],
                    id_bf.ap(), start=True, stop=True, is_transpose=True), mark=True)
                P.wait("vector", "tC", tr_mm)
                if t < 16:
                    ev = P.cop("vector", lambda q, t=t, jt=jt, bk=bk: q.tensor_copy(
                        xT_sb.ap()[:, 2048 * jt + 128 * t: 2048 * jt + 128 * t + 128],
                        bk), serial=False)
                else:
                    ev = P.cop("vector", lambda q, jt=jt, bk=bk: q.tensor_copy(
                        Btok_sb.ap()[:, 128 * jt:128 * jt + 128],
                        bk), serial=False)
                tevac.append(ev)
                if c == 0 and stop_after == "s_tr2" and t == 0:
                    break
            if c == 0 and stop_after == "s_tr2" and t == 0:
                break
        if c == 0 and stop_after == "s_tr2":
            break
        P.wait("vector", "vC", wdtT_v)
        for jt in range(2):
            for h in range(NHC):
                xw_v = P.cop("vector", lambda q, jt=jt, h=h: q.tensor_scalar(
                    xw_sb.ap()[:, 2048 * jt + 64 * h: 2048 * jt + 64 * h + 64],
                    xT_sb.ap()[:, 2048 * jt + 64 * h: 2048 * jt + 64 * h + 64],
                    wdtT_sb.ap()[:, 32 * jt + h:32 * jt + h + 1], None, OP.mult),
                    serial=(jt == 0 and h == 0))

        if c == 0 and stop_after == "s_transp":
            break
        Bcm = xBC_sb.ap()[:, 259 * 16 + 3: 259 * 16 + 259]
        Ccm = xBC_sb.ap()[:, 259 * 17 + 3: 259 * 17 + 259]
        P.wait("tensor", "vC", P.val("vC"))
        P.top(lambda q: q.matmul(pb[2].ap()[:, 0:256], Bcm[:, 0:128], Ccm, start=True, stop=True))
        cbt_mm = P.top(lambda q: q.matmul(pb[3].ap()[:, 0:128], Bcm[:, 128:256], Ccm[:, 128:256], start=True, stop=True), mark=True)
        P.wait("vector", "tC", cbt_mm)
        P.cop("vector", lambda q: q.tensor_tensor(CBTm_sb.ap()[:, 0:256], pb[2].ap()[:, 0:256], mask_sb.ap()[:, 0:256], OP.mult))
        cbt_v = P.cop("vector", lambda q: q.tensor_tensor(CBTm_sb.ap()[:, 256:384], pb[3].ap()[:, 0:128], mask_sb.ap()[:, 0:128], OP.mult))

        hTb_v = P.cop("vector", lambda q: q.tensor_copy(hT_bf.ap(), h_sb.ap()))

        if c == 0 and stop_after == "s_cbt":
            break
        for qtr in range(4):
            P.wait("sync", "vC", dtpath_v)
            P.wait("sync", "tC", P.val("tC"))
            P.dma("sync", flats.ap()[1:2, 0:2048], smalls.ap()[8 * qtr:8 * qtr + 8, sm["negcs"]:sm["negcs"] + 256], "dFl")
            P.dma("sync", flats.ap()[2:3, 0:2048], smalls.ap()[8 * qtr:8 * qtr + 8, sm["lndt"]:sm["lndt"] + 256], "dFl")
            P.dma("sync", flats.ap()[0:1, 2048:4096], smalls.ap()[8 * qtr:8 * qtr + 8, sm["cs"]:sm["cs"] + 256], "dFl")
            P.dma("sync", flats.ap()[32:33, 2048:4096], smalls.ap()[8 * qtr:8 * qtr + 8, sm["expf"]:sm["expf"] + 256], "dFl")
            if qtr < 3:
                P.dma("sync", flats.ap()[33:34, 2048:4096],
                      smalls.ap()[8 * qtr + 1:8 * qtr + 9, sm["expf"]:sm["expf"] + 256], "dFl")
            else:
                P.dma("sync", flats.ap()[33:34, 2048:3840],
                      smalls.ap()[8 * qtr + 1:8 * qtr + 8, sm["expf"]:sm["expf"] + 256], "dFl")
            fl_v = P.val("dFl")
            P.wait("tensor", "dFl", fl_v)

            P.wait("vector", "sC", dexp_v)
            cdr_v = P.cop("vector", lambda q, qtr=qtr: q.tensor_scalar(
                cdr_sb.ap(), ind_sb.ap()[:, 512 * qtr:512 * qtr + 512], SM("cd", 1), None, OP.mult))
            P.wait("tensor", "vC", cdr_v)
            cdb_mm = P.top(lambda q: q.matmul(
                pb[6].ap()[:, 0:512], ones32_sb.ap(), cdr_sb.ap(), start=True, stop=True), mark=True)
            P.wait("vector", "tC", cdb_mm)
            P.wait("vector", "vC", hTb_v)
            hsc_v = P.cop("vector", lambda q, qtr=qtr: q.tensor_tensor(
                h_sb.ap()[:, 512 * qtr:512 * qtr + 512], h_sb.ap()[:, 512 * qtr:512 * qtr + 512],
                pb[6].ap()[:, 0:512], OP.mult))
            P.wait("tensor", "vC", hsc_v)
            for jt in range(2):
                st_mm = P.top(lambda q, jt=jt, qtr=qtr: q.matmul(
                    pb[6].ap()[:, 0:512], Btok_sb.ap()[:, 128 * jt:128 * jt + 128],
                    xw_sb.ap()[:, 2048 * jt + 512 * qtr: 2048 * jt + 512 * qtr + 512],
                    start=(jt == 0), stop=(jt == 1)), mark=(jt == 1))
            P.wait("vector", "tC", st_mm)
            P.cop("vector", lambda q, qtr=qtr: q.tensor_tensor(
                h_sb.ap()[:, 512 * qtr:512 * qtr + 512], h_sb.ap()[:, 512 * qtr:512 * qtr + 512],
                pb[6].ap()[:, 0:512], OP.add))

            for pp in range(4):
                g = 4 * qtr + pp
                j0 = 2 * pp
                P.wait("tensor", "vC", P.val("vC"))
                for hi in range(2):
                    col = 256 * (j0 + hi)
                    bankS = pb[2] if hi == 0 else pb[3]
                    P.top(lambda q, col=col, bankS=bankS: q.matmul(
                        bankS.ap()[:, 0:256], flats.ap()[0:3, col:col + 128],
                        flats.ap()[0:3, 2048 + col:2048 + col + 256], start=True, stop=True))
                    seg_mm = P.top(lambda q, col=col, bankS=bankS: q.matmul(
                        bankS.ap()[:, 256:384], flats.ap()[0:3, col + 128:col + 256],
                        flats.ap()[0:3, 2048 + col + 128:2048 + col + 256], start=True, stop=True), mark=True)
                P.wait("scalar", "tC", seg_mm)
                for hi in range(2):
                    bankS = pb[2] if hi == 0 else pb[3]
                    P.cop("scalar", lambda q, hi=hi, bankS=bankS: q.activation(
                        ST_sb[hi].ap()[:, 0:256], bankS.ap()[:, 0:256], AF.Exp))
                    seg_e_v = P.cop("scalar", lambda q, hi=hi, bankS=bankS: q.activation(
                        ST_sb[hi].ap()[:, 256:384], bankS.ap()[:, 256:384], AF.Exp))
                P.wait("vector", "sC", seg_e_v)
                P.wait("vector", "vC", cbt_v)
                for hi in range(2):
                    st_v = P.cop("vector", lambda q, hi=hi: q.scalar_tensor_tensor(
                        ST_sb[hi].ap()[:, 0:384], ST_sb[hi].ap()[:, 0:384], 1e30,
                        CBTm_sb.ap()[:, 0:384], OP.min, OP.mult), serial=(hi == 0))
                if c == 0 and g == 0 and tap_st is not None:
                    P.wait("sync", "vC", st_v)
                    P.dma("sync", tap_st, ST_sb[0].ap(), "dTap")
                    P.wait("sync", "dTap", P.val("dTap"))
                    P.wait("scalar", "dTap", P.val("dTap"))
                    P.wait("vector", "dTap", P.val("dTap"))

                P.wait("tensor", "vC", hTb_v)
                P.wait("tensor", "sC", P.val("sC"))
                P.wait("tensor", "vC", P.val("vC"))
                P.top(lambda q, g=g: q.matmul(
                    pb[4].ap()[:, 256:512], hT_bf.ap()[:, 128 * g:128 * g + 128], Ccm,
                    start=True, stop=True))
                P.top(lambda q, j0=j0: q.matmul(
                    pb[5].ap()[:, 256:512], flats.ap()[32:34, 0:128],
                    flats.ap()[32:34, 2048 + 256 * j0:2048 + 256 * j0 + 256], start=True, stop=True))
                P.wait("tensor", "vC", st_v)
                for hi in range(2):
                    P.top(lambda q, hi=hi, g=g: q.matmul(
                        pb[4].ap()[64 * hi:64 * hi + 64, 0:256],
                        xT_sb.ap()[:, 64 * (2 * g + hi): 64 * (2 * g + hi) + 64],
                        ST_sb[hi].ap()[:, 0:256],
                        start=True, stop=False, tile_position=(0, 64 * hi),
                        skip_group_check=True))
                for hi in range(2):
                    y_mm = P.top(lambda q, hi=hi, g=g: q.matmul(
                        pb[4].ap()[64 * hi:64 * hi + 64, 128:256],
                        xT_sb.ap()[:, 2048 + 64 * (2 * g + hi): 2048 + 64 * (2 * g + hi) + 64],
                        ST_sb[hi].ap()[:, 256:384],
                        start=False, stop=True, tile_position=(0, 64 * hi),
                        skip_group_check=True), mark=(hi == 1))
                P.wait("scalar", "tC", y_mm)
                P.wait("scalar", "vC", P.val("vC"))
                eb_v = P.cop("scalar", lambda q: q.activation(e_sb.ap(), pb[5].ap()[:, 256:512], AF.Copy))
                P.wait("vector", "sC", eb_v)
                if g == 0:
                    P.wait("vector", "dStF", P.val("dStF"))
                P.cop("vector", lambda q: q.tensor_tensor(ycomb.ap(), pb[4].ap()[:, 256:512], e_sb.ap(), OP.mult))
                P.cop("vector", lambda q: q.tensor_tensor(ycomb.ap(), ycomb.ap(), pb[4].ap()[:, 0:256], OP.add))
                ych_v = P.cop("vector", lambda q, g=g, yb=yb: q.scalar_tensor_tensor(
                    ypre_sb[yb].ap()[:, 256 * g:256 * g + 256],
                    xBC_sb.ap()[:, 259 * g + 3:259 * g + 259],
                    dpB_sb.ap()[:, g:g + 1], ycomb.ap(), OP.mult, OP.add))
        if c == 0 and tap_yp is not None:
            P.wait("sync", "vC", ych_v)
            P.dma("sync", tap_yp, ypre_sb[0].ap(), "dTap")
            P.wait("sync", "dTap", P.val("dTap"))
            P.wait("scalar", "dTap", P.val("dTap"))
            P.wait("vector", "dTap", P.val("dTap"))
        if c == 0 and tap_h is not None:
            P.wait("sync", "vC", P.val("vC"))
            P.dma("sync", tap_h, h_sb.ap(), "dTap")
            P.wait("sync", "dTap", P.val("dTap"))

        P.wait("scalar", "vC", ych_v)
        P.dma("scalar", S_d.ap()[half, :, 256 * (c % 8):256 * (c % 8) + 256]
              .rearrange("(g p) T -> p g T", p=128),
              ypre_sb[yb].ap().rearrange("p (g T) -> p g T", T=256), "dStF")

        prev_te_done = P.val("tC")
        prev_ych_v = ych_v
        if stop_after == f"chunk{c}":
            break

    st_ab.close()
    ab_stash_dma = P.val("dStF")

    ag_v = 0
    if stop_after is None:
        P.wait("gpsimd", "dStF", ab_stash_dma)
        ag_v = P.cop("gpsimd", lambda q: q.collective_compute(
            "AllGather", OP.bypass, replica_groups=[[0, 1], [2, 3], [4, 5], [6, 7]],
            ins=[S_d.ap()], outs=[G_d.ap()]))

    if stop_after is not None:
        zo = nc.alloc_sbuf_tensor("zo", [128, 512], F32)
        zv = P.cop("vector", lambda q: q.memset(zo.ap(), 0.0))
        P.wait("scalar", "vC", zv)
        for mt in range(16):
            for tb in range(4):
                P.dma("scalar", out_d[128 * mt:128 * mt + 128, 512 * tb:512 * tb + 512], zo.ap(), "dOut")
        P.wait("scalar", "dOut", P.val("dOut"))
        P.wait("scalar", "dTap", P.val("dTap"))
        P.wait("scalar", "dStF", P.val("dStF"))
    else:
        def fence():
            tv, vv, sv = P.val("tC"), P.val("vC"), P.val("sC")
            for e in ENGS:
                P.wait(e, "tC", tv)
                P.wait(e, "vC", vv)
                P.wait(e, "sC", sv)

        uTh3 = uTh.rearrange("(kt p) T -> p kt T", p=128)

        def ln_block(cb, dst_sb, dst_col0, rms_adjust, last_u_free, usb, ubf, usq):
            P.wait("sync", "vC", last_u_free)
            P.dma("sync", usb.ap().rearrange("p (kt T) -> p kt T", T=256),
                  uTh3[:, :, 256 * cb: 256 * cb + 256], "dU")
            P.wait("vector", "dU", P.val("dU"))
            ub_v = P.cop("vector", lambda q: q.tensor_copy(ubf.ap(), usb.ap()))
            P.wait("scalar", "vC", ub_v)
            us_v = P.cop("scalar", lambda q: q.activation(usq.ap(), ubf.ap(), AF.Square))
            P.wait("tensor", "vC", ub_v)
            P.wait("tensor", "sC", us_v)
            P.wait("tensor", "vC", P.val("vC"))
            for kt in range(NKT):
                P.top(lambda q, kt=kt: q.matmul(
                    pb[6].ap()[0:1, 0:256], ones_cb.ap(),
                    ubf.ap()[:, 256 * kt:256 * kt + 256],
                    start=(kt == 0), stop=(kt == NKT - 1)))
            for kt in range(NKT):
                smm = P.top(lambda q, kt=kt: q.matmul(
                    pb[6].ap()[0:1, 256:512], ones_cb.ap(),
                    usq.ap()[:, 256 * kt:256 * kt + 256],
                    start=(kt == 0), stop=(kt == NKT - 1)), mark=(kt == NKT - 1))
            P.wait("vector", "tC", smm)
            P.cop("vector", lambda q: q.tensor_scalar_mul(row1.ap()[0:1, 0:256], pb[6].ap()[0:1, 0:256], 1.0 / D))
            P.cop("vector", lambda q: q.tensor_scalar_mul(row2.ap()[0:1, 0:256], pb[6].ap()[0:1, 256:512], 1.0 / D))
            P.cop("vector", lambda q: q.tensor_mul(row3.ap()[0:1, 0:256], row1.ap()[0:1, 0:256], row1.ap()[0:1, 0:256]))
            sv2 = P.cop("vector", lambda q: q.tensor_sub(row2.ap()[0:1, 0:256], row2.ap()[0:1, 0:256], row3.ap()[0:1, 0:256]))
            if rms_adjust:
                sv2 = P.cop("vector", lambda q: q.tensor_scalar(
                    row2.ap()[0:1, 0:256], row2.ap()[0:1, 0:256],
                    1.0 + FRMS_EPS, FRMS_EPS * LN_EPS, OP.mult, OP.add))
                bias = 0.0
            else:
                bias = LN_EPS
            P.wait("scalar", "vC", sv2)
            sq2 = P.cop("scalar", lambda q, bias=bias: q.activation(
                row3.ap()[0:1, 0:256], row2.ap()[0:1, 0:256], AF.Sqrt, bias=bias))
            P.wait("vector", "sC", sq2)
            rs_v = P.cop("vector", lambda q: q.reciprocal(row3.ap()[0:1, 0:256], row3.ap()[0:1, 0:256]))
            P.wait("tensor", "vC", rs_v)
            P.wait("tensor", "sC", P.val("sC"))
            P.top(lambda q: q.matmul(pb[5].ap()[:, 0:256], ones_rf.ap(), row1.ap()[0:1, 0:256], start=True, stop=True))
            bmm = P.top(lambda q: q.matmul(pb[5].ap()[:, 256:512], ones_rf.ap(), row3.ap()[0:1, 0:256], start=True, stop=True), mark=True)
            P.wait("vector", "tC", bmm)
            for kt in range(NKT):
                P.cop("vector", lambda q, kt=kt: q.tensor_sub(
                    usb.ap()[:, 256 * kt:256 * kt + 256], usb.ap()[:, 256 * kt:256 * kt + 256],
                    pb[5].ap()[:, 0:256]), serial=(kt == 0))
            for kt in range(NKT):
                outc = dst_col0 + kt * TOK + 256 * cb
                hv = P.cop("vector", lambda q, kt=kt, outc=outc: q.tensor_tensor(
                    dst_sb.ap()[:, outc:outc + 256], usb.ap()[:, 256 * kt:256 * kt + 256],
                    pb[5].ap()[:, 256:512], OP.mult), serial=(kt == 0))
            return hv

        class WStream:

            def __init__(self, name, width, sem, scale_cols=None, nkt=16):
                self.stage = SBE(f"{name}_st", [128, width], F32)
                self.out = [SBE(f"{name}_o{i}", [128, width], BF16) for i in range(2)]
                self.sem = sem
                self.scale_cols = scale_cols
                self.nkt = nkt
                self.kw = width // nkt
                self.n = 0
                self.cast_v = [0, 0]

            def load(self, src_ap):
                dst = self.stage.ap()
                if src_ap.ndim == 3:
                    dst = dst.rearrange("p (a b) -> p a b", b=src_ap.shape[2])
                i = self.n % 2
                if self.n >= 1:
                    P.wait("sync", "vC", self.cast_v[(self.n - 1) % 2])
                P.dma("sync", dst, src_ap, self.sem)
                dv = P.val(self.sem)
                P.wait("vector", self.sem, dv)
                for kt in range(self.nkt):
                    sl = slice(self.kw * kt, self.kw * kt + self.kw)
                    if self.scale_cols is not None:
                        cv = P.cop("vector", lambda q, sl=sl, kt=kt, i=i: q.tensor_scalar(
                            self.out[i].ap()[:, sl], self.stage.ap()[:, sl],
                            self.scale_cols[:, kt:kt + 1], None, OP.mult), serial=(kt == 0))
                    else:
                        cv = P.cop("vector", lambda q, sl=sl, i=i: q.tensor_copy(
                            self.out[i].ap()[:, sl], self.stage.ap()[:, sl]), serial=(kt == 0))
                self.cast_v[i] = cv
                self.n += 1
                return self.out[i], cv

        fence()
        st_e = contextlib.ExitStack()
        SBE = lambda *a: st_e.enter_context(nc.sbuf_tensor(*a))
        h_ffn = SBE("h_ffn", [128, NKT * TOK], BF16)
        dacc = SBE("dacc", [128, 16 * 1024], F32)
        din = SBE("din", [128, 8 * 1024], BF16)
        lg_sb = SBE("lg_sb", [8, TOK], F32)

        st_p0b = contextlib.ExitStack()
        SBP = lambda *a: st_p0b.enter_context(nc.sbuf_tensor(*a))
        u_e = SBP("u_sbE", [128, NKT * 256], F32)
        ubf_e2 = SBP("ubf_e", [128, NKT * 256], BF16)
        usq_e2 = SBP("usq_e", [128, NKT * 256], BF16)
        lgt_sb = SBP("lgt_sb", [128, 16 * 8], F32)
        wn_sb = SBP("wn_sb", [128, 16 * 8], F32)
        rt1 = SBP("rt1", [128, 1], F32)
        rt2 = SBP("rt2", [128, 1], F32)
        rwb_sb = SBP("rwb_sb", [128, 128], BF16)

        hv = 0
        for cb in range(8):
            hv = ln_block(cb, h_ffn, 0, True, hv, u_e, ubf_e2, usq_e2)
        hffn_v = hv

        P.wait("sync", "vC", hv)
        P.dma("sync", u_e.ap()[:, 0:128].rearrange("p (kt e) -> p kt e", e=8),
              rw_d.rearrange("(kt p) e -> p kt e", p=128), "dW1")
        P.wait("vector", "dW1", P.val("dW1"))
        for kt in range(NKT):
            rww_v = P.cop("vector", lambda q, kt=kt, u=u_e: q.tensor_scalar(
                rwb_sb.ap()[:, 8 * kt:8 * kt + 8], u.ap()[:, 8 * kt:8 * kt + 8],
                rmsw_sb.ap()[:, kt:kt + 1], None, OP.mult), serial=(kt == 0))
        P.wait("tensor", "vC", max(rww_v, hffn_v))
        for tb in range(4):
            P.wait("tensor", "sC", P.val("sC"))
            for kt in range(NKT):
                lmm = P.top(lambda q, tb=tb, kt=kt: q.matmul(
                    pb[4].ap()[0:8, 0:512], rwb_sb.ap()[:, 8 * kt:8 * kt + 8],
                    h_ffn.ap()[:, TOK * kt + 512 * tb: TOK * kt + 512 * tb + 512],
                    start=(kt == 0), stop=(kt == NKT - 1)), mark=(kt == NKT - 1))
            P.wait("scalar", "tC", lmm)
            P.cop("scalar", lambda q, tb=tb: q.copy(lg_sb.ap()[:, 512 * tb:512 * tb + 512], pb[4].ap()[0:8, 0:512]))
        lg_v = P.val("sC")
        P.wait("tensor", "sC", lg_v)
        for tt in range(16):
            P.wait("tensor", "vC", P.val("vC"))
            tmm = P.top(lambda q, tt=tt: q.matmul(
                pb[4].ap()[:, 0:8], lg_sb.ap()[0:8, 128 * tt:128 * tt + 128],
                id_f.ap()[0:8, 0:8], start=True, stop=True, is_transpose=True), mark=True)
            P.wait("vector", "tC", tmm)
            P.cop("vector", lambda q, tt=tt: q.tensor_copy(lgt_sb.ap()[:, 8 * tt:8 * tt + 8], pb[4].ap()[:, 0:8]))
        for tt in range(16):
            L = lambda: lgt_sb.ap()[:, 8 * tt:8 * tt + 8]
            P.cop("vector", lambda q, tt=tt: q.reduce_max(rt1.ap(), lgt_sb.ap()[:, 8 * tt:8 * tt + 8], AX.X))
            sub_v = P.cop("vector", lambda q, tt=tt: q.tensor_scalar(
                lgt_sb.ap()[:, 8 * tt:8 * tt + 8], lgt_sb.ap()[:, 8 * tt:8 * tt + 8], rt1.ap()[:, 0:1], None, OP.subtract))
            P.wait("scalar", "vC", sub_v)
            ex_v = P.cop("scalar", lambda q, tt=tt: q.activation(
                lgt_sb.ap()[:, 8 * tt:8 * tt + 8], lgt_sb.ap()[:, 8 * tt:8 * tt + 8], AF.Exp))
            P.wait("vector", "sC", ex_v)
            P.cop("vector", lambda q, tt=tt: q.reduce_sum(rt1.ap(), lgt_sb.ap()[:, 8 * tt:8 * tt + 8], AX.X))
            P.cop("vector", lambda q: q.reciprocal(rt2.ap(), rt1.ap()))
            P.cop("vector", lambda q, tt=tt: q.tensor_scalar(
                lgt_sb.ap()[:, 8 * tt:8 * tt + 8], lgt_sb.ap()[:, 8 * tt:8 * tt + 8], rt2.ap()[:, 0:1], None, OP.mult))
            P.cop("vector", lambda q, tt=tt: q.reduce_max(rt1.ap(), lgt_sb.ap()[:, 8 * tt:8 * tt + 8], AX.X))
            P.cop("vector", lambda q, tt=tt: q.tensor_scalar(
                wn_sb.ap()[:, 8 * tt:8 * tt + 8], lgt_sb.ap()[:, 8 * tt:8 * tt + 8], rt1.ap()[:, 0:1], None, OP.is_ge))
            P.cop("vector", lambda q, tt=tt: q.scalar_tensor_tensor(
                wn_sb.ap()[:, 8 * tt:8 * tt + 8], wn_sb.ap()[:, 8 * tt:8 * tt + 8], -1e9,
                lgt_sb.ap()[:, 8 * tt:8 * tt + 8], OP.mult, OP.add))
            P.cop("vector", lambda q, tt=tt: q.reduce_max(rt2.ap(), wn_sb.ap()[:, 8 * tt:8 * tt + 8], AX.X))
            P.cop("vector", lambda q, tt=tt: q.tensor_scalar(
                wn_sb.ap()[:, 8 * tt:8 * tt + 8], lgt_sb.ap()[:, 8 * tt:8 * tt + 8], rt2.ap()[:, 0:1], None, OP.is_ge))
            P.cop("vector", lambda q, tt=tt: q.tensor_mul(
                wn_sb.ap()[:, 8 * tt:8 * tt + 8], wn_sb.ap()[:, 8 * tt:8 * tt + 8], lgt_sb.ap()[:, 8 * tt:8 * tt + 8]))
            P.cop("vector", lambda q, tt=tt: q.reduce_sum(rt1.ap(), wn_sb.ap()[:, 8 * tt:8 * tt + 8], AX.X))
            P.cop("vector", lambda q: q.reciprocal(rt2.ap(), rt1.ap()))
            wnv = P.cop("vector", lambda q, tt=tt: q.tensor_scalar(
                wn_sb.ap()[:, 8 * tt:8 * tt + 8], wn_sb.ap()[:, 8 * tt:8 * tt + 8], rt2.ap()[:, 0:1], None, OP.mult))
        P.wait("tensor", "vC", wnv)
        for tt in range(16):
            P.wait("tensor", "sC", P.val("sC"))
            tm2 = P.top(lambda q, tt=tt: q.matmul(
                pb[4].ap()[0:8, 0:128], wn_sb.ap()[:, 8 * tt:8 * tt + 8],
                id_f.ap(), start=True, stop=True, is_transpose=True), mark=True)
            P.wait("scalar", "tC", tm2)
            P.cop("scalar", lambda q, tt=tt: q.copy(lg_sb.ap()[:, 128 * tt:128 * tt + 128], pb[4].ap()[0:8, 0:128]))
        wdT_v = P.val("sC")
        fence()
        st_p0b.close()
        wfl = SBE("wfl", [1, 1024], F32)
        wB_sb = SBE("wB_sb", [128, 1024], BF16)
        sgt = SBE("sgt", [128, 512], BF16)
        t1b = SBE("t1b", [128, 512], BF16)
        fstg = SBE("fstg", [128, 1024], BF16)

        wg3 = wg_d.rearrange("e (kt p) f -> e p kt f", p=128)
        wu3 = wu_d.rearrange("e (kt p) f -> e p kt f", p=128)
        wd3 = wd_d.rearrange("e (kf p) d -> e p kf d", p=128)
        swg3 = swg_d.rearrange("(kt p) f -> p kt f", p=128)
        swu3 = swu_d.rearrange("(kt p) f -> p kt f", p=128)
        swd3 = swd_d.rearrange("(kf p) d -> p kf d", p=128)

        ws_gu = WStream("wsgu", 2048, "dW2", rmsw_sb.ap())
        ws_d = WStream("wsd", 1024, "dW4", None, nkt=8)

        for hf in range(2):
            t0 = 1024 * hf
            z0 = P.cop("vector", lambda q: q.memset(dacc.ap(), 0.0))
            for e in range(9):
                shared = (e == 8)
                if not shared:
                    P.wait("sync", "sC", wdT_v)
                    P.wait("sync", "tC", P.val("tC"))
                    P.dma("sync", wfl.ap(), lg_sb.ap()[e:e + 1, t0:t0 + 1024], "dW1")
                    P.wait("tensor", "dW1", P.val("dW1"))
                    P.wait("scalar", "gC", P.val("gC"))
                    for tb2 in range(2):
                        P.wait("tensor", "sC", P.val("sC"))
                        wbm = P.top(lambda q, tb2=tb2: q.matmul(
                            pb[6].ap()[:, 0:512], ones_rf.ap(),
                            wfl.ap()[0:1, 512 * tb2:512 * tb2 + 512], start=True, stop=True), mark=True)
                        P.wait("scalar", "tC", wbm)
                        wb_v = P.cop("scalar", lambda q, tb2=tb2: q.activation(
                            wB_sb.ap()[:, 512 * tb2:512 * tb2 + 512], pb[6].ap()[:, 0:512], AF.Copy))
                for mt in range(8):
                    if shared:
                        gsrc = swg3[:, :, 128 * mt:128 * mt + 128]
                        usrc = swu3[:, :, 128 * mt:128 * mt + 128]
                    else:
                        gsrc = wg3[e, :, :, 128 * mt:128 * mt + 128]
                        usrc = wu3[e, :, :, 128 * mt:128 * mt + 128]
                    wgb, gcv = ws_gu.load(gsrc)
                    wub, ucv = ws_gu.load(usrc)
                    P.wait("tensor", "vC", max(gcv, hffn_v))
                    P.wait("tensor", "gC", P.val("gC"))
                    for kt in range(NKT):
                        P.top(lambda q, kt=kt, wgb=wgb, t0=t0: q.matmul(
                            pb[0].ap(), wgb.ap()[:, 128 * kt:128 * kt + 128],
                            h_ffn.ap()[:, TOK * kt + t0: TOK * kt + t0 + 512],
                            start=(kt == 0), stop=(kt == NKT - 1)))
                        P.top(lambda q, kt=kt, wgb=wgb, t0=t0: q.matmul(
                            pb[1].ap(), wgb.ap()[:, 128 * kt:128 * kt + 128],
                            h_ffn.ap()[:, TOK * kt + t0 + 512: TOK * kt + t0 + 1024],
                            start=(kt == 0), stop=(kt == NKT - 1)))
                    P.wait("tensor", "vC", ucv)
                    for kt in range(NKT):
                        P.top(lambda q, kt=kt, wub=wub, t0=t0: q.matmul(
                            pb[2].ap(), wub.ap()[:, 128 * kt:128 * kt + 128],
                            h_ffn.ap()[:, TOK * kt + t0: TOK * kt + t0 + 512],
                            start=(kt == 0), stop=(kt == NKT - 1)))
                        gumm = P.top(lambda q, kt=kt, wub=wub, t0=t0: q.matmul(
                            pb[3].ap(), wub.ap()[:, 128 * kt:128 * kt + 128],
                            h_ffn.ap()[:, TOK * kt + t0 + 512: TOK * kt + t0 + 1024],
                            start=(kt == 0), stop=(kt == NKT - 1)), mark=(kt == NKT - 1))
                    P.wait("scalar", "tC", gumm)
                    for tb2 in range(2):
                        pg = pb[0] if tb2 == 0 else pb[1]
                        pu = pb[2] if tb2 == 0 else pb[3]
                        P.wait("scalar", "gC", P.val("gC"))
                        sg_v = P.cop("scalar", lambda q, pg=pg: q.activation(sgt.ap(), pg.ap(), AF.Silu))
                        P.wait("vector", "sC", sg_v)
                        tt_v = P.cop("vector", lambda q, pu=pu: q.tensor_tensor(t1b.ap(), pu.ap(), sgt.ap(), OP.mult))
                        P.wait("gpsimd", "vC", tt_v)
                        if mt == 0 and tb2 == 0:
                            P.wait("gpsimd", "tC", P.val("tC"))
                        if shared:
                            dv = P.cop("gpsimd", lambda q, mt=mt, tb2=tb2: q.tensor_copy(
                                din.ap()[:, 1024 * mt + 512 * tb2: 1024 * mt + 512 * tb2 + 512], t1b.ap()))
                        else:
                            P.wait("gpsimd", "sC", wb_v)
                            dv = P.cop("gpsimd", lambda q, mt=mt, tb2=tb2: q.tensor_tensor(
                                din.ap()[:, 1024 * mt + 512 * tb2: 1024 * mt + 512 * tb2 + 512],
                                t1b.ap(), wB_sb.ap()[:, 512 * tb2:512 * tb2 + 512], OP.mult))
                din_v = P.val("gC")
                for md in range(16):
                    if shared:
                        dsrc = swd3[:, :, 128 * md:128 * md + 128]
                    else:
                        dsrc = wd3[e, :, :, 128 * md:128 * md + 128]
                    wdb, dcv = ws_d.load(dsrc)
                    P.wait("tensor", "vC", dcv)
                    P.wait("tensor", "gC", din_v)
                    P.wait("tensor", "vC", P.val("vC"))
                    for kf in range(8):
                        P.top(lambda q, kf=kf, wdb=wdb: q.matmul(
                            pb[4].ap(), wdb.ap()[:, 128 * kf:128 * kf + 128],
                            din.ap()[:, 1024 * kf: 1024 * kf + 512],
                            start=(kf == 0), stop=(kf == 7)))
                        dmm = P.top(lambda q, kf=kf, wdb=wdb: q.matmul(
                            pb[5].ap(), wdb.ap()[:, 128 * kf:128 * kf + 128],
                            din.ap()[:, 1024 * kf + 512: 1024 * kf + 1024],
                            start=(kf == 0), stop=(kf == 7)), mark=(kf == 7))
                    P.wait("vector", "tC", dmm)
                    if e == 0:
                        P.wait("vector", "vC", z0)
                    P.cop("vector", lambda q, md=md: q.tensor_tensor(
                        dacc.ap()[:, 1024 * md: 1024 * md + 512],
                        dacc.ap()[:, 1024 * md: 1024 * md + 512], pb[4].ap(), OP.add))
                    P.cop("vector", lambda q, md=md: q.tensor_tensor(
                        dacc.ap()[:, 1024 * md + 512: 1024 * md + 1024],
                        dacc.ap()[:, 1024 * md + 512: 1024 * md + 1024], pb[5].ap(), OP.add))
            for md in range(16):
                P.wait("vector", "dFf", P.val("dFf"))
                fc_v = P.cop("vector", lambda q, md=md: q.tensor_copy(
                    fstg.ap(), dacc.ap()[:, 1024 * md:1024 * md + 1024]))
                P.wait("sync", "vC", fc_v)
                P.dma("sync", ffn_d.ap()[128 * md:128 * md + 128, t0:t0 + 1024], fstg.ap(), "dFf")
                P.wait("sync", "dFf", P.val("dFf"))
        st_e.close()

        fence()
        st_c = contextlib.ExitStack()
        SBE = lambda *a: st_c.enter_context(nc.sbuf_tensor(*a))
        xnh = SBE("xnh", [128, NKT * TOK], BF16)
        u_c = SBE("u_sbC", [128, NKT * 256], F32)
        ubf_c = SBE("ubf_c", [128, NKT * 256], BF16)
        usq_c = SBE("usq_c", [128, NKT * 256], BF16)
        g0t = [SBE(f"g0t{i}", [128, 512], BF16) for i in range(2)]
        g1t = [SBE(f"g1t{i}", [128, 512], BF16) for i in range(2)]
        ysel = SBE("ysel", [128, 512], BF16)
        szt = SBE("szt", [128, 512], BF16)
        gatb = SBE("gatb", [128, 512], BF16)
        sqt = SBE("sqt", [128, 512], BF16)
        sums_sb = SBE("sums_sb", [1, TOK], F32)
        ws_z = WStream("wsz", 2048, "dW2", None)

        hv = 0
        for cb in range(8):
            hv = ln_block(cb, xnh, 0, False, hv, u_c, ubf_c, usq_c)
        xnh_v = hv
        zs_v = P.cop("vector", lambda q: q.memset(sums_sb.ap(), 0.0))
        P.wait("gpsimd", "gC", P.val("gC"))

        if stop_after is None:
            P.wait("sync", "gC", ag_v)
        gdma_n = 0
        for mt in range(32):
            wzb, zcv = ws_z.load(Wz_d.rearrange("(kt p) z -> p kt z", p=128)[:, :, 128 * mt:128 * mt + 128])
            P.wait("tensor", "vC", max(zcv, xnh_v))
            P.wait("tensor", "sC", P.val("sC"))
            for kt in range(NKT):
                for tb in range(4):
                    zmm = P.top(lambda q, kt=kt, tb=tb, wzb=wzb: q.matmul(
                        pb[tb].ap(), wzb.ap()[:, 128 * kt:128 * kt + 128],
                        xnh.ap()[:, TOK * kt + 512 * tb: TOK * kt + 512 * tb + 512],
                        start=(kt == 0), stop=(kt == NKT - 1)), mark=(kt == NKT - 1 and tb == 3))
            r, chp = mt // 16, mt % 16
            for tb in range(4):
                gb = gdma_n % 2
                P.wait("sync", "vC", P.val("vC"))
                P.dma("sync", g0t[gb].ap(), G_d.ap()[r, 0, 128 * chp:128 * chp + 128, 512 * tb:512 * tb + 512], f"dG{gb}")
                P.dma("sync", g1t[gb].ap(), G_d.ap()[r, 1, 128 * chp:128 * chp + 128, 512 * tb:512 * tb + 512], f"dG{gb}")
                gv = P.val(f"dG{gb}")
                gdma_n += 1
                P.wait("scalar", "tC", zmm)
                P.wait("scalar", "vC", P.val("vC"))
                sz_v = P.cop("scalar", lambda q, tb=tb: q.activation(szt.ap(), pb[tb].ap(), AF.Silu))
                P.wait("vector", f"dG{gb}", gv)
                P.cop("vector", lambda q, gb=gb: q.tensor_scalar(
                    ysel.ap(), g0t[gb].ap(), selm_sb.ap()[:, 0:1], None, OP.mult))
                P.cop("vector", lambda q, gb=gb: q.scalar_tensor_tensor(
                    ysel.ap(), g1t[gb].ap(), selm_sb.ap()[:, 1:2], ysel.ap(), OP.mult, OP.add))
                P.wait("vector", "sC", sz_v)
                P.wait("vector", "dGa", P.val("dGa"))
                gat_v = P.cop("vector", lambda q: q.tensor_tensor(gatb.ap(), ysel.ap(), szt.ap(), OP.mult))
                P.wait("scalar", "vC", gat_v)
                sq_v = P.cop("scalar", lambda q: q.activation(sqt.ap(), gatb.ap(), AF.Square))
                P.dma("scalar", gat_d.ap()[128 * mt:128 * mt + 128, 512 * tb:512 * tb + 512], gatb.ap(), "dGa")
                P.wait("scalar", "dGa", P.val("dGa"))
                P.wait("tensor", "sC", sq_v)
                P.wait("tensor", "vC", P.val("vC"))
                qmm = P.top(lambda q: q.matmul(
                    pb[6].ap()[0:1, 0:512], ones_cb.ap(), sqt.ap(), start=True, stop=True), mark=True)
                P.wait("vector", "tC", qmm)
                if mt == 0 and tb == 0:
                    P.wait("vector", "vC", zs_v)
                P.cop("vector", lambda q, tb=tb: q.tensor_tensor(
                    sums_sb.ap()[0:1, 512 * tb:512 * tb + 512], sums_sb.ap()[0:1, 512 * tb:512 * tb + 512],
                    pb[6].ap()[0:1, 0:512], OP.add))
        sums_v = P.val("vC")
        st_c.close()

        fence()
        st_d = contextlib.ExitStack()
        SBE = lambda *a: st_d.enter_context(nc.sbuf_tensor(*a))
        ynorm = SBE("ynorm", [128, 32 * 1024], BF16)
        rstdB = SBE("rstdB", [128, TOK], F32)
        row4 = SBE("row4", [1, TOK], F32)
        gstg = [SBE(f"gstg{i}", [128, 1024], BF16) for i in range(2)]
        ustg = [SBE(f"ustg{i}", [128, 512], F32) for i in range(2)]
        ffstg = [SBE(f"ffstg{i}", [128, 512], BF16) for i in range(2)]
        otb = [SBE(f"otb{i}", [128, 512], F32) for i in range(2)]
        ws_o = WStream("wso", 4096, "dW2", nrmw_sb.ap(), nkt=32)

        P.wait("vector", "vC", sums_v)
        P.cop("vector", lambda q: q.tensor_scalar_mul(row4.ap(), sums_sb.ap(), 1.0 / DIN))
        rw4 = P.val("vC")
        P.wait("scalar", "vC", rw4)
        sq4 = P.cop("scalar", lambda q: q.activation(row4.ap(), row4.ap(), AF.Sqrt, bias=MRMS_EPS))
        P.wait("vector", "sC", sq4)
        rc4 = P.cop("vector", lambda q: q.reciprocal(row4.ap(), row4.ap()))
        P.wait("tensor", "vC", rc4)
        for tb in range(4):
            P.wait("tensor", "sC", P.val("sC"))
            bmm = P.top(lambda q, tb=tb: q.matmul(
                pb[4].ap(), ones_rf.ap(), row4.ap()[0:1, 512 * tb:512 * tb + 512],
                start=True, stop=True), mark=True)
            P.wait("scalar", "tC", bmm)
            P.cop("scalar", lambda q, tb=tb: q.activation(
                rstdB.ap()[:, 512 * tb:512 * tb + 512], pb[4].ap(), AF.Copy))
        rstdB_v = P.val("sC")

        uTh3b = uTh.rearrange("(mo p) T -> p mo T", p=128)
        for hf in range(2):
            t0 = 1024 * hf
            gn = 0
            for kt in range(32):
                gb = gn % 2
                if gn >= 2:
                    P.wait("sync", "vC", P.val("vC"))
                P.dma("sync", gstg[gb].ap(), gat_d.ap()[128 * kt:128 * kt + 128, t0:t0 + 1024], f"dGs{gb}")
                gv = P.val(f"dGs{gb}")
                gn += 1
                P.wait("vector", f"dGs{gb}", gv)
                if kt == 0 and hf == 0:
                    P.wait("vector", "sC", rstdB_v)
                yv = P.cop("vector", lambda q, kt=kt, gb=gb, t0=t0: q.tensor_tensor(
                    ynorm.ap()[:, 1024 * kt:1024 * kt + 1024], gstg[gb].ap(),
                    rstdB.ap()[:, t0:t0 + 1024], OP.mult))
            yn_v = yv
            for mo in range(16):
                wob, ocv = ws_o.load(Wo_d.rearrange("(kt p) d -> p kt d", p=128)[:, :, 128 * mo:128 * mo + 128])
                P.wait("tensor", "vC", max(ocv, yn_v, P.val("vC") if mo > 0 else 0))
                P.wait("tensor", "gC", P.val("gC"))
                for kt in range(32):
                    for tb2 in range(2):
                        omm = P.top(lambda q, kt=kt, tb2=tb2, wob=wob: q.matmul(
                            pb[tb2].ap(), wob.ap()[:, 128 * kt:128 * kt + 128],
                            ynorm.ap()[:, 1024 * kt + 512 * tb2: 1024 * kt + 512 * tb2 + 512],
                            start=(kt == 0), stop=(kt == 31)), mark=(kt == 31 and tb2 == 1))
                for tb2 in range(2):
                    ub = tb2 % 2
                    P.wait("sync", "vC", P.val("vC"))
                    P.dma("sync", ustg[ub].ap(),
                          uTh3b[:, mo, t0 + 512 * tb2: t0 + 512 * tb2 + 512], "dU")
                    P.dma("sync", ffstg[ub].ap(),
                          ffn_d.ap()[128 * mo:128 * mo + 128, t0 + 512 * tb2: t0 + 512 * tb2 + 512], "dU")
                    uv = P.val("dU")
                    P.wait("vector", "tC", omm)
                    P.wait("vector", "dU", uv)
                    P.wait("vector", "gC", P.val("gC"))
                    P.wait("vector", "dOut", P.val("dOut"))
                    tv1 = P.cop("vector", lambda q, tb2=tb2, ub=ub: q.tensor_tensor(
                        otb[tb2].ap(), pb[tb2].ap(), ustg[ub].ap(), OP.add))
                    P.wait("gpsimd", "vC", tv1)
                    ov = P.cop("gpsimd", lambda q, tb2=tb2, ub=ub: q.tensor_tensor(
                        otb[tb2].ap(), otb[tb2].ap(), ffstg[ub].ap(), OP.add))
                    P.wait("gpsimd", "gC", ov)
                    P.dma("gpsimd", out_d[128 * mo:128 * mo + 128, t0 + 512 * tb2: t0 + 512 * tb2 + 512],
                          otb[tb2].ap(), "dOut")
        st_d.close()
        P.wait("gpsimd", "dOut", P.val("dOut"))
        P.wait("scalar", "dOut", P.val("dOut"))
        P.wait("scalar", "dStF", P.val("dStF"))
        P.wait("scalar", "dGa", P.val("dGa"))
        P.wait("scalar", "dFf", P.val("dFf"))

    with nc.Block() as block:
        @block.sync
        def _(q):
            for f in P.streams["sync"]:
                f(q)

        @block.scalar
        def _(q):
            for f in P.streams["scalar"]:
                f(q)

        @block.vector
        def _(q):
            for f in P.streams["vector"]:
                f(q)

        @block.tensor
        def _(q):
            for f in P.streams["tensor"]:
                f(q)

        @block.gpsimd
        def _(q):
            for f in P.streams["gpsimd"]:
                f(q)

    return nc, dbg_aps


def kernel(**inputs):
    in_maps = host_layout(inputs)
    nc, _ = build_program()
    res = run_bass_kernel_spmd(nc, in_maps, core_ids=list(range(8)))
    out = np.zeros((NB, LQ, D), dtype=np.float32)
    for c in range(8):
        b, s = c // 2, c % 2
        out[b, TOK * s:TOK * s + TOK, :] = res.results[c]["out"].T
    return out


# version 04
# speedup vs baseline: 1.0941x; 1.0941x over previous
import contextlib

import numpy as np
import ml_dtypes
import concourse.bass as bass
import concourse.mybir as mybir
from concourse.bass_utils import run_bass_kernel_spmd

F32 = mybir.dt.float32
BF16 = mybir.dt.bfloat16
AF = mybir.ActivationFunctionType
OP = mybir.AluOpType
AX = mybir.AxisListType

D = 2048
LQ = 4096
NB = 4
DIN = 4096
NHC = 32
HP = 64
DS = 128
CH = 256
NCHUNK = LQ // CH
CONVC = NHC * HP + 2 * DS
NCT = CONVC // 128
WAC = CONVC + NHC
NKT = D // 128
TOK = 2048
NE = 8
FF = 1024
SF = 1024
LN_EPS = 1e-5
MRMS_EPS = 1e-5
FRMS_EPS = 1e-6

ENGS = ("sync", "scalar", "vector", "tensor", "gpsimd")
CHAIN = {"scalar": "sC", "vector": "vC", "gpsimd": "gC"}


class Prog:
    def __init__(self, nc):
        self.nc = nc
        self.streams = {e: [] for e in ENGS}
        self.sems = {}
        self.cnt = {}

    def sem(self, name):
        if name not in self.sems:
            self.sems[name] = self.nc.alloc_semaphore(name)
            self.cnt[name] = 0
        return self.sems[name]

    def emit(self, eng, fn):
        self.streams[eng].append(fn)

    def wait(self, eng, sname, val=None):
        h = self.sem(sname)
        v = self.cnt[sname] if val is None else val
        if v > 0:
            self.emit(eng, lambda q, h=h, v=v: q.wait_ge(h, v))
        return v

    def cop(self, eng, fn, serial=True):
        ch = CHAIN[eng]
        h = self.sem(ch)
        prev = self.cnt[ch]
        if serial and prev > 0:
            self.emit(eng, lambda q, h=h, v=prev: q.wait_ge(h, v))
        self.cnt[ch] = prev + 1
        self.emit(eng, lambda q, fn=fn, h=h: fn(q).then_inc(h))
        return prev + 1

    def top(self, fn, mark=False):
        if not mark:
            self.emit("tensor", fn)
            return self.cnt.get("tC", 0)
        h = self.sem("tC")
        self.cnt["tC"] += 1
        self.emit("tensor", lambda q, fn=fn, h=h: fn(q).then_inc(h))
        return self.cnt["tC"]

    def dma(self, eng, dst, src, inc, slow=False):
        h = self.sem(inc)
        self.cnt[inc] += 16
        self.emit(eng, lambda q, d=dst, s=src, sl=slow, h=h: q.dma_start(
            out=d, in_=s, allow_slow_non_contiguous=sl).then_inc(h, 16))
        return self.cnt[inc]

    def val(self, sname):
        return self.cnt.get(sname, 0)


def host_layout(inputs):
    u = np.asarray(inputs["L0_attn_u"], dtype=np.float32)
    W_in = np.asarray(inputs["L0_attn_W_in"], dtype=np.float32)
    conv_w = np.asarray(inputs["L0_attn_conv_w"], dtype=np.float32)
    conv_b = np.asarray(inputs["L0_attn_conv_b"], dtype=np.float32)
    dt_bias = np.asarray(inputs["L0_attn_dt_bias"], dtype=np.float32)
    A_log = np.asarray(inputs["L0_attn_A_log"], dtype=np.float32)
    Dp = np.asarray(inputs["L0_attn_Dp"], dtype=np.float32)
    norm_w = np.asarray(inputs["L0_attn_norm_w"], dtype=np.float32)
    W_out = np.asarray(inputs["L0_attn_W_out"], dtype=np.float32)
    rms_w = np.asarray(inputs["L0_ffn_rms_weight"], dtype=np.float32)
    router_w = np.asarray(inputs["L0_ffn_router_w"], dtype=np.float32)
    w_gate = np.asarray(inputs["L0_ffn_w_gate"], dtype=np.float32)
    w_up = np.asarray(inputs["L0_ffn_w_up"], dtype=np.float32)
    w_down = np.asarray(inputs["L0_ffn_w_down"], dtype=np.float32)
    sw_gate = np.asarray(inputs["L0_ffn_sw_gate"], dtype=np.float32)
    sw_up = np.asarray(inputs["L0_ffn_sw_up"], dtype=np.float32)
    sw_down = np.asarray(inputs["L0_ffn_sw_down"], dtype=np.float32)

    in_maps = []
    for c in range(8):
        b, s = c // 2, c % 2
        hh = slice(32 * s, 32 * s + 32)
        xch = slice(DIN + 2048 * s, DIN + 2048 * s + 2048)
        WA = np.concatenate([W_in[xch], W_in[8192:8448],
                             W_in[8448 + 32 * s: 8448 + 32 * s + 32]], axis=0)
        cw = np.concatenate([conv_w[2048 * s:2048 * s + 2048], conv_w[4096:4352]], axis=0)
        cb = np.concatenate([conv_b[2048 * s:2048 * s + 2048], conv_b[4096:4352]], axis=0)
        m = {
            "uTf": np.ascontiguousarray(u[b].T),
            "uTh": np.ascontiguousarray(u[b, 2048 * s:2048 * s + 2048].T),
            "WA": np.ascontiguousarray(WA.T),
            "Wz": np.ascontiguousarray(W_in[0:DIN].T),
            "Wo": np.ascontiguousarray(W_out.T),
            "convw": np.ascontiguousarray(cw.reshape(NCT, 128, 4).transpose(1, 0, 2).reshape(128, NCT * 4)),
            "convb": np.ascontiguousarray(cb.reshape(NCT, 128).T),
            "dtb": np.ascontiguousarray(dt_bias[hh].reshape(32, 1)),
            "alog": np.ascontiguousarray(A_log[hh].reshape(32, 1)),
            "dpv": np.ascontiguousarray(Dp[hh].reshape(32, 1)),
            "nrmw": np.ascontiguousarray(norm_w.reshape(32, 128).T),
            "rmsw": np.ascontiguousarray(rms_w.reshape(16, 128).T),
            "rw": router_w,
            "wg": w_gate, "wu": w_up, "wd": w_down,
            "swg": sw_gate, "swu": sw_up, "swd": sw_down,
            "selm": np.ascontiguousarray(
                np.repeat(np.array([[1.0 - s, float(s)]], dtype=np.float32), 128, axis=0)),
        }
        in_maps.append(m)
    return in_maps


def build_program(debug=(), stop_after=None):
    nc = bass.Bass("TRN2", target_bir_lowering=False)
    P = Prog(nc)
    dbg = set(debug)
    dbg_aps = {}

    def mkdbg(name, shape, dtype=F32):
        if name in dbg:
            ap = nc.dram_tensor("dbg_" + name, list(shape), dtype, kind="ExternalOutput").ap()
            dbg_aps[name] = ap
            return ap
        return None

    uTf = nc.dram_tensor("uTf", [D, LQ], F32, kind="ExternalInput").ap()
    uTh = nc.dram_tensor("uTh", [D, TOK], F32, kind="ExternalInput").ap()
    WA_d = nc.dram_tensor("WA", [D, WAC], F32, kind="ExternalInput").ap()
    Wz_d = nc.dram_tensor("Wz", [D, DIN], F32, kind="ExternalInput").ap()
    Wo_d = nc.dram_tensor("Wo", [DIN, D], F32, kind="ExternalInput").ap()
    convw_d = nc.dram_tensor("convw", [128, NCT * 4], F32, kind="ExternalInput").ap()
    convb_d = nc.dram_tensor("convb", [128, NCT], F32, kind="ExternalInput").ap()
    dtb_d = nc.dram_tensor("dtb", [32, 1], F32, kind="ExternalInput").ap()
    alog_d = nc.dram_tensor("alog", [32, 1], F32, kind="ExternalInput").ap()
    dp_d = nc.dram_tensor("dpv", [32, 1], F32, kind="ExternalInput").ap()
    nrmw_d = nc.dram_tensor("nrmw", [128, 32], F32, kind="ExternalInput").ap()
    rmsw_d = nc.dram_tensor("rmsw", [128, 16], F32, kind="ExternalInput").ap()
    rw_d = nc.dram_tensor("rw", [D, NE], F32, kind="ExternalInput").ap()
    wg_d = nc.dram_tensor("wg", [NE, D, FF], F32, kind="ExternalInput").ap()
    wu_d = nc.dram_tensor("wu", [NE, D, FF], F32, kind="ExternalInput").ap()
    wd_d = nc.dram_tensor("wd", [NE, FF, D], F32, kind="ExternalInput").ap()
    swg_d = nc.dram_tensor("swg", [D, SF], F32, kind="ExternalInput").ap()
    swu_d = nc.dram_tensor("swu", [D, SF], F32, kind="ExternalInput").ap()
    swd_d = nc.dram_tensor("swd", [SF, D], F32, kind="ExternalInput").ap()
    selm_d = nc.dram_tensor("selm", [128, 2], F32, kind="ExternalInput").ap()
    out_d = nc.dram_tensor("out", [D, TOK], F32, kind="ExternalOutput").ap()

    S_d = nc.dram_tensor("S", [2, D, TOK], BF16, kind="Internal")
    G_d = nc.dram_tensor("G", [2, 2, D, TOK], BF16, kind="Internal")
    gat_d = nc.dram_tensor("gat", [DIN, TOK], BF16, kind="Internal")
    ffn_d = nc.dram_tensor("ffn", [D, TOK], BF16, kind="Internal")

    id_bf_t = nc.inline_tensor(np.eye(128, dtype=ml_dtypes.bfloat16), name="t_id_bf")
    id_f_t = nc.inline_tensor(np.eye(128, dtype=np.float32), name="t_id_f")
    ones_cb_t = nc.inline_tensor(np.ones((128, 1), dtype=ml_dtypes.bfloat16), name="t_ones_cb")
    ones_rf_t = nc.inline_tensor(np.ones((1, 128), dtype=np.float32), name="t_ones_rf")
    psel = np.zeros((2, 128), dtype=np.float32)
    psel[0, 0:64] = 1.0
    psel[1, 64:128] = 1.0
    psel_t = nc.inline_tensor(psel, name="t_psel")
    mj = (np.arange(128)[:, None] <= np.arange(256)[None, :]).astype(ml_dtypes.bfloat16)
    mask_t = nc.inline_tensor(np.ascontiguousarray(mj), name="t_mask")
    ind = (np.arange(32)[:, None] == (np.arange(2048)[None, :] // 64)).astype(ml_dtypes.bfloat16)
    ind_t = nc.inline_tensor(np.ascontiguousarray(ind), name="t_ind")
    ones32_t = nc.inline_tensor(np.ones((32, 128), dtype=np.float32), name="t_ones32")
    SCALAR_CONSTS = [LN_EPS, 1.0, MRMS_EPS, FRMS_EPS, 0.5, 2.0]
    sconst_t = nc.inline_tensor(
        np.ascontiguousarray(np.repeat(np.array([SCALAR_CONSTS], dtype=np.float32), 128, axis=0)),
        name="t_sconst")

    A = nc.alloc_sbuf_tensor
    id_bf = A("id_bf", [128, 128], BF16)
    id_f = A("id_f", [128, 128], F32)
    ones_cb = A("ones_cb", [128, 1], BF16)
    ones_rf = A("ones_rf", [1, 128], F32)
    mask_sb = A("mask_sb", [128, 256], BF16)
    convw_sb = A("convw_sb", [128, NCT * 4], F32)
    convb_sb = A("convb_sb", [128, NCT], F32)
    dtb_sb = A("dtb_sb", [32, 1], F32)
    alog_sb = A("alog_sb", [32, 1], F32)
    nA_sb = A("nA_sb", [32, 1], F32)
    ones32_sb = A("ones32_sb", [32, 128], F32)
    dpB_sb = A("dpB_sb", [128, 16], F32)
    selm_sb = A("selm_sb", [128, 2], F32)
    nrmw_sb = A("nrmw_sb", [128, 32], F32)
    rmsw_sb = A("rmsw_sb", [128, 16], F32)
    row1 = A("row1", [1, 512], F32)
    row2 = A("row2", [1, 512], F32)
    row3 = A("row3", [1, 512], F32)
    sconst_sb = A("sconst_sb", [128, len(SCALAR_CONSTS)], F32)
    for i, v in enumerate(SCALAR_CONSTS):
        nc.const_aps.register(F32, v, sconst_sb.ap()[:, i:i + 1])

    pb = [nc.alloc_psum_tensor(f"pb{i}", [128, 512], F32) for i in range(7)]
    pbT = nc.alloc_psum_tensor("pbT", [128, 1024], BF16)

    for dst, src in [(id_bf, id_bf_t), (id_f, id_f_t), (ones_cb, ones_cb_t),
                     (ones_rf, ones_rf_t), (mask_sb, mask_t),
                     (sconst_sb, sconst_t), (ones32_sb, ones32_t)]:
        P.dma("sync", dst.ap(), src.ap(), "dIn")
    P.dma("sync", convw_sb.ap(), convw_d, "dIn")
    P.dma("sync", convb_sb.ap(), convb_d, "dIn")
    P.dma("sync", dtb_sb.ap(), dtb_d, "dIn")
    P.dma("sync", alog_sb.ap(), alog_d, "dIn")
    P.dma("sync", selm_sb.ap(), selm_d, "dIn")
    P.dma("sync", nrmw_sb.ap(), nrmw_d, "dIn")
    P.dma("sync", rmsw_sb.ap(), rmsw_d, "dIn")
    P.dma("gpsimd", dpB_sb.ap()[0:64, :],
          dp_d[0:32:2, 0:1].rearrange("a b -> b a").partition_broadcast(64).squeeze(1), "dGp", slow=True)
    P.dma("gpsimd", dpB_sb.ap()[64:128, :],
          dp_d[1:32:2, 0:1].rearrange("a b -> b a").partition_broadcast(64).squeeze(1), "dGp", slow=True)
    cload = P.val("dIn")

    P.wait("scalar", "dIn", cload)
    P.cop("scalar", lambda q: q.activation(nA_sb.ap(), alog_sb.ap(), AF.Exp))
    P.wait("vector", "sC")
    P.cop("vector", lambda q: q.tensor_scalar_mul(nA_sb.ap(), nA_sb.ap(), -1.0))

    st_ab = contextlib.ExitStack()
    SB = lambda *a: st_ab.enter_context(nc.sbuf_tensor(*a))
    WA_sb = SB("WA_sb", [128, NKT * WAC], BF16)
    xn_sb = [SB(f"xn_sb{i}", [128, NKT * 256], BF16) for i in range(2)]
    xBC_sb = SB("xBC_sb", [128, NCT * 259], BF16)
    convacc = SB("convacc", [128, 512], F32)
    xT_sb = SB("xT_sb", [128, 2 * 2048], BF16)
    xw_sb = SB("xw_sb", [128, 2 * 2048], BF16)
    usq_sb = xw_sb
    Btok_sb = SB("Btok_sb", [128, 2 * 128], BF16)
    wdtT_sb = SB("wdtT_sb", [128, 2 * 32], F32)
    h_sb = SB("h_sb", [128, NHC * HP], F32)
    hT_bf = SB("hT_bf", [128, NHC * HP], BF16)
    smalls = SB("smalls", [32, 2816], F32)
    flats = SB("flats", [128, 4096], F32)
    u_sb = SB("u_sb", [128, NKT * 256], F32)
    ind_sb = SB("ind_sb", [32, 2048], BF16)
    cdr_sb = SB("cdr_sb", [32, 512], F32)

    sm = {}
    off = 0
    for nm, w in [("dtraw", 256), ("dt", 256), ("cs", 256), ("negcs", 256),
                  ("expf", 256), ("decay", 256), ("wdt", 256), ("lndt", 256),
                  ("tmp1", 256), ("tmp2", 256), ("cd", 1), ("cse", 1)]:
        sm[nm] = off
        off += w

    def SM(name, w=256):
        return smalls.ap()[:, sm[name]:sm[name] + w]

    tap_xn = mkdbg("xn0", [128, NKT * 256], BF16)
    tap_xbc = mkdbg("xbc0", [128, NCT * 256], BF16)
    tap_dt = mkdbg("dt0", [32, 256])
    tap_cs = mkdbg("cs0", [32, 256])
    tap_st = mkdbg("st0", [128, 384], BF16)
    tap_h = mkdbg("h0", [128, NHC * HP])
    tap_yp = mkdbg("yp0", [128, 16 * 256], BF16)

    P.wait("vector", "dGp", P.val("dGp"))
    P.cop("vector", lambda q: q.memset(h_sb.ap(), 0.0))
    for t in range(NCT):
        P.cop("vector", lambda q, t=t: q.memset(xBC_sb.ap()[:, 259 * t:259 * t + 3], 0.0), serial=False)
    P.cop("vector", lambda q: q.memset(flats.ap()[0:1, 0:2048], 1.0), serial=False)
    ab_init_v = P.cop("vector", lambda q: q.memset(flats.ap()[0:3, 2048:4096], 1.0), serial=False)
    P.wait("sync", "dIn", cload)
    P.dma("sync", flats.ap()[32:34, 0:128], psel_t.ap(), "dIn")
    P.dma("sync", ind_sb.ap(), ind_t.ap(), "dIn")
    P.wait("tensor", "dIn", P.val("dIn"))

    with nc.sbuf_tensor("wstage0", [128, WAC], F32) as ws0:
        for kt in range(NKT):
            if kt >= 1:
                P.wait("sync", "vC", P.val("vC"))
            P.dma("sync", ws0.ap(), WA_d[128 * kt:128 * kt + 128, :], "dWA")
            P.wait("vector", "dWA", P.val("dWA"))
            P.cop("vector", lambda q, kt=kt: q.tensor_copy(
                WA_sb.ap()[:, WAC * kt: WAC * kt + WAC], ws0.ap()), serial=False)
    wa_ready_v = P.val("vC")

    e_sb = SB("e_sb", [128, 256], F32)
    ycomb = SB("ycomb", [128, 256], F32)
    ypre_sb = [SB(f"ypre{i}", [128, 16 * 256], BF16) for i in range(2)]
    ST_sb = [SB(f"ST{i}", [128, 384], BF16) for i in range(2)]
    CBTm_sb = SB("CBTm", [128, 384], BF16)

    uTf3 = uTf.rearrange("(kt p) T -> p kt T", p=128)
    last_xn_v = 0
    prev_te_done = 0
    prev_ych_v = 0

    if stop_after == "s_init":
        chunk_range = []
    else:
        chunk_range = list(range(NCHUNK))
    for c in chunk_range:
        xb = c % 2
        half = c // 8
        yb = c % 2

        P.wait("sync", "vC", last_xn_v if c > 0 else wa_ready_v)
        P.dma("sync", u_sb.ap().rearrange("p (kt T) -> p kt T", T=256),
              uTf3[:, :, CH * c: CH * c + CH], "dU")
        P.wait("vector", "dU", P.val("dU"))
        P.wait("vector", "tC", prev_te_done)
        ubf_v = P.cop("vector", lambda q, xb=xb, u=u_sb: q.tensor_copy(xn_sb[xb].ap(), u.ap()))
        P.wait("scalar", "vC", ubf_v)
        P.wait("scalar", "tC", prev_te_done)
        usq_v = P.cop("scalar", lambda q, xb=xb: q.activation(usq_sb.ap(), xn_sb[xb].ap(), AF.Square))
        if c == 0 and stop_after == "s_p0a":
            break
        P.wait("tensor", "vC", ubf_v)
        P.wait("tensor", "sC", usq_v)
        P.wait("tensor", "vC", P.val("vC"))
        for kt in range(NKT):
            P.top(lambda q, kt=kt, xb=xb: q.matmul(
                pb[6].ap()[0:1, 0:256], ones_cb.ap(),
                xn_sb[xb].ap()[:, 256 * kt:256 * kt + 256],
                start=(kt == 0), stop=(kt == NKT - 1)))
        for kt in range(NKT):
            sums_mm = P.top(lambda q, kt=kt: q.matmul(
                pb[6].ap()[0:1, 256:512], ones_cb.ap(),
                usq_sb.ap()[:, 256 * kt:256 * kt + 256],
                start=(kt == 0), stop=(kt == NKT - 1)), mark=(kt == NKT - 1))
        if c == 0 and stop_after == "s_p0b":
            break
        P.wait("vector", "tC", sums_mm)
        P.cop("vector", lambda q: q.tensor_scalar_mul(row1.ap()[0:1, 0:256], pb[6].ap()[0:1, 0:256], 1.0 / D))
        P.cop("vector", lambda q: q.tensor_scalar_mul(row2.ap()[0:1, 0:256], pb[6].ap()[0:1, 256:512], 1.0 / D))
        P.cop("vector", lambda q: q.tensor_mul(row3.ap()[0:1, 0:256], row1.ap()[0:1, 0:256], row1.ap()[0:1, 0:256]))
        stats_v = P.cop("vector", lambda q: q.tensor_sub(row2.ap()[0:1, 0:256], row2.ap()[0:1, 0:256], row3.ap()[0:1, 0:256]))
        P.wait("scalar", "vC", stats_v)
        sq_v = P.cop("scalar", lambda q: q.activation(row3.ap()[0:1, 0:256], row2.ap()[0:1, 0:256], AF.Sqrt, bias=LN_EPS))
        P.wait("vector", "sC", sq_v)
        rstd_v = P.cop("vector", lambda q: q.reciprocal(row3.ap()[0:1, 0:256], row3.ap()[0:1, 0:256]))
        if c == 0 and stop_after == "s_p0c":
            break
        P.wait("tensor", "vC", rstd_v)
        P.wait("tensor", "sC", P.val("sC"))
        P.top(lambda q: q.matmul(pb[5].ap()[:, 0:256], ones_rf.ap(), row1.ap()[0:1, 0:256], start=True, stop=True))
        bc_mm = P.top(lambda q: q.matmul(pb[5].ap()[:, 256:512], ones_rf.ap(), row3.ap()[0:1, 0:256], start=True, stop=True), mark=True)
        P.wait("vector", "tC", bc_mm)
        if c == 0 and stop_after == "s_p0d":
            break
        for kt in range(NKT):
            P.cop("vector", lambda q, kt=kt, u=u_sb: q.tensor_sub(
                u.ap()[:, 256 * kt:256 * kt + 256], u.ap()[:, 256 * kt:256 * kt + 256],
                pb[5].ap()[:, 0:256]), serial=(kt == 0))
        if c == 0 and stop_after == "s_p0e":
            break
        for kt in range(NKT):
            xn_v = P.cop("vector", lambda q, kt=kt, xb=xb, u=u_sb: q.tensor_tensor(
                xn_sb[xb].ap()[:, 256 * kt:256 * kt + 256], u.ap()[:, 256 * kt:256 * kt + 256],
                pb[5].ap()[:, 256:512], OP.mult), serial=(kt == 0))
        last_xn_v = xn_v
        if c == 0 and tap_xn is not None:
            P.wait("sync", "vC", xn_v)
            P.dma("sync", tap_xn, xn_sb[0].ap(), "dTap")
            P.wait("sync", "dTap", P.val("dTap"))
            P.wait("vector", "dTap", P.val("dTap"))
            P.wait("tensor", "dTap", P.val("dTap"))


        P.wait("tensor", "vC", xn_v)
        if c == 0:
            P.wait("tensor", "vC", ab_init_v)
        evac_vals = []
        for mt in range(NCT + 1):
            bank = pb[mt % 2]
            mw = 128 if mt < NCT else 32
            if len(evac_vals) >= 2:
                P.wait("tensor", "sC", evac_vals[-2])
            for kt in range(NKT):
                mm_v = P.top(lambda q, mt=mt, kt=kt, bank=bank, mw=mw, xb=xb: q.matmul(
                    bank.ap()[0:mw, 0:256],
                    WA_sb.ap()[:, WAC * kt + 128 * mt: WAC * kt + 128 * mt + mw],
                    xn_sb[xb].ap()[:, 256 * kt:256 * kt + 256],
                    start=(kt == 0), stop=(kt == NKT - 1)), mark=(kt == NKT - 1))
            P.wait("scalar", "tC", mm_v)
            if mt == 0:
                P.wait("scalar", "tC", prev_te_done)
                P.wait("scalar", "vC", prev_ych_v)
            if mt < NCT:
                ev = P.cop("scalar", lambda q, mt=mt, bank=bank: q.copy(
                    xBC_sb.ap()[:, 259 * mt + 3:259 * mt + 259], bank.ap()[:, 0:256]))
            else:
                ev = P.cop("scalar", lambda q, bank=bank: q.copy(
                    SM("dtraw"), bank.ap()[0:32, 0:256]))
            evac_vals.append(ev)
        inproj_evac = evac_vals[-1]

        if c == 0 and stop_after == "s_inproj":
            break
        P.wait("vector", "sC", inproj_evac)
        silu_vals = []
        for t in range(NCT):
            ca = convacc.ap()[:, 256 * (t % 2):256 * (t % 2) + 256]
            if t >= 2:
                P.wait("vector", "sC", silu_vals[t - 2])
            P.cop("vector", lambda q, t=t, ca=ca: q.tensor_scalar(
                ca, xBC_sb.ap()[:, 259 * t + 0:259 * t + 256],
                convw_sb.ap()[:, 4 * t + 0:4 * t + 1], None, OP.mult))
            for k in (1, 2, 3):
                P.cop("vector", lambda q, t=t, k=k, ca=ca: q.scalar_tensor_tensor(
                    ca, xBC_sb.ap()[:, 259 * t + k:259 * t + k + 256],
                    convw_sb.ap()[:, 4 * t + k:4 * t + k + 1], ca, OP.mult, OP.add))
            P.cop("vector", lambda q, t=t: q.tensor_copy(
                xBC_sb.ap()[:, 259 * t + 0:259 * t + 3], xBC_sb.ap()[:, 259 * t + 256:259 * t + 259]))
            vv = P.val("vC")
            P.wait("scalar", "vC", vv)
            sv = P.cop("scalar", lambda q, t=t, ca=ca: q.activation(
                xBC_sb.ap()[:, 259 * t + 3:259 * t + 259], ca,
                AF.Silu, bias=convb_sb.ap()[:, t:t + 1]))
            silu_vals.append(sv)
        conv_done = P.val("sC")
        if c == 0 and tap_xbc is not None:
            P.wait("sync", "sC", conv_done)
            P.dma("sync", tap_xbc.rearrange("p (t T) -> p t T", T=256),
                  xBC_sb.ap().rearrange("p (t T) -> p t T", T=259)[:, :, 3:259], "dTap")
            P.wait("sync", "dTap", P.val("dTap"))
            P.wait("scalar", "dTap", P.val("dTap"))
            P.wait("vector", "dTap", P.val("dTap"))

        if c == 0 and stop_after == "s_conv":
            break
        P.wait("vector", "sC", inproj_evac)
        P.cop("vector", lambda q: q.tensor_scalar(SM("tmp1"), SM("dtraw"), dtb_sb.ap()[0:32, 0:1], None, OP.add))
        P.cop("vector", lambda q: q.tensor_scalar(SM("tmp2"), SM("tmp1"), 0.0, None, OP.max))
        relu_v = P.val("vC")
        P.wait("scalar", "vC", relu_v)
        P.cop("scalar", lambda q: q.activation(SM("dt"), SM("tmp1"), AF.Abs))
        P.cop("scalar", lambda q: q.activation(SM("dt"), SM("dt"), AF.Exp, scale=-1.0))
        ln_v = P.cop("scalar", lambda q: q.activation(SM("dt"), SM("dt"), AF.Ln, bias=1.0))
        P.wait("vector", "sC", ln_v)
        dt_v = P.cop("vector", lambda q: q.tensor_add(SM("dt"), SM("dt"), SM("tmp2")))
        P.wait("scalar", "vC", dt_v)
        P.cop("scalar", lambda q: q.activation(SM("lndt"), SM("dt"), AF.Ln))
        P.cop("vector", lambda q: q.tensor_scalar(SM("tmp1"), SM("dt"), nA_sb.ap()[0:32, 0:1], None, OP.mult))
        P.cop("vector", lambda q: q.tensor_tensor_scan(SM("cs"), SM("tmp1"), SM("tmp1"), 0.0, OP.add, OP.bypass))
        P.cop("vector", lambda q: q.tensor_scalar_mul(SM("negcs"), SM("cs"), -1.0))
        cse_v = P.cop("vector", lambda q: q.tensor_copy(SM("cse", 1), smalls.ap()[:, sm["cs"] + 255:sm["cs"] + 256]))
        P.wait("scalar", "vC", cse_v)
        P.cop("scalar", lambda q: q.activation(SM("expf"), SM("cs"), AF.Exp))
        P.cop("scalar", lambda q: q.activation(SM("decay"), SM("cs"), AF.Exp, scale=-1.0, bias=SM("cse", 1)))
        dexp_v = P.cop("scalar", lambda q: q.activation(SM("cd", 1), SM("cse", 1), AF.Exp))
        P.wait("vector", "sC", dexp_v)
        dtpath_v = P.cop("vector", lambda q: q.tensor_mul(SM("wdt"), SM("decay"), SM("dt")))
        if c == 0 and tap_dt is not None:
            P.wait("sync", "vC", dtpath_v)
            P.dma("sync", tap_dt, SM("dt"), "dTap")
            P.wait("sync", "dTap", P.val("dTap"))
            P.dma("sync", tap_cs, SM("cs"), "dTap")
            P.wait("sync", "dTap", P.val("dTap"))
            P.wait("scalar", "dTap", P.val("dTap"))
            P.wait("vector", "dTap", P.val("dTap"))

        if c == 0 and stop_after == "s_dt":
            break
        P.wait("tensor", "vC", dtpath_v)
        for jt in range(2):
            wdtT_mm = P.top(lambda q, jt=jt: q.matmul(
                pb[2].ap()[0:128, 32 * jt:32 * jt + 32], SM("wdt")[0:32, 128 * jt:128 * jt + 128],
                id_f.ap()[0:32, 0:32], start=True, stop=True, is_transpose=True), mark=(jt == 1))
        P.wait("vector", "tC", wdtT_mm)
        wdtT_v = P.cop("vector", lambda q: q.tensor_copy(wdtT_sb.ap()[:, 0:64], pb[2].ap()[:, 0:64]))
        if c == 0 and stop_after == "s_tr1":
            break
        P.wait("tensor", "sC", conv_done)
        tp_banks = [pbT.ap()[:, 0:128], pb[3].ap().bitcast(BF16)[:, 0:128]]
        tevac = []
        for t in range(17):
            for jt in range(2):
                tpos = 2 * t + jt
                bk = tp_banks[tpos % 2]
                if tpos >= 2:
                    P.wait("tensor", "vC", tevac[tpos - 2])
                tr_mm = P.top(lambda q, t=t, jt=jt, bk=bk: q.matmul(
                    bk, xBC_sb.ap()[:, 259 * t + 3 + 128 * jt: 259 * t + 3 + 128 * jt + 128],
                    id_bf.ap(), start=True, stop=True, is_transpose=True), mark=True)
                P.wait("vector", "tC", tr_mm)
                if t < 16:
                    ev = P.cop("vector", lambda q, t=t, jt=jt, bk=bk: q.tensor_copy(
                        xT_sb.ap()[:, 2048 * jt + 128 * t: 2048 * jt + 128 * t + 128],
                        bk), serial=False)
                else:
                    ev = P.cop("vector", lambda q, jt=jt, bk=bk: q.tensor_copy(
                        Btok_sb.ap()[:, 128 * jt:128 * jt + 128],
                        bk), serial=False)
                tevac.append(ev)
                if c == 0 and stop_after == "s_tr2" and t == 0:
                    break
            if c == 0 and stop_after == "s_tr2" and t == 0:
                break
        if c == 0 and stop_after == "s_tr2":
            break
        P.wait("vector", "vC", wdtT_v)
        for jt in range(2):
            for h in range(NHC):
                xw_v = P.cop("vector", lambda q, jt=jt, h=h: q.tensor_scalar(
                    xw_sb.ap()[:, 2048 * jt + 64 * h: 2048 * jt + 64 * h + 64],
                    xT_sb.ap()[:, 2048 * jt + 64 * h: 2048 * jt + 64 * h + 64],
                    wdtT_sb.ap()[:, 32 * jt + h:32 * jt + h + 1], None, OP.mult),
                    serial=(jt == 0 and h == 0))

        if c == 0 and stop_after == "s_transp":
            break
        Bcm = xBC_sb.ap()[:, 259 * 16 + 3: 259 * 16 + 259]
        Ccm = xBC_sb.ap()[:, 259 * 17 + 3: 259 * 17 + 259]
        P.wait("tensor", "vC", P.val("vC"))
        P.top(lambda q: q.matmul(pb[2].ap()[:, 0:256], Bcm[:, 0:128], Ccm, start=True, stop=True))
        cbt_mm = P.top(lambda q: q.matmul(pb[3].ap()[:, 0:128], Bcm[:, 128:256], Ccm[:, 128:256], start=True, stop=True), mark=True)
        P.wait("vector", "tC", cbt_mm)
        P.cop("vector", lambda q: q.tensor_tensor(CBTm_sb.ap()[:, 0:256], pb[2].ap()[:, 0:256], mask_sb.ap()[:, 0:256], OP.mult))
        cbt_v = P.cop("vector", lambda q: q.tensor_tensor(CBTm_sb.ap()[:, 256:384], pb[3].ap()[:, 0:128], mask_sb.ap()[:, 0:128], OP.mult))

        hTb_v = P.cop("vector", lambda q: q.tensor_copy(hT_bf.ap(), h_sb.ap()))

        if c == 0 and stop_after == "s_cbt":
            break
        for qtr in range(4):
            P.wait("sync", "vC", dtpath_v)
            P.wait("sync", "tC", P.val("tC"))
            P.dma("sync", flats.ap()[1:2, 0:2048], smalls.ap()[8 * qtr:8 * qtr + 8, sm["negcs"]:sm["negcs"] + 256], "dFl")
            P.dma("sync", flats.ap()[2:3, 0:2048], smalls.ap()[8 * qtr:8 * qtr + 8, sm["lndt"]:sm["lndt"] + 256], "dFl")
            P.dma("sync", flats.ap()[0:1, 2048:4096], smalls.ap()[8 * qtr:8 * qtr + 8, sm["cs"]:sm["cs"] + 256], "dFl")
            P.dma("sync", flats.ap()[32:33, 2048:4096], smalls.ap()[8 * qtr:8 * qtr + 8, sm["expf"]:sm["expf"] + 256], "dFl")
            if qtr < 3:
                P.dma("sync", flats.ap()[33:34, 2048:4096],
                      smalls.ap()[8 * qtr + 1:8 * qtr + 9, sm["expf"]:sm["expf"] + 256], "dFl")
            else:
                P.dma("sync", flats.ap()[33:34, 2048:3840],
                      smalls.ap()[8 * qtr + 1:8 * qtr + 8, sm["expf"]:sm["expf"] + 256], "dFl")
            fl_v = P.val("dFl")
            P.wait("tensor", "dFl", fl_v)

            P.wait("vector", "sC", dexp_v)
            cdr_v = P.cop("vector", lambda q, qtr=qtr: q.tensor_scalar(
                cdr_sb.ap(), ind_sb.ap()[:, 512 * qtr:512 * qtr + 512], SM("cd", 1), None, OP.mult))
            P.wait("tensor", "vC", cdr_v)
            cdb_mm = P.top(lambda q: q.matmul(
                pb[6].ap()[:, 0:512], ones32_sb.ap(), cdr_sb.ap(), start=True, stop=True), mark=True)
            P.wait("vector", "tC", cdb_mm)
            P.wait("vector", "vC", hTb_v)
            hsc_v = P.cop("vector", lambda q, qtr=qtr: q.tensor_tensor(
                h_sb.ap()[:, 512 * qtr:512 * qtr + 512], h_sb.ap()[:, 512 * qtr:512 * qtr + 512],
                pb[6].ap()[:, 0:512], OP.mult))
            P.wait("tensor", "vC", hsc_v)
            for jt in range(2):
                st_mm = P.top(lambda q, jt=jt, qtr=qtr: q.matmul(
                    pb[6].ap()[:, 0:512], Btok_sb.ap()[:, 128 * jt:128 * jt + 128],
                    xw_sb.ap()[:, 2048 * jt + 512 * qtr: 2048 * jt + 512 * qtr + 512],
                    start=(jt == 0), stop=(jt == 1)), mark=(jt == 1))
            P.wait("vector", "tC", st_mm)
            P.cop("vector", lambda q, qtr=qtr: q.tensor_tensor(
                h_sb.ap()[:, 512 * qtr:512 * qtr + 512], h_sb.ap()[:, 512 * qtr:512 * qtr + 512],
                pb[6].ap()[:, 0:512], OP.add))

            for pp in range(4):
                g = 4 * qtr + pp
                j0 = 2 * pp
                P.wait("tensor", "vC", P.val("vC"))
                for hi in range(2):
                    col = 256 * (j0 + hi)
                    bankS = pb[2] if hi == 0 else pb[3]
                    P.top(lambda q, col=col, bankS=bankS: q.matmul(
                        bankS.ap()[:, 0:256], flats.ap()[0:3, col:col + 128],
                        flats.ap()[0:3, 2048 + col:2048 + col + 256], start=True, stop=True))
                    seg_mm = P.top(lambda q, col=col, bankS=bankS: q.matmul(
                        bankS.ap()[:, 256:384], flats.ap()[0:3, col + 128:col + 256],
                        flats.ap()[0:3, 2048 + col + 128:2048 + col + 256], start=True, stop=True), mark=True)
                P.wait("scalar", "tC", seg_mm)
                for hi in range(2):
                    bankS = pb[2] if hi == 0 else pb[3]
                    P.cop("scalar", lambda q, hi=hi, bankS=bankS: q.activation(
                        ST_sb[hi].ap()[:, 0:256], bankS.ap()[:, 0:256], AF.Exp))
                    seg_e_v = P.cop("scalar", lambda q, hi=hi, bankS=bankS: q.activation(
                        ST_sb[hi].ap()[:, 256:384], bankS.ap()[:, 256:384], AF.Exp))
                P.wait("vector", "sC", seg_e_v)
                P.wait("vector", "vC", cbt_v)
                for hi in range(2):
                    st_v = P.cop("vector", lambda q, hi=hi: q.scalar_tensor_tensor(
                        ST_sb[hi].ap()[:, 0:384], ST_sb[hi].ap()[:, 0:384], 1e30,
                        CBTm_sb.ap()[:, 0:384], OP.min, OP.mult), serial=(hi == 0))
                if c == 0 and g == 0 and tap_st is not None:
                    P.wait("sync", "vC", st_v)
                    P.dma("sync", tap_st, ST_sb[0].ap(), "dTap")
                    P.wait("sync", "dTap", P.val("dTap"))
                    P.wait("scalar", "dTap", P.val("dTap"))
                    P.wait("vector", "dTap", P.val("dTap"))

                P.wait("tensor", "vC", hTb_v)
                P.wait("tensor", "sC", P.val("sC"))
                P.wait("tensor", "vC", P.val("vC"))
                P.top(lambda q, g=g: q.matmul(
                    pb[4].ap()[:, 256:512], hT_bf.ap()[:, 128 * g:128 * g + 128], Ccm,
                    start=True, stop=True))
                P.top(lambda q, j0=j0: q.matmul(
                    pb[5].ap()[:, 256:512], flats.ap()[32:34, 0:128],
                    flats.ap()[32:34, 2048 + 256 * j0:2048 + 256 * j0 + 256], start=True, stop=True))
                P.wait("tensor", "vC", st_v)
                for hi in range(2):
                    P.top(lambda q, hi=hi, g=g: q.matmul(
                        pb[4].ap()[64 * hi:64 * hi + 64, 0:256],
                        xT_sb.ap()[:, 64 * (2 * g + hi): 64 * (2 * g + hi) + 64],
                        ST_sb[hi].ap()[:, 0:256],
                        start=True, stop=False, tile_position=(0, 64 * hi),
                        skip_group_check=True))
                for hi in range(2):
                    y_mm = P.top(lambda q, hi=hi, g=g: q.matmul(
                        pb[4].ap()[64 * hi:64 * hi + 64, 128:256],
                        xT_sb.ap()[:, 2048 + 64 * (2 * g + hi): 2048 + 64 * (2 * g + hi) + 64],
                        ST_sb[hi].ap()[:, 256:384],
                        start=False, stop=True, tile_position=(0, 64 * hi),
                        skip_group_check=True), mark=(hi == 1))
                P.wait("scalar", "tC", y_mm)
                P.wait("scalar", "vC", P.val("vC"))
                eb_v = P.cop("scalar", lambda q: q.activation(e_sb.ap(), pb[5].ap()[:, 256:512], AF.Copy))
                P.wait("vector", "sC", eb_v)
                if g == 0:
                    P.wait("vector", "dStF", P.val("dStF"))
                P.cop("vector", lambda q: q.tensor_tensor(ycomb.ap(), pb[4].ap()[:, 256:512], e_sb.ap(), OP.mult))
                P.cop("vector", lambda q: q.tensor_tensor(ycomb.ap(), ycomb.ap(), pb[4].ap()[:, 0:256], OP.add))
                ych_v = P.cop("vector", lambda q, g=g, yb=yb: q.scalar_tensor_tensor(
                    ypre_sb[yb].ap()[:, 256 * g:256 * g + 256],
                    xBC_sb.ap()[:, 259 * g + 3:259 * g + 259],
                    dpB_sb.ap()[:, g:g + 1], ycomb.ap(), OP.mult, OP.add))
        if c == 0 and tap_yp is not None:
            P.wait("sync", "vC", ych_v)
            P.dma("sync", tap_yp, ypre_sb[0].ap(), "dTap")
            P.wait("sync", "dTap", P.val("dTap"))
            P.wait("scalar", "dTap", P.val("dTap"))
            P.wait("vector", "dTap", P.val("dTap"))
        if c == 0 and tap_h is not None:
            P.wait("sync", "vC", P.val("vC"))
            P.dma("sync", tap_h, h_sb.ap(), "dTap")
            P.wait("sync", "dTap", P.val("dTap"))

        P.wait("scalar", "vC", ych_v)
        P.dma("scalar", S_d.ap()[half, :, 256 * (c % 8):256 * (c % 8) + 256]
              .rearrange("(g p) T -> p g T", p=128),
              ypre_sb[yb].ap().rearrange("p (g T) -> p g T", T=256), "dStF")

        prev_te_done = P.val("tC")
        prev_ych_v = ych_v
        if stop_after == f"chunk{c}":
            break

    st_ab.close()
    ab_stash_dma = P.val("dStF")

    ag_v = 0
    if stop_after is None:
        P.wait("gpsimd", "dStF", ab_stash_dma)
        ag_v = P.cop("gpsimd", lambda q: q.collective_compute(
            "AllGather", OP.bypass, replica_groups=[[0, 1], [2, 3], [4, 5], [6, 7]],
            ins=[S_d.ap()], outs=[G_d.ap()]))

    if stop_after is not None:
        zo = nc.alloc_sbuf_tensor("zo", [128, 512], F32)
        zv = P.cop("vector", lambda q: q.memset(zo.ap(), 0.0))
        P.wait("scalar", "vC", zv)
        for mt in range(16):
            for tb in range(4):
                P.dma("scalar", out_d[128 * mt:128 * mt + 128, 512 * tb:512 * tb + 512], zo.ap(), "dOut")
        P.wait("scalar", "dOut", P.val("dOut"))
        P.wait("scalar", "dTap", P.val("dTap"))
        P.wait("scalar", "dStF", P.val("dStF"))
    else:
        def fence():
            tv, vv, sv = P.val("tC"), P.val("vC"), P.val("sC")
            for e in ENGS:
                P.wait(e, "tC", tv)
                P.wait(e, "vC", vv)
                P.wait(e, "sC", sv)

        uTh3 = uTh.rearrange("(kt p) T -> p kt T", p=128)

        def ln_block(cb, dst_sb, dst_col0, rms_adjust, last_u_free, usb, ubf, usq):
            P.wait("sync", "vC", last_u_free)
            P.dma("sync", usb.ap().rearrange("p (kt T) -> p kt T", T=256),
                  uTh3[:, :, 256 * cb: 256 * cb + 256], "dU")
            P.wait("vector", "dU", P.val("dU"))
            ub_v = P.cop("vector", lambda q: q.tensor_copy(ubf.ap(), usb.ap()))
            P.wait("scalar", "vC", ub_v)
            us_v = P.cop("scalar", lambda q: q.activation(usq.ap(), ubf.ap(), AF.Square))
            P.wait("tensor", "vC", ub_v)
            P.wait("tensor", "sC", us_v)
            P.wait("tensor", "vC", P.val("vC"))
            for kt in range(NKT):
                P.top(lambda q, kt=kt: q.matmul(
                    pb[6].ap()[0:1, 0:256], ones_cb.ap(),
                    ubf.ap()[:, 256 * kt:256 * kt + 256],
                    start=(kt == 0), stop=(kt == NKT - 1)))
            for kt in range(NKT):
                smm = P.top(lambda q, kt=kt: q.matmul(
                    pb[6].ap()[0:1, 256:512], ones_cb.ap(),
                    usq.ap()[:, 256 * kt:256 * kt + 256],
                    start=(kt == 0), stop=(kt == NKT - 1)), mark=(kt == NKT - 1))
            P.wait("vector", "tC", smm)
            P.cop("vector", lambda q: q.tensor_scalar_mul(row1.ap()[0:1, 0:256], pb[6].ap()[0:1, 0:256], 1.0 / D))
            P.cop("vector", lambda q: q.tensor_scalar_mul(row2.ap()[0:1, 0:256], pb[6].ap()[0:1, 256:512], 1.0 / D))
            P.cop("vector", lambda q: q.tensor_mul(row3.ap()[0:1, 0:256], row1.ap()[0:1, 0:256], row1.ap()[0:1, 0:256]))
            sv2 = P.cop("vector", lambda q: q.tensor_sub(row2.ap()[0:1, 0:256], row2.ap()[0:1, 0:256], row3.ap()[0:1, 0:256]))
            if rms_adjust:
                sv2 = P.cop("vector", lambda q: q.tensor_scalar(
                    row2.ap()[0:1, 0:256], row2.ap()[0:1, 0:256],
                    1.0 + FRMS_EPS, FRMS_EPS * LN_EPS, OP.mult, OP.add))
                bias = 0.0
            else:
                bias = LN_EPS
            P.wait("scalar", "vC", sv2)
            sq2 = P.cop("scalar", lambda q, bias=bias: q.activation(
                row3.ap()[0:1, 0:256], row2.ap()[0:1, 0:256], AF.Sqrt, bias=bias))
            P.wait("vector", "sC", sq2)
            rs_v = P.cop("vector", lambda q: q.reciprocal(row3.ap()[0:1, 0:256], row3.ap()[0:1, 0:256]))
            P.wait("tensor", "vC", rs_v)
            P.wait("tensor", "sC", P.val("sC"))
            P.top(lambda q: q.matmul(pb[5].ap()[:, 0:256], ones_rf.ap(), row1.ap()[0:1, 0:256], start=True, stop=True))
            bmm = P.top(lambda q: q.matmul(pb[5].ap()[:, 256:512], ones_rf.ap(), row3.ap()[0:1, 0:256], start=True, stop=True), mark=True)
            P.wait("vector", "tC", bmm)
            for kt in range(NKT):
                P.cop("vector", lambda q, kt=kt: q.tensor_sub(
                    usb.ap()[:, 256 * kt:256 * kt + 256], usb.ap()[:, 256 * kt:256 * kt + 256],
                    pb[5].ap()[:, 0:256]), serial=(kt == 0))
            for kt in range(NKT):
                outc = dst_col0 + kt * TOK + 256 * cb
                if rms_adjust:
                    hv = P.cop("vector", lambda q, kt=kt, outc=outc: q.scalar_tensor_tensor(
                        dst_sb.ap()[:, outc:outc + 256], usb.ap()[:, 256 * kt:256 * kt + 256],
                        rmsw_sb.ap()[:, kt:kt + 1], pb[5].ap()[:, 256:512], OP.mult, OP.mult),
                        serial=(kt == 0))
                else:
                    hv = P.cop("vector", lambda q, kt=kt, outc=outc: q.tensor_tensor(
                        dst_sb.ap()[:, outc:outc + 256], usb.ap()[:, 256 * kt:256 * kt + 256],
                        pb[5].ap()[:, 256:512], OP.mult), serial=(kt == 0))
            return hv

        class WStream:

            def __init__(self, name, width, sem, scale_cols=None, nkt=16):
                self.stage = SBE(f"{name}_st", [128, width], F32)
                self.out = [SBE(f"{name}_o{i}", [128, width], BF16) for i in range(2)]
                self.sem = sem
                self.n = 0
                self.cast_v = [0, 0]

            def load(self, src_ap):
                dst = self.stage.ap()
                if src_ap.ndim == 3:
                    dst = dst.rearrange("p (a b) -> p a b", b=src_ap.shape[2])
                i = self.n % 2
                if self.n >= 1:
                    P.wait("sync", "sC", self.cast_v[(self.n - 1) % 2])
                P.dma("sync", dst, src_ap, self.sem)
                dv = P.val(self.sem)
                P.wait("scalar", self.sem, dv)
                P.wait("scalar", "tC", P.val("tC"))
                cv = P.cop("scalar", lambda q, i=i: q.activation(
                    self.out[i].ap(), self.stage.ap(), AF.Copy))
                self.cast_v[i] = cv
                self.n += 1
                return self.out[i], cv

        fence()
        st_e = contextlib.ExitStack()
        SBE = lambda *a: st_e.enter_context(nc.sbuf_tensor(*a))
        h_ffn = SBE("h_ffn", [128, NKT * TOK], BF16)
        dacc = SBE("dacc", [128, 16 * 1024], F32)
        din = SBE("din", [128, 8 * 1024], BF16)
        lg_sb = SBE("lg_sb", [8, TOK], F32)

        st_p0b = contextlib.ExitStack()
        SBP = lambda *a: st_p0b.enter_context(nc.sbuf_tensor(*a))
        u_e = SBP("u_sbE", [128, NKT * 256], F32)
        ubf_e2 = SBP("ubf_e", [128, NKT * 256], BF16)
        usq_e2 = SBP("usq_e", [128, NKT * 256], BF16)
        lgt_sb = SBP("lgt_sb", [128, 16 * 8], F32)
        wn_sb = SBP("wn_sb", [128, 16 * 8], F32)
        rt1 = SBP("rt1", [128, 1], F32)
        rt2 = SBP("rt2", [128, 1], F32)
        rwb_sb = SBP("rwb_sb", [128, 128], BF16)

        hv = 0
        for cb in range(8):
            hv = ln_block(cb, h_ffn, 0, True, hv, u_e, ubf_e2, usq_e2)
        hffn_v = hv

        P.wait("sync", "vC", hv)
        P.dma("sync", u_e.ap()[:, 0:128].rearrange("p (kt e) -> p kt e", e=8),
              rw_d.rearrange("(kt p) e -> p kt e", p=128), "dW1")
        P.wait("vector", "dW1", P.val("dW1"))
        rww_v = P.cop("vector", lambda q, u=u_e: q.tensor_copy(
            rwb_sb.ap(), u.ap()[:, 0:128]))
        P.wait("tensor", "vC", max(rww_v, hffn_v))
        for tb in range(4):
            P.wait("tensor", "sC", P.val("sC"))
            for kt in range(NKT):
                lmm = P.top(lambda q, tb=tb, kt=kt: q.matmul(
                    pb[4].ap()[0:8, 0:512], rwb_sb.ap()[:, 8 * kt:8 * kt + 8],
                    h_ffn.ap()[:, TOK * kt + 512 * tb: TOK * kt + 512 * tb + 512],
                    start=(kt == 0), stop=(kt == NKT - 1)), mark=(kt == NKT - 1))
            P.wait("scalar", "tC", lmm)
            P.cop("scalar", lambda q, tb=tb: q.copy(lg_sb.ap()[:, 512 * tb:512 * tb + 512], pb[4].ap()[0:8, 0:512]))
        lg_v = P.val("sC")
        P.wait("tensor", "sC", lg_v)
        for tt in range(16):
            P.wait("tensor", "vC", P.val("vC"))
            tmm = P.top(lambda q, tt=tt: q.matmul(
                pb[4].ap()[:, 0:8], lg_sb.ap()[0:8, 128 * tt:128 * tt + 128],
                id_f.ap()[0:8, 0:8], start=True, stop=True, is_transpose=True), mark=True)
            P.wait("vector", "tC", tmm)
            P.cop("vector", lambda q, tt=tt: q.tensor_copy(lgt_sb.ap()[:, 8 * tt:8 * tt + 8], pb[4].ap()[:, 0:8]))
        for tt in range(16):
            L = lambda: lgt_sb.ap()[:, 8 * tt:8 * tt + 8]
            P.cop("vector", lambda q, tt=tt: q.reduce_max(rt1.ap(), lgt_sb.ap()[:, 8 * tt:8 * tt + 8], AX.X))
            sub_v = P.cop("vector", lambda q, tt=tt: q.tensor_scalar(
                lgt_sb.ap()[:, 8 * tt:8 * tt + 8], lgt_sb.ap()[:, 8 * tt:8 * tt + 8], rt1.ap()[:, 0:1], None, OP.subtract))
            P.wait("scalar", "vC", sub_v)
            ex_v = P.cop("scalar", lambda q, tt=tt: q.activation(
                lgt_sb.ap()[:, 8 * tt:8 * tt + 8], lgt_sb.ap()[:, 8 * tt:8 * tt + 8], AF.Exp))
            P.wait("vector", "sC", ex_v)
            P.cop("vector", lambda q, tt=tt: q.reduce_sum(rt1.ap(), lgt_sb.ap()[:, 8 * tt:8 * tt + 8], AX.X))
            P.cop("vector", lambda q: q.reciprocal(rt2.ap(), rt1.ap()))
            P.cop("vector", lambda q, tt=tt: q.tensor_scalar(
                lgt_sb.ap()[:, 8 * tt:8 * tt + 8], lgt_sb.ap()[:, 8 * tt:8 * tt + 8], rt2.ap()[:, 0:1], None, OP.mult))
            P.cop("vector", lambda q, tt=tt: q.reduce_max(rt1.ap(), lgt_sb.ap()[:, 8 * tt:8 * tt + 8], AX.X))
            P.cop("vector", lambda q, tt=tt: q.tensor_scalar(
                wn_sb.ap()[:, 8 * tt:8 * tt + 8], lgt_sb.ap()[:, 8 * tt:8 * tt + 8], rt1.ap()[:, 0:1], None, OP.is_ge))
            P.cop("vector", lambda q, tt=tt: q.scalar_tensor_tensor(
                wn_sb.ap()[:, 8 * tt:8 * tt + 8], wn_sb.ap()[:, 8 * tt:8 * tt + 8], -1e9,
                lgt_sb.ap()[:, 8 * tt:8 * tt + 8], OP.mult, OP.add))
            P.cop("vector", lambda q, tt=tt: q.reduce_max(rt2.ap(), wn_sb.ap()[:, 8 * tt:8 * tt + 8], AX.X))
            P.cop("vector", lambda q, tt=tt: q.tensor_scalar(
                wn_sb.ap()[:, 8 * tt:8 * tt + 8], lgt_sb.ap()[:, 8 * tt:8 * tt + 8], rt2.ap()[:, 0:1], None, OP.is_ge))
            P.cop("vector", lambda q, tt=tt: q.tensor_mul(
                wn_sb.ap()[:, 8 * tt:8 * tt + 8], wn_sb.ap()[:, 8 * tt:8 * tt + 8], lgt_sb.ap()[:, 8 * tt:8 * tt + 8]))
            P.cop("vector", lambda q, tt=tt: q.reduce_sum(rt1.ap(), wn_sb.ap()[:, 8 * tt:8 * tt + 8], AX.X))
            P.cop("vector", lambda q: q.reciprocal(rt2.ap(), rt1.ap()))
            wnv = P.cop("vector", lambda q, tt=tt: q.tensor_scalar(
                wn_sb.ap()[:, 8 * tt:8 * tt + 8], wn_sb.ap()[:, 8 * tt:8 * tt + 8], rt2.ap()[:, 0:1], None, OP.mult))
        P.wait("tensor", "vC", wnv)
        for tt in range(16):
            P.wait("tensor", "sC", P.val("sC"))
            tm2 = P.top(lambda q, tt=tt: q.matmul(
                pb[4].ap()[0:8, 0:128], wn_sb.ap()[:, 8 * tt:8 * tt + 8],
                id_f.ap(), start=True, stop=True, is_transpose=True), mark=True)
            P.wait("scalar", "tC", tm2)
            P.cop("scalar", lambda q, tt=tt: q.copy(lg_sb.ap()[:, 128 * tt:128 * tt + 128], pb[4].ap()[0:8, 0:128]))
        wdT_v = P.val("sC")
        fence()
        st_p0b.close()
        wfl = SBE("wfl", [1, 1024], F32)
        wB_sb = SBE("wB_sb", [128, 1024], BF16)
        sgt = SBE("sgt", [128, 512], BF16)
        t1b = SBE("t1b", [128, 512], BF16)
        fstg = SBE("fstg", [128, 1024], BF16)

        wg3 = wg_d.rearrange("e (kt p) f -> e p kt f", p=128)
        wu3 = wu_d.rearrange("e (kt p) f -> e p kt f", p=128)
        wd3 = wd_d.rearrange("e (kf p) d -> e p kf d", p=128)
        swg3 = swg_d.rearrange("(kt p) f -> p kt f", p=128)
        swu3 = swu_d.rearrange("(kt p) f -> p kt f", p=128)
        swd3 = swd_d.rearrange("(kf p) d -> p kf d", p=128)

        ws_gu = WStream("wsgu", 2048, "dW2", rmsw_sb.ap())
        ws_d = WStream("wsd", 1024, "dW4", None, nkt=8)

        for hf in range(2):
            t0 = 1024 * hf
            z0 = P.cop("vector", lambda q: q.memset(dacc.ap(), 0.0))
            for e in range(9):
                shared = (e == 8)
                if not shared:
                    P.wait("sync", "sC", wdT_v)
                    P.wait("sync", "tC", P.val("tC"))
                    P.dma("sync", wfl.ap(), lg_sb.ap()[e:e + 1, t0:t0 + 1024], "dW1")
                    P.wait("tensor", "dW1", P.val("dW1"))
                    P.wait("scalar", "gC", P.val("gC"))
                    for tb2 in range(2):
                        P.wait("tensor", "sC", P.val("sC"))
                        wbm = P.top(lambda q, tb2=tb2: q.matmul(
                            pb[6].ap()[:, 0:512], ones_rf.ap(),
                            wfl.ap()[0:1, 512 * tb2:512 * tb2 + 512], start=True, stop=True), mark=True)
                        P.wait("scalar", "tC", wbm)
                        wb_v = P.cop("scalar", lambda q, tb2=tb2: q.activation(
                            wB_sb.ap()[:, 512 * tb2:512 * tb2 + 512], pb[6].ap()[:, 0:512], AF.Copy))
                for mt in range(8):
                    if shared:
                        gsrc = swg3[:, :, 128 * mt:128 * mt + 128]
                        usrc = swu3[:, :, 128 * mt:128 * mt + 128]
                    else:
                        gsrc = wg3[e, :, :, 128 * mt:128 * mt + 128]
                        usrc = wu3[e, :, :, 128 * mt:128 * mt + 128]
                    wgb, gcv = ws_gu.load(gsrc)
                    wub, ucv = ws_gu.load(usrc)
                    P.wait("tensor", "sC", gcv)
                    P.wait("tensor", "vC", P.val("vC"))
                    P.wait("tensor", "vC", hffn_v)
                    P.wait("tensor", "gC", P.val("gC"))
                    for kt in range(NKT):
                        P.top(lambda q, kt=kt, wgb=wgb, t0=t0: q.matmul(
                            pb[0].ap(), wgb.ap()[:, 128 * kt:128 * kt + 128],
                            h_ffn.ap()[:, TOK * kt + t0: TOK * kt + t0 + 512],
                            start=(kt == 0), stop=(kt == NKT - 1)))
                        P.top(lambda q, kt=kt, wgb=wgb, t0=t0: q.matmul(
                            pb[1].ap(), wgb.ap()[:, 128 * kt:128 * kt + 128],
                            h_ffn.ap()[:, TOK * kt + t0 + 512: TOK * kt + t0 + 1024],
                            start=(kt == 0), stop=(kt == NKT - 1)))
                    P.wait("tensor", "sC", ucv)
                    for kt in range(NKT):
                        P.top(lambda q, kt=kt, wub=wub, t0=t0: q.matmul(
                            pb[2].ap(), wub.ap()[:, 128 * kt:128 * kt + 128],
                            h_ffn.ap()[:, TOK * kt + t0: TOK * kt + t0 + 512],
                            start=(kt == 0), stop=(kt == NKT - 1)))
                        gumm = P.top(lambda q, kt=kt, wub=wub, t0=t0: q.matmul(
                            pb[3].ap(), wub.ap()[:, 128 * kt:128 * kt + 128],
                            h_ffn.ap()[:, TOK * kt + t0 + 512: TOK * kt + t0 + 1024],
                            start=(kt == 0), stop=(kt == NKT - 1)), mark=(kt == NKT - 1))
                    P.wait("scalar", "tC", gumm)
                    for tb2 in range(2):
                        pg = pb[0] if tb2 == 0 else pb[1]
                        pu = pb[2] if tb2 == 0 else pb[3]
                        P.wait("scalar", "gC", P.val("gC"))
                        sg_v = P.cop("scalar", lambda q, pg=pg: q.activation(sgt.ap(), pg.ap(), AF.Silu))
                        P.wait("vector", "sC", sg_v)
                        tt_v = P.cop("vector", lambda q, pu=pu: q.tensor_tensor(t1b.ap(), pu.ap(), sgt.ap(), OP.mult))
                        P.wait("gpsimd", "vC", tt_v)
                        if mt == 0 and tb2 == 0:
                            P.wait("gpsimd", "tC", P.val("tC"))
                        if shared:
                            dv = P.cop("gpsimd", lambda q, mt=mt, tb2=tb2: q.tensor_copy(
                                din.ap()[:, 1024 * mt + 512 * tb2: 1024 * mt + 512 * tb2 + 512], t1b.ap()))
                        else:
                            P.wait("gpsimd", "sC", wb_v)
                            dv = P.cop("gpsimd", lambda q, mt=mt, tb2=tb2: q.tensor_tensor(
                                din.ap()[:, 1024 * mt + 512 * tb2: 1024 * mt + 512 * tb2 + 512],
                                t1b.ap(), wB_sb.ap()[:, 512 * tb2:512 * tb2 + 512], OP.mult))
                din_v = P.val("gC")
                for md in range(16):
                    if shared:
                        dsrc = swd3[:, :, 128 * md:128 * md + 128]
                    else:
                        dsrc = wd3[e, :, :, 128 * md:128 * md + 128]
                    wdb, dcv = ws_d.load(dsrc)
                    P.wait("tensor", "sC", dcv)
                    P.wait("tensor", "gC", din_v)
                    P.wait("tensor", "vC", P.val("vC"))
                    for kf in range(8):
                        P.top(lambda q, kf=kf, wdb=wdb: q.matmul(
                            pb[4].ap(), wdb.ap()[:, 128 * kf:128 * kf + 128],
                            din.ap()[:, 1024 * kf: 1024 * kf + 512],
                            start=(kf == 0), stop=(kf == 7)))
                        dmm = P.top(lambda q, kf=kf, wdb=wdb: q.matmul(
                            pb[5].ap(), wdb.ap()[:, 128 * kf:128 * kf + 128],
                            din.ap()[:, 1024 * kf + 512: 1024 * kf + 1024],
                            start=(kf == 0), stop=(kf == 7)), mark=(kf == 7))
                    P.wait("vector", "tC", dmm)
                    if e == 0:
                        P.wait("vector", "vC", z0)
                    P.cop("vector", lambda q, md=md: q.tensor_tensor(
                        dacc.ap()[:, 1024 * md: 1024 * md + 512],
                        dacc.ap()[:, 1024 * md: 1024 * md + 512], pb[4].ap(), OP.add))
                    P.cop("vector", lambda q, md=md: q.tensor_tensor(
                        dacc.ap()[:, 1024 * md + 512: 1024 * md + 1024],
                        dacc.ap()[:, 1024 * md + 512: 1024 * md + 1024], pb[5].ap(), OP.add))
            for md in range(16):
                P.wait("vector", "dFf", P.val("dFf"))
                fc_v = P.cop("vector", lambda q, md=md: q.tensor_copy(
                    fstg.ap(), dacc.ap()[:, 1024 * md:1024 * md + 1024]))
                P.wait("sync", "vC", fc_v)
                P.dma("sync", ffn_d.ap()[128 * md:128 * md + 128, t0:t0 + 1024], fstg.ap(), "dFf")
                P.wait("sync", "dFf", P.val("dFf"))
        st_e.close()

        fence()
        st_c = contextlib.ExitStack()
        SBE = lambda *a: st_c.enter_context(nc.sbuf_tensor(*a))
        xnh = SBE("xnh", [128, NKT * TOK], BF16)
        u_c = SBE("u_sbC", [128, NKT * 256], F32)
        ubf_c = SBE("ubf_c", [128, NKT * 256], BF16)
        usq_c = SBE("usq_c", [128, NKT * 256], BF16)
        g0t = [SBE(f"g0t{i}", [128, 512], BF16) for i in range(2)]
        g1t = [SBE(f"g1t{i}", [128, 512], BF16) for i in range(2)]
        ysel = SBE("ysel", [128, 512], BF16)
        szt = SBE("szt", [128, 512], BF16)
        gatb = SBE("gatb", [128, 512], BF16)
        sqt = SBE("sqt", [128, 512], BF16)
        sums_sb = SBE("sums_sb", [1, TOK], F32)
        ws_z = WStream("wsz", 2048, "dW2", None)

        hv = 0
        for cb in range(8):
            hv = ln_block(cb, xnh, 0, False, hv, u_c, ubf_c, usq_c)
        xnh_v = hv
        zs_v = P.cop("vector", lambda q: q.memset(sums_sb.ap(), 0.0))
        P.wait("gpsimd", "gC", P.val("gC"))

        if stop_after is None:
            P.wait("sync", "gC", ag_v)
        gdma_n = 0
        for mt in range(32):
            wzb, zcv = ws_z.load(Wz_d.rearrange("(kt p) z -> p kt z", p=128)[:, :, 128 * mt:128 * mt + 128])
            P.wait("tensor", "sC", zcv)
            P.wait("tensor", "vC", xnh_v)
            P.wait("tensor", "sC", P.val("sC"))
            for kt in range(NKT):
                for tb in range(4):
                    zmm = P.top(lambda q, kt=kt, tb=tb, wzb=wzb: q.matmul(
                        pb[tb].ap(), wzb.ap()[:, 128 * kt:128 * kt + 128],
                        xnh.ap()[:, TOK * kt + 512 * tb: TOK * kt + 512 * tb + 512],
                        start=(kt == 0), stop=(kt == NKT - 1)), mark=(kt == NKT - 1 and tb == 3))
            r, chp = mt // 16, mt % 16
            for tb in range(4):
                gb = gdma_n % 2
                P.wait("sync", "vC", P.val("vC"))
                P.dma("sync", g0t[gb].ap(), G_d.ap()[r, 0, 128 * chp:128 * chp + 128, 512 * tb:512 * tb + 512], f"dG{gb}")
                P.dma("sync", g1t[gb].ap(), G_d.ap()[r, 1, 128 * chp:128 * chp + 128, 512 * tb:512 * tb + 512], f"dG{gb}")
                gv = P.val(f"dG{gb}")
                gdma_n += 1
                P.wait("scalar", "tC", zmm)
                P.wait("scalar", "vC", P.val("vC"))
                sz_v = P.cop("scalar", lambda q, tb=tb: q.activation(szt.ap(), pb[tb].ap(), AF.Silu))
                P.wait("vector", f"dG{gb}", gv)
                P.cop("vector", lambda q, gb=gb: q.tensor_scalar(
                    ysel.ap(), g0t[gb].ap(), selm_sb.ap()[:, 0:1], None, OP.mult))
                P.cop("vector", lambda q, gb=gb: q.scalar_tensor_tensor(
                    ysel.ap(), g1t[gb].ap(), selm_sb.ap()[:, 1:2], ysel.ap(), OP.mult, OP.add))
                P.wait("vector", "sC", sz_v)
                P.wait("vector", "dGa", P.val("dGa"))
                gat_v = P.cop("vector", lambda q: q.tensor_tensor(gatb.ap(), ysel.ap(), szt.ap(), OP.mult))
                P.wait("scalar", "vC", gat_v)
                sq_v = P.cop("scalar", lambda q: q.activation(sqt.ap(), gatb.ap(), AF.Square))
                P.dma("scalar", gat_d.ap()[128 * mt:128 * mt + 128, 512 * tb:512 * tb + 512], gatb.ap(), "dGa")
                P.wait("scalar", "dGa", P.val("dGa"))
                P.wait("tensor", "sC", sq_v)
                P.wait("tensor", "vC", P.val("vC"))
                qmm = P.top(lambda q: q.matmul(
                    pb[6].ap()[0:1, 0:512], ones_cb.ap(), sqt.ap(), start=True, stop=True), mark=True)
                P.wait("vector", "tC", qmm)
                if mt == 0 and tb == 0:
                    P.wait("vector", "vC", zs_v)
                P.cop("vector", lambda q, tb=tb: q.tensor_tensor(
                    sums_sb.ap()[0:1, 512 * tb:512 * tb + 512], sums_sb.ap()[0:1, 512 * tb:512 * tb + 512],
                    pb[6].ap()[0:1, 0:512], OP.add))
        sums_v = P.val("vC")
        st_c.close()

        fence()
        st_d = contextlib.ExitStack()
        SBE = lambda *a: st_d.enter_context(nc.sbuf_tensor(*a))
        ynorm = SBE("ynorm", [128, 32 * 1024], BF16)
        rstdB = SBE("rstdB", [128, TOK], F32)
        row4 = SBE("row4", [1, TOK], F32)
        gstg = [SBE(f"gstg{i}", [128, 1024], BF16) for i in range(2)]
        ustg = [SBE(f"ustg{i}", [128, 512], F32) for i in range(2)]
        ffstg = [SBE(f"ffstg{i}", [128, 512], BF16) for i in range(2)]
        otb = [SBE(f"otb{i}", [128, 512], F32) for i in range(2)]
        ws_o = WStream("wso", 4096, "dW2", None, nkt=32)

        P.wait("vector", "vC", sums_v)
        P.cop("vector", lambda q: q.tensor_scalar_mul(row4.ap(), sums_sb.ap(), 1.0 / DIN))
        rw4 = P.val("vC")
        P.wait("scalar", "vC", rw4)
        sq4 = P.cop("scalar", lambda q: q.activation(row4.ap(), row4.ap(), AF.Sqrt, bias=MRMS_EPS))
        P.wait("vector", "sC", sq4)
        rc4 = P.cop("vector", lambda q: q.reciprocal(row4.ap(), row4.ap()))
        P.wait("tensor", "vC", rc4)
        for tb in range(4):
            P.wait("tensor", "sC", P.val("sC"))
            bmm = P.top(lambda q, tb=tb: q.matmul(
                pb[4].ap(), ones_rf.ap(), row4.ap()[0:1, 512 * tb:512 * tb + 512],
                start=True, stop=True), mark=True)
            P.wait("scalar", "tC", bmm)
            P.cop("scalar", lambda q, tb=tb: q.activation(
                rstdB.ap()[:, 512 * tb:512 * tb + 512], pb[4].ap(), AF.Copy))
        rstdB_v = P.val("sC")

        uTh3b = uTh.rearrange("(mo p) T -> p mo T", p=128)
        for hf in range(2):
            t0 = 1024 * hf
            gn = 0
            for kt in range(32):
                gb = gn % 2
                if gn >= 2:
                    P.wait("sync", "vC", P.val("vC"))
                P.dma("sync", gstg[gb].ap(), gat_d.ap()[128 * kt:128 * kt + 128, t0:t0 + 1024], f"dGs{gb}")
                gv = P.val(f"dGs{gb}")
                gn += 1
                P.wait("vector", f"dGs{gb}", gv)
                if kt == 0 and hf == 0:
                    P.wait("vector", "sC", rstdB_v)
                yv = P.cop("vector", lambda q, kt=kt, gb=gb, t0=t0: q.scalar_tensor_tensor(
                    ynorm.ap()[:, 1024 * kt:1024 * kt + 1024], gstg[gb].ap(),
                    nrmw_sb.ap()[:, kt:kt + 1], rstdB.ap()[:, t0:t0 + 1024], OP.mult, OP.mult))
            yn_v = yv
            for mo in range(16):
                wob, ocv = ws_o.load(Wo_d.rearrange("(kt p) d -> p kt d", p=128)[:, :, 128 * mo:128 * mo + 128])
                P.wait("tensor", "sC", ocv)
                P.wait("tensor", "vC", max(yn_v, P.val("vC") if mo > 0 else 0))
                P.wait("tensor", "gC", P.val("gC"))
                for kt in range(32):
                    for tb2 in range(2):
                        omm = P.top(lambda q, kt=kt, tb2=tb2, wob=wob: q.matmul(
                            pb[tb2].ap(), wob.ap()[:, 128 * kt:128 * kt + 128],
                            ynorm.ap()[:, 1024 * kt + 512 * tb2: 1024 * kt + 512 * tb2 + 512],
                            start=(kt == 0), stop=(kt == 31)), mark=(kt == 31 and tb2 == 1))
                for tb2 in range(2):
                    ub = tb2 % 2
                    P.wait("sync", "vC", P.val("vC"))
                    P.dma("sync", ustg[ub].ap(),
                          uTh3b[:, mo, t0 + 512 * tb2: t0 + 512 * tb2 + 512], "dU")
                    P.dma("sync", ffstg[ub].ap(),
                          ffn_d.ap()[128 * mo:128 * mo + 128, t0 + 512 * tb2: t0 + 512 * tb2 + 512], "dU")
                    uv = P.val("dU")
                    P.wait("vector", "tC", omm)
                    P.wait("vector", "dU", uv)
                    P.wait("vector", "gC", P.val("gC"))
                    P.wait("vector", "dOut", P.val("dOut"))
                    tv1 = P.cop("vector", lambda q, tb2=tb2, ub=ub: q.tensor_tensor(
                        otb[tb2].ap(), pb[tb2].ap(), ustg[ub].ap(), OP.add))
                    P.wait("gpsimd", "vC", tv1)
                    ov = P.cop("gpsimd", lambda q, tb2=tb2, ub=ub: q.tensor_tensor(
                        otb[tb2].ap(), otb[tb2].ap(), ffstg[ub].ap(), OP.add))
                    P.wait("gpsimd", "gC", ov)
                    P.dma("gpsimd", out_d[128 * mo:128 * mo + 128, t0 + 512 * tb2: t0 + 512 * tb2 + 512],
                          otb[tb2].ap(), "dOut")
        st_d.close()
        P.wait("gpsimd", "dOut", P.val("dOut"))
        P.wait("scalar", "dOut", P.val("dOut"))
        P.wait("scalar", "dStF", P.val("dStF"))
        P.wait("scalar", "dGa", P.val("dGa"))
        P.wait("scalar", "dFf", P.val("dFf"))

    with nc.Block() as block:
        @block.sync
        def _(q):
            for f in P.streams["sync"]:
                f(q)

        @block.scalar
        def _(q):
            for f in P.streams["scalar"]:
                f(q)

        @block.vector
        def _(q):
            for f in P.streams["vector"]:
                f(q)

        @block.tensor
        def _(q):
            for f in P.streams["tensor"]:
                f(q)

        @block.gpsimd
        def _(q):
            for f in P.streams["gpsimd"]:
                f(q)

    return nc, dbg_aps


def kernel(**inputs):
    in_maps = host_layout(inputs)
    nc, _ = build_program()
    res = run_bass_kernel_spmd(nc, in_maps, core_ids=list(range(8)))
    out = np.zeros((NB, LQ, D), dtype=np.float32)
    for c in range(8):
        b, s = c // 2, c % 2
        out[b, TOK * s:TOK * s + TOK, :] = res.results[c]["out"].T
    return out


# version 06
# speedup vs baseline: 1.1769x; 1.0757x over previous
import contextlib

import numpy as np
import ml_dtypes
import concourse.bass as bass
import concourse.mybir as mybir
from concourse.bass_utils import run_bass_kernel_spmd

F32 = mybir.dt.float32
BF16 = mybir.dt.bfloat16
AF = mybir.ActivationFunctionType
OP = mybir.AluOpType
AX = mybir.AxisListType

D = 2048
LQ = 4096
NB = 4
DIN = 4096
NHC = 32
HP = 64
DS = 128
CH = 256
NCHUNK = LQ // CH
CONVC = NHC * HP + 2 * DS
NCT = CONVC // 128
WAC = CONVC + NHC
NKT = D // 128
TOK = 2048
NE = 8
FF = 1024
SF = 1024
LN_EPS = 1e-5
MRMS_EPS = 1e-5
FRMS_EPS = 1e-6

ENGS = ("sync", "scalar", "vector", "tensor", "gpsimd")
CHAIN = {"scalar": "sC", "vector": "vC", "gpsimd": "gC"}


class Prog:
    def __init__(self, nc):
        self.nc = nc
        self.streams = {e: [] for e in ENGS}
        self.sems = {}
        self.cnt = {}

    def sem(self, name):
        if name not in self.sems:
            self.sems[name] = self.nc.alloc_semaphore(name)
            self.cnt[name] = 0
        return self.sems[name]

    def emit(self, eng, fn):
        self.streams[eng].append(fn)

    def wait(self, eng, sname, val=None):
        h = self.sem(sname)
        v = self.cnt[sname] if val is None else val
        if v > 0:
            self.emit(eng, lambda q, h=h, v=v: q.wait_ge(h, v))
        return v

    def cop(self, eng, fn, serial=True):
        ch = CHAIN[eng]
        h = self.sem(ch)
        prev = self.cnt[ch]
        if serial and prev > 0:
            self.emit(eng, lambda q, h=h, v=prev: q.wait_ge(h, v))
        self.cnt[ch] = prev + 1
        self.emit(eng, lambda q, fn=fn, h=h: fn(q).then_inc(h))
        return prev + 1

    def top(self, fn, mark=False):
        if not mark:
            self.emit("tensor", fn)
            return self.cnt.get("tC", 0)
        h = self.sem("tC")
        self.cnt["tC"] += 1
        self.emit("tensor", lambda q, fn=fn, h=h: fn(q).then_inc(h))
        return self.cnt["tC"]

    def dma(self, eng, dst, src, inc, slow=False):
        h = self.sem(inc)
        self.cnt[inc] += 16
        self.emit(eng, lambda q, d=dst, s=src, sl=slow, h=h: q.dma_start(
            out=d, in_=s, allow_slow_non_contiguous=sl).then_inc(h, 16))
        return self.cnt[inc]

    def val(self, sname):
        return self.cnt.get(sname, 0)


def host_layout(inputs):
    u = np.asarray(inputs["L0_attn_u"], dtype=np.float32)
    W_in = np.asarray(inputs["L0_attn_W_in"], dtype=np.float32)
    conv_w = np.asarray(inputs["L0_attn_conv_w"], dtype=np.float32)
    conv_b = np.asarray(inputs["L0_attn_conv_b"], dtype=np.float32)
    dt_bias = np.asarray(inputs["L0_attn_dt_bias"], dtype=np.float32)
    A_log = np.asarray(inputs["L0_attn_A_log"], dtype=np.float32)
    Dp = np.asarray(inputs["L0_attn_Dp"], dtype=np.float32)
    norm_w = np.asarray(inputs["L0_attn_norm_w"], dtype=np.float32)
    W_out = np.asarray(inputs["L0_attn_W_out"], dtype=np.float32)
    rms_w = np.asarray(inputs["L0_ffn_rms_weight"], dtype=np.float32)
    router_w = np.asarray(inputs["L0_ffn_router_w"], dtype=np.float32)
    w_gate = np.asarray(inputs["L0_ffn_w_gate"], dtype=np.float32)
    w_up = np.asarray(inputs["L0_ffn_w_up"], dtype=np.float32)
    w_down = np.asarray(inputs["L0_ffn_w_down"], dtype=np.float32)
    sw_gate = np.asarray(inputs["L0_ffn_sw_gate"], dtype=np.float32)
    sw_up = np.asarray(inputs["L0_ffn_sw_up"], dtype=np.float32)
    sw_down = np.asarray(inputs["L0_ffn_sw_down"], dtype=np.float32)

    in_maps = []
    for c in range(8):
        b, s = c // 2, c % 2
        hh = slice(32 * s, 32 * s + 32)
        xch = slice(DIN + 2048 * s, DIN + 2048 * s + 2048)
        WA = np.concatenate([W_in[xch], W_in[8192:8448],
                             W_in[8448 + 32 * s: 8448 + 32 * s + 32]], axis=0)
        cw = np.concatenate([conv_w[2048 * s:2048 * s + 2048], conv_w[4096:4352]], axis=0)
        cb = np.concatenate([conv_b[2048 * s:2048 * s + 2048], conv_b[4096:4352]], axis=0)
        m = {
            "uTf": np.ascontiguousarray(u[b].T),
            "uTh": np.ascontiguousarray(u[b, 2048 * s:2048 * s + 2048].T),
            "WA": np.ascontiguousarray(WA.T),
            "Wz": np.ascontiguousarray(W_in[0:DIN].T),
            "Wo": np.ascontiguousarray(W_out.T),
            "convw": np.ascontiguousarray(cw.reshape(NCT, 128, 4).transpose(1, 0, 2).reshape(128, NCT * 4)),
            "convb": np.ascontiguousarray(cb.reshape(NCT, 128).T),
            "dtb": np.ascontiguousarray(dt_bias[hh].reshape(32, 1)),
            "alog": np.ascontiguousarray(A_log[hh].reshape(32, 1)),
            "dpv": np.ascontiguousarray(Dp[hh].reshape(32, 1)),
            "nrmw": np.ascontiguousarray(norm_w.reshape(32, 128).T),
            "rmsw": np.ascontiguousarray(rms_w.reshape(16, 128).T),
            "rw": router_w,
            "wg": w_gate, "wu": w_up, "wd": w_down,
            "swg": sw_gate, "swu": sw_up, "swd": sw_down,
            "selm": np.ascontiguousarray(
                np.repeat(np.array([[1.0 - s, float(s)]], dtype=np.float32), 128, axis=0)),
        }
        in_maps.append(m)
    return in_maps


def build_program(debug=(), stop_after=None):
    nc = bass.Bass("TRN2", target_bir_lowering=False)
    P = Prog(nc)
    dbg = set(debug)
    dbg_aps = {}

    def mkdbg(name, shape, dtype=F32):
        if name in dbg:
            ap = nc.dram_tensor("dbg_" + name, list(shape), dtype, kind="ExternalOutput").ap()
            dbg_aps[name] = ap
            return ap
        return None

    uTf = nc.dram_tensor("uTf", [D, LQ], F32, kind="ExternalInput").ap()
    uTh = nc.dram_tensor("uTh", [D, TOK], F32, kind="ExternalInput").ap()
    WA_d = nc.dram_tensor("WA", [D, WAC], F32, kind="ExternalInput").ap()
    Wz_d = nc.dram_tensor("Wz", [D, DIN], F32, kind="ExternalInput").ap()
    Wo_d = nc.dram_tensor("Wo", [DIN, D], F32, kind="ExternalInput").ap()
    convw_d = nc.dram_tensor("convw", [128, NCT * 4], F32, kind="ExternalInput").ap()
    convb_d = nc.dram_tensor("convb", [128, NCT], F32, kind="ExternalInput").ap()
    dtb_d = nc.dram_tensor("dtb", [32, 1], F32, kind="ExternalInput").ap()
    alog_d = nc.dram_tensor("alog", [32, 1], F32, kind="ExternalInput").ap()
    dp_d = nc.dram_tensor("dpv", [32, 1], F32, kind="ExternalInput").ap()
    nrmw_d = nc.dram_tensor("nrmw", [128, 32], F32, kind="ExternalInput").ap()
    rmsw_d = nc.dram_tensor("rmsw", [128, 16], F32, kind="ExternalInput").ap()
    rw_d = nc.dram_tensor("rw", [D, NE], F32, kind="ExternalInput").ap()
    wg_d = nc.dram_tensor("wg", [NE, D, FF], F32, kind="ExternalInput").ap()
    wu_d = nc.dram_tensor("wu", [NE, D, FF], F32, kind="ExternalInput").ap()
    wd_d = nc.dram_tensor("wd", [NE, FF, D], F32, kind="ExternalInput").ap()
    swg_d = nc.dram_tensor("swg", [D, SF], F32, kind="ExternalInput").ap()
    swu_d = nc.dram_tensor("swu", [D, SF], F32, kind="ExternalInput").ap()
    swd_d = nc.dram_tensor("swd", [SF, D], F32, kind="ExternalInput").ap()
    selm_d = nc.dram_tensor("selm", [128, 2], F32, kind="ExternalInput").ap()
    out_d = nc.dram_tensor("out", [D, TOK], F32, kind="ExternalOutput").ap()

    S_d = nc.dram_tensor("S", [2 * D, TOK], BF16, kind="Internal")
    G_d = nc.dram_tensor("G", [2 * 2 * D, TOK], BF16, kind="Internal")
    gat_d = nc.dram_tensor("gat", [DIN, TOK], BF16, kind="Internal")
    ffn_d = nc.dram_tensor("ffn", [D, TOK], BF16, kind="Internal")

    id_bf_t = nc.inline_tensor(np.eye(128, dtype=ml_dtypes.bfloat16), name="t_id_bf")
    id_f_t = nc.inline_tensor(np.eye(128, dtype=np.float32), name="t_id_f")
    ones_cb_t = nc.inline_tensor(np.ones((128, 1), dtype=ml_dtypes.bfloat16), name="t_ones_cb")
    ones_rf_t = nc.inline_tensor(np.ones((1, 128), dtype=np.float32), name="t_ones_rf")
    psel = np.zeros((2, 128), dtype=np.float32)
    psel[0, 0:64] = 1.0
    psel[1, 64:128] = 1.0
    psel_t = nc.inline_tensor(psel, name="t_psel")
    mj = (np.arange(128)[:, None] <= np.arange(256)[None, :]).astype(ml_dtypes.bfloat16)
    mask_t = nc.inline_tensor(np.ascontiguousarray(mj), name="t_mask")
    ind = (np.arange(32)[:, None] == (np.arange(2048)[None, :] // 64)).astype(ml_dtypes.bfloat16)
    ind_t = nc.inline_tensor(np.ascontiguousarray(ind), name="t_ind")
    ones32_t = nc.inline_tensor(np.ones((32, 128), dtype=np.float32), name="t_ones32")
    SCALAR_CONSTS = [LN_EPS, 1.0, MRMS_EPS, FRMS_EPS, 0.5, 2.0]
    sconst_t = nc.inline_tensor(
        np.ascontiguousarray(np.repeat(np.array([SCALAR_CONSTS], dtype=np.float32), 128, axis=0)),
        name="t_sconst")

    A = nc.alloc_sbuf_tensor
    id_bf = A("id_bf", [128, 128], BF16)
    id_f = A("id_f", [128, 128], F32)
    ones_cb = A("ones_cb", [128, 1], BF16)
    ones_rf = A("ones_rf", [1, 128], F32)
    mask_sb = A("mask_sb", [128, 256], BF16)
    convw_sb = A("convw_sb", [128, NCT * 4], F32)
    convb_sb = A("convb_sb", [128, NCT], F32)
    dtb_sb = A("dtb_sb", [32, 1], F32)
    alog_sb = A("alog_sb", [32, 1], F32)
    nA_sb = A("nA_sb", [32, 1], F32)
    ones32_sb = A("ones32_sb", [32, 128], F32)
    dpB_sb = A("dpB_sb", [128, 16], F32)
    selm_sb = A("selm_sb", [128, 2], F32)
    nrmw_sb = A("nrmw_sb", [128, 32], F32)
    rmsw_sb = A("rmsw_sb", [128, 16], F32)
    row1 = A("row1", [1, 512], F32)
    row2 = A("row2", [1, 512], F32)
    row3 = A("row3", [1, 512], F32)
    sconst_sb = A("sconst_sb", [128, len(SCALAR_CONSTS)], F32)
    for i, v in enumerate(SCALAR_CONSTS):
        nc.const_aps.register(F32, v, sconst_sb.ap()[:, i:i + 1])

    pb = [nc.alloc_psum_tensor(f"pb{i}", [128, 512], F32) for i in range(8)]

    for dst, src in [(id_bf, id_bf_t), (id_f, id_f_t), (ones_cb, ones_cb_t),
                     (ones_rf, ones_rf_t), (mask_sb, mask_t),
                     (sconst_sb, sconst_t), (ones32_sb, ones32_t)]:
        P.dma("sync", dst.ap(), src.ap(), "dIn")
    P.dma("sync", convw_sb.ap(), convw_d, "dIn")
    P.dma("sync", convb_sb.ap(), convb_d, "dIn")
    P.dma("sync", dtb_sb.ap(), dtb_d, "dIn")
    P.dma("sync", alog_sb.ap(), alog_d, "dIn")
    P.dma("sync", selm_sb.ap(), selm_d, "dIn")
    P.dma("sync", nrmw_sb.ap(), nrmw_d, "dIn")
    P.dma("sync", rmsw_sb.ap(), rmsw_d, "dIn")
    P.dma("gpsimd", dpB_sb.ap()[0:64, :],
          dp_d[0:32:2, 0:1].rearrange("a b -> b a").partition_broadcast(64).squeeze(1), "dGp", slow=True)
    P.dma("gpsimd", dpB_sb.ap()[64:128, :],
          dp_d[1:32:2, 0:1].rearrange("a b -> b a").partition_broadcast(64).squeeze(1), "dGp", slow=True)
    cload = P.val("dIn")

    P.wait("scalar", "dIn", cload)
    P.cop("scalar", lambda q: q.activation(nA_sb.ap(), alog_sb.ap(), AF.Exp))
    P.wait("vector", "sC")
    P.cop("vector", lambda q: q.tensor_scalar_mul(nA_sb.ap(), nA_sb.ap(), -1.0))

    st_ab = contextlib.ExitStack()
    SB = lambda *a: st_ab.enter_context(nc.sbuf_tensor(*a))
    WA_sb = SB("WA_sb", [128, NKT * WAC], BF16)
    xn_sb = [SB(f"xn_sb{i}", [128, NKT * 256], BF16) for i in range(2)]
    xBC_sb = SB("xBC_sb", [128, NCT * 259], BF16)
    convacc = SB("convacc", [128, 512], F32)
    xT_sb = SB("xT_sb", [128, 2 * 2048], BF16)
    xw_sb = SB("xw_sb", [128, 2 * 2048], BF16)
    usq_sb = xw_sb
    Btok_sb = SB("Btok_sb", [128, 2 * 128], BF16)
    wdtT_sb = SB("wdtT_sb", [128, 2 * 32], F32)
    h_sb = SB("h_sb", [128, NHC * HP], F32)
    hT_bf = SB("hT_bf", [128, NHC * HP], BF16)
    smalls = SB("smalls", [32, 2816], F32)
    flats = SB("flats", [128, 4096], F32)
    u_sb = SB("u_sb", [128, NKT * 256], F32)
    ind_sb = SB("ind_sb", [32, 2048], BF16)
    cdr_sb = SB("cdr_sb", [32, 512], F32)

    sm = {}
    off = 0
    for nm, w in [("dtraw", 256), ("dt", 256), ("cs", 256), ("negcs", 256),
                  ("expf", 256), ("decay", 256), ("wdt", 256), ("lndt", 256),
                  ("tmp1", 256), ("tmp2", 256), ("cd", 1), ("cse", 1)]:
        sm[nm] = off
        off += w

    def SM(name, w=256):
        return smalls.ap()[:, sm[name]:sm[name] + w]

    tap_xn = mkdbg("xn0", [128, NKT * 256], BF16)
    tap_xbc = mkdbg("xbc0", [128, NCT * 256], BF16)
    tap_dt = mkdbg("dt0", [32, 256])
    tap_cs = mkdbg("cs0", [32, 256])
    tap_st = mkdbg("st0", [128, 384], BF16)
    tap_h = mkdbg("h0", [128, NHC * HP])
    tap_yp = mkdbg("yp0", [128, 16 * 256], BF16)

    P.wait("vector", "dGp", P.val("dGp"))
    P.wait("vector", "dIn", cload)
    P.wait("gpsimd", "dIn", cload)
    P.cop("vector", lambda q: q.memset(h_sb.ap(), 0.0))
    for t in range(NCT):
        P.cop("vector", lambda q, t=t: q.memset(xBC_sb.ap()[:, 259 * t:259 * t + 3], 0.0), serial=False)
    P.cop("vector", lambda q: q.memset(flats.ap()[0:1, 0:2048], 1.0), serial=False)
    ab_init_v = P.cop("vector", lambda q: q.memset(flats.ap()[0:3, 2048:4096], 1.0), serial=False)
    P.wait("sync", "dIn", cload)
    P.dma("sync", flats.ap()[32:34, 0:128], psel_t.ap(), "dIn")
    P.dma("sync", ind_sb.ap(), ind_t.ap(), "dIn")
    P.wait("tensor", "dIn", P.val("dIn"))

    with nc.sbuf_tensor("wstage0", [128, WAC], F32) as ws0:
        for kt in range(NKT):
            if kt >= 1:
                P.wait("sync", "vC", P.val("vC"))
            P.dma("sync", ws0.ap(), WA_d[128 * kt:128 * kt + 128, :], "dWA")
            P.wait("vector", "dWA", P.val("dWA"))
            P.cop("vector", lambda q, kt=kt: q.tensor_copy(
                WA_sb.ap()[:, WAC * kt: WAC * kt + WAC], ws0.ap()), serial=False)
    wa_ready_v = P.val("vC")

    e_sb = SB("e_sb", [128, 256], F32)
    ycomb = SB("ycomb", [128, 256], F32)
    ypre_sb = [SB(f"ypre{i}", [128, 16 * 256], BF16) for i in range(2)]
    ST_sb = [SB(f"ST{i}", [128, 384], BF16) for i in range(2)]
    CBTm_sb = SB("CBTm", [128, 384], BF16)

    uTf3 = uTf.rearrange("(kt p) T -> p kt T", p=128)
    last_xn_v = 0
    prev_te_done = 0
    prev_ych_v = 0

    if stop_after == "s_init":
        chunk_range = []
    else:
        chunk_range = list(range(NCHUNK))
    for c in chunk_range:
        xb = c % 2
        half = c // 8
        yb = c % 2

        P.wait("sync", "vC", last_xn_v if c > 0 else wa_ready_v)
        P.dma("sync", u_sb.ap().rearrange("p (kt T) -> p kt T", T=256),
              uTf3[:, :, CH * c: CH * c + CH], "dU")
        P.wait("vector", "dU", P.val("dU"))
        P.wait("vector", "tC", prev_te_done)
        ubf_v = P.cop("vector", lambda q, xb=xb, u=u_sb: q.tensor_copy(xn_sb[xb].ap(), u.ap()))
        P.wait("scalar", "vC", ubf_v)
        P.wait("scalar", "tC", prev_te_done)
        usq_v = P.cop("scalar", lambda q, xb=xb: q.activation(usq_sb.ap(), xn_sb[xb].ap(), AF.Square))
        if c == 0 and stop_after == "s_p0a":
            break
        P.wait("tensor", "vC", ubf_v)
        P.wait("tensor", "sC", usq_v)
        P.wait("tensor", "vC", P.val("vC"))
        for kt in range(NKT):
            P.top(lambda q, kt=kt, xb=xb: q.matmul(
                pb[6].ap()[0:1, 0:256], ones_cb.ap(),
                xn_sb[xb].ap()[:, 256 * kt:256 * kt + 256],
                start=(kt == 0), stop=(kt == NKT - 1)))
        for kt in range(NKT):
            sums_mm = P.top(lambda q, kt=kt: q.matmul(
                pb[6].ap()[0:1, 256:512], ones_cb.ap(),
                usq_sb.ap()[:, 256 * kt:256 * kt + 256],
                start=(kt == 0), stop=(kt == NKT - 1)), mark=(kt == NKT - 1))
        if c == 0 and stop_after == "s_p0b":
            break
        P.wait("vector", "tC", sums_mm)
        P.cop("vector", lambda q: q.tensor_scalar_mul(row1.ap()[0:1, 0:256], pb[6].ap()[0:1, 0:256], 1.0 / D))
        P.cop("vector", lambda q: q.tensor_scalar_mul(row2.ap()[0:1, 0:256], pb[6].ap()[0:1, 256:512], 1.0 / D))
        P.cop("vector", lambda q: q.tensor_mul(row3.ap()[0:1, 0:256], row1.ap()[0:1, 0:256], row1.ap()[0:1, 0:256]))
        stats_v = P.cop("vector", lambda q: q.tensor_sub(row2.ap()[0:1, 0:256], row2.ap()[0:1, 0:256], row3.ap()[0:1, 0:256]))
        P.wait("scalar", "vC", stats_v)
        sq_v = P.cop("scalar", lambda q: q.activation(row3.ap()[0:1, 0:256], row2.ap()[0:1, 0:256], AF.Sqrt, bias=LN_EPS))
        P.wait("vector", "sC", sq_v)
        rstd_v = P.cop("vector", lambda q: q.reciprocal(row3.ap()[0:1, 0:256], row3.ap()[0:1, 0:256]))
        if c == 0 and stop_after == "s_p0c":
            break
        P.wait("tensor", "vC", rstd_v)
        P.wait("tensor", "sC", P.val("sC"))
        P.top(lambda q: q.matmul(pb[5].ap()[:, 0:256], ones_rf.ap(), row1.ap()[0:1, 0:256], start=True, stop=True))
        bc_mm = P.top(lambda q: q.matmul(pb[5].ap()[:, 256:512], ones_rf.ap(), row3.ap()[0:1, 0:256], start=True, stop=True), mark=True)
        P.wait("vector", "tC", bc_mm)
        if c == 0 and stop_after == "s_p0d":
            break
        for kt in range(NKT):
            P.cop("vector", lambda q, kt=kt, u=u_sb: q.tensor_sub(
                u.ap()[:, 256 * kt:256 * kt + 256], u.ap()[:, 256 * kt:256 * kt + 256],
                pb[5].ap()[:, 0:256]), serial=(kt == 0))
        if c == 0 and stop_after == "s_p0e":
            break
        for kt in range(NKT):
            xn_v = P.cop("vector", lambda q, kt=kt, xb=xb, u=u_sb: q.tensor_tensor(
                xn_sb[xb].ap()[:, 256 * kt:256 * kt + 256], u.ap()[:, 256 * kt:256 * kt + 256],
                pb[5].ap()[:, 256:512], OP.mult), serial=(kt == 0))
        last_xn_v = xn_v
        if c == 0 and tap_xn is not None:
            P.wait("sync", "vC", xn_v)
            P.dma("sync", tap_xn, xn_sb[0].ap(), "dTap")
            P.wait("sync", "dTap", P.val("dTap"))
            P.wait("vector", "dTap", P.val("dTap"))
            P.wait("tensor", "dTap", P.val("dTap"))


        P.wait("tensor", "vC", xn_v)
        if c == 0:
            P.wait("tensor", "vC", ab_init_v)
        evac_vals = []
        for mt in range(NCT + 1):
            bank = pb[mt % 2]
            mw = 128 if mt < NCT else 32
            if len(evac_vals) >= 2:
                P.wait("tensor", "sC", evac_vals[-2])
            for kt in range(NKT):
                mm_v = P.top(lambda q, mt=mt, kt=kt, bank=bank, mw=mw, xb=xb: q.matmul(
                    bank.ap()[0:mw, 0:256],
                    WA_sb.ap()[:, WAC * kt + 128 * mt: WAC * kt + 128 * mt + mw],
                    xn_sb[xb].ap()[:, 256 * kt:256 * kt + 256],
                    start=(kt == 0), stop=(kt == NKT - 1)), mark=(kt == NKT - 1))
            P.wait("scalar", "tC", mm_v)
            if mt == 0:
                P.wait("scalar", "tC", prev_te_done)
                P.wait("scalar", "vC", prev_ych_v)
            if mt < NCT:
                ev = P.cop("scalar", lambda q, mt=mt, bank=bank: q.copy(
                    xBC_sb.ap()[:, 259 * mt + 3:259 * mt + 259], bank.ap()[:, 0:256]))
            else:
                ev = P.cop("scalar", lambda q, bank=bank: q.copy(
                    SM("dtraw"), bank.ap()[0:32, 0:256]))
            evac_vals.append(ev)
        inproj_evac = evac_vals[-1]

        if c == 0 and stop_after == "s_inproj":
            break
        P.wait("vector", "sC", inproj_evac)
        silu_vals = []
        for t in range(NCT):
            ca = convacc.ap()[:, 256 * (t % 2):256 * (t % 2) + 256]
            if t >= 2:
                P.wait("vector", "sC", silu_vals[t - 2])
            P.cop("vector", lambda q, t=t, ca=ca: q.tensor_scalar(
                ca, xBC_sb.ap()[:, 259 * t + 0:259 * t + 256],
                convw_sb.ap()[:, 4 * t + 0:4 * t + 1], None, OP.mult))
            for k in (1, 2, 3):
                P.cop("vector", lambda q, t=t, k=k, ca=ca: q.scalar_tensor_tensor(
                    ca, xBC_sb.ap()[:, 259 * t + k:259 * t + k + 256],
                    convw_sb.ap()[:, 4 * t + k:4 * t + k + 1], ca, OP.mult, OP.add))
            P.cop("vector", lambda q, t=t: q.tensor_copy(
                xBC_sb.ap()[:, 259 * t + 0:259 * t + 3], xBC_sb.ap()[:, 259 * t + 256:259 * t + 259]))
            vv = P.val("vC")
            P.wait("scalar", "vC", vv)
            sv = P.cop("scalar", lambda q, t=t, ca=ca: q.activation(
                xBC_sb.ap()[:, 259 * t + 3:259 * t + 259], ca,
                AF.Silu, bias=convb_sb.ap()[:, t:t + 1]))
            silu_vals.append(sv)
        conv_done = P.val("sC")
        if c == 0 and tap_xbc is not None:
            P.wait("sync", "sC", conv_done)
            P.dma("sync", tap_xbc.rearrange("p (t T) -> p t T", T=256),
                  xBC_sb.ap().rearrange("p (t T) -> p t T", T=259)[:, :, 3:259], "dTap")
            P.wait("sync", "dTap", P.val("dTap"))
            P.wait("scalar", "dTap", P.val("dTap"))
            P.wait("vector", "dTap", P.val("dTap"))

        if c == 0 and stop_after == "s_conv":
            break
        P.wait("vector", "sC", inproj_evac)
        P.cop("vector", lambda q: q.tensor_scalar(SM("tmp1"), SM("dtraw"), dtb_sb.ap()[0:32, 0:1], None, OP.add))
        P.cop("vector", lambda q: q.tensor_scalar(SM("tmp2"), SM("tmp1"), 0.0, None, OP.max))
        relu_v = P.val("vC")
        P.wait("scalar", "vC", relu_v)
        P.cop("scalar", lambda q: q.activation(SM("dt"), SM("tmp1"), AF.Abs))
        P.cop("scalar", lambda q: q.activation(SM("dt"), SM("dt"), AF.Exp, scale=-1.0))
        ln_v = P.cop("scalar", lambda q: q.activation(SM("dt"), SM("dt"), AF.Ln, bias=1.0))
        P.wait("vector", "sC", ln_v)
        dt_v = P.cop("vector", lambda q: q.tensor_add(SM("dt"), SM("dt"), SM("tmp2")))
        P.wait("scalar", "vC", dt_v)
        P.cop("scalar", lambda q: q.activation(SM("lndt"), SM("dt"), AF.Ln))
        P.cop("vector", lambda q: q.tensor_scalar(SM("tmp1"), SM("dt"), nA_sb.ap()[0:32, 0:1], None, OP.mult))
        P.cop("vector", lambda q: q.tensor_tensor_scan(SM("cs"), SM("tmp1"), SM("tmp1"), 0.0, OP.add, OP.bypass))
        P.cop("vector", lambda q: q.tensor_scalar_mul(SM("negcs"), SM("cs"), -1.0))
        cse_v = P.cop("vector", lambda q: q.tensor_copy(SM("cse", 1), smalls.ap()[:, sm["cs"] + 255:sm["cs"] + 256]))
        P.wait("scalar", "vC", cse_v)
        P.cop("scalar", lambda q: q.activation(SM("expf"), SM("cs"), AF.Exp))
        P.cop("scalar", lambda q: q.activation(SM("decay"), SM("cs"), AF.Exp, scale=-1.0, bias=SM("cse", 1)))
        dexp_v = P.cop("scalar", lambda q: q.activation(SM("cd", 1), SM("cse", 1), AF.Exp))
        P.wait("vector", "sC", dexp_v)
        dtpath_v = P.cop("vector", lambda q: q.tensor_mul(SM("wdt"), SM("decay"), SM("dt")))
        if c == 0 and tap_dt is not None:
            P.wait("sync", "vC", dtpath_v)
            P.dma("sync", tap_dt, SM("dt"), "dTap")
            P.wait("sync", "dTap", P.val("dTap"))
            P.dma("sync", tap_cs, SM("cs"), "dTap")
            P.wait("sync", "dTap", P.val("dTap"))
            P.wait("scalar", "dTap", P.val("dTap"))
            P.wait("vector", "dTap", P.val("dTap"))

        if c == 0 and stop_after == "s_dt":
            break
        P.wait("tensor", "vC", dtpath_v)
        for jt in range(2):
            wdtT_mm = P.top(lambda q, jt=jt: q.matmul(
                pb[2].ap()[0:128, 32 * jt:32 * jt + 32], SM("wdt")[0:32, 128 * jt:128 * jt + 128],
                id_f.ap()[0:32, 0:32], start=True, stop=True, is_transpose=True), mark=(jt == 1))
        P.wait("vector", "tC", wdtT_mm)
        wdtT_v = P.cop("vector", lambda q: q.tensor_copy(wdtT_sb.ap()[:, 0:64], pb[2].ap()[:, 0:64]))
        if c == 0 and stop_after == "s_tr1":
            break
        P.wait("tensor", "sC", conv_done)
        tp_banks = [pb[7].ap().bitcast(BF16)[:, 0:128], pb[3].ap().bitcast(BF16)[:, 0:128]]
        tevac = []
        for t in range(17):
            for jt in range(2):
                tpos = 2 * t + jt
                bk = tp_banks[tpos % 2]
                if tpos >= 2:
                    P.wait("tensor", "vC", tevac[tpos - 2])
                tr_mm = P.top(lambda q, t=t, jt=jt, bk=bk: q.matmul(
                    bk, xBC_sb.ap()[:, 259 * t + 3 + 128 * jt: 259 * t + 3 + 128 * jt + 128],
                    id_bf.ap(), start=True, stop=True, is_transpose=True), mark=True)
                P.wait("vector", "tC", tr_mm)
                if t < 16:
                    ev = P.cop("vector", lambda q, t=t, jt=jt, bk=bk: q.tensor_copy(
                        xT_sb.ap()[:, 2048 * jt + 128 * t: 2048 * jt + 128 * t + 128],
                        bk), serial=False)
                else:
                    ev = P.cop("vector", lambda q, jt=jt, bk=bk: q.tensor_copy(
                        Btok_sb.ap()[:, 128 * jt:128 * jt + 128],
                        bk), serial=False)
                tevac.append(ev)
                if c == 0 and stop_after == "s_tr2" and t == 0:
                    break
            if c == 0 and stop_after == "s_tr2" and t == 0:
                break
        if c == 0 and stop_after == "s_tr2":
            break
        P.wait("vector", "vC", wdtT_v)
        for jt in range(2):
            for h in range(NHC):
                xw_v = P.cop("vector", lambda q, jt=jt, h=h: q.tensor_scalar(
                    xw_sb.ap()[:, 2048 * jt + 64 * h: 2048 * jt + 64 * h + 64],
                    xT_sb.ap()[:, 2048 * jt + 64 * h: 2048 * jt + 64 * h + 64],
                    wdtT_sb.ap()[:, 32 * jt + h:32 * jt + h + 1], None, OP.mult),
                    serial=(jt == 0 and h == 0))

        if c == 0 and stop_after == "s_transp":
            break
        Bcm = xBC_sb.ap()[:, 259 * 16 + 3: 259 * 16 + 259]
        Ccm = xBC_sb.ap()[:, 259 * 17 + 3: 259 * 17 + 259]
        P.wait("tensor", "vC", P.val("vC"))
        P.top(lambda q: q.matmul(pb[2].ap()[:, 0:256], Bcm[:, 0:128], Ccm, start=True, stop=True))
        cbt_mm = P.top(lambda q: q.matmul(pb[3].ap()[:, 0:128], Bcm[:, 128:256], Ccm[:, 128:256], start=True, stop=True), mark=True)
        P.wait("vector", "tC", cbt_mm)
        P.cop("vector", lambda q: q.tensor_tensor(CBTm_sb.ap()[:, 0:256], pb[2].ap()[:, 0:256], mask_sb.ap()[:, 0:256], OP.mult))
        cbt_v = P.cop("vector", lambda q: q.tensor_tensor(CBTm_sb.ap()[:, 256:384], pb[3].ap()[:, 0:128], mask_sb.ap()[:, 0:128], OP.mult))

        hTb_v = P.cop("vector", lambda q: q.tensor_copy(hT_bf.ap(), h_sb.ap()))

        if c == 0 and stop_after == "s_cbt":
            break
        for qtr in range(4):
            P.wait("sync", "vC", dtpath_v)
            P.wait("sync", "tC", P.val("tC"))
            P.dma("sync", flats.ap()[1:2, 0:2048], smalls.ap()[8 * qtr:8 * qtr + 8, sm["negcs"]:sm["negcs"] + 256], "dFl")
            P.dma("sync", flats.ap()[2:3, 0:2048], smalls.ap()[8 * qtr:8 * qtr + 8, sm["lndt"]:sm["lndt"] + 256], "dFl")
            P.dma("sync", flats.ap()[0:1, 2048:4096], smalls.ap()[8 * qtr:8 * qtr + 8, sm["cs"]:sm["cs"] + 256], "dFl")
            P.dma("sync", flats.ap()[32:33, 2048:4096], smalls.ap()[8 * qtr:8 * qtr + 8, sm["expf"]:sm["expf"] + 256], "dFl")
            if qtr < 3:
                P.dma("sync", flats.ap()[33:34, 2048:4096],
                      smalls.ap()[8 * qtr + 1:8 * qtr + 9, sm["expf"]:sm["expf"] + 256], "dFl")
            else:
                P.dma("sync", flats.ap()[33:34, 2048:3840],
                      smalls.ap()[8 * qtr + 1:8 * qtr + 8, sm["expf"]:sm["expf"] + 256], "dFl")
            fl_v = P.val("dFl")
            P.wait("tensor", "dFl", fl_v)

            P.wait("vector", "sC", dexp_v)
            cdr_v = P.cop("vector", lambda q, qtr=qtr: q.tensor_scalar(
                cdr_sb.ap(), ind_sb.ap()[:, 512 * qtr:512 * qtr + 512], SM("cd", 1), None, OP.mult))
            P.wait("tensor", "vC", cdr_v)
            cdb_mm = P.top(lambda q: q.matmul(
                pb[6].ap()[:, 0:512], ones32_sb.ap(), cdr_sb.ap(), start=True, stop=True), mark=True)
            P.wait("vector", "tC", cdb_mm)
            P.wait("vector", "vC", hTb_v)
            hsc_v = P.cop("vector", lambda q, qtr=qtr: q.tensor_tensor(
                h_sb.ap()[:, 512 * qtr:512 * qtr + 512], h_sb.ap()[:, 512 * qtr:512 * qtr + 512],
                pb[6].ap()[:, 0:512], OP.mult))
            P.wait("tensor", "vC", hsc_v)
            for jt in range(2):
                st_mm = P.top(lambda q, jt=jt, qtr=qtr: q.matmul(
                    pb[6].ap()[:, 0:512], Btok_sb.ap()[:, 128 * jt:128 * jt + 128],
                    xw_sb.ap()[:, 2048 * jt + 512 * qtr: 2048 * jt + 512 * qtr + 512],
                    start=(jt == 0), stop=(jt == 1)), mark=(jt == 1))
            P.wait("vector", "tC", st_mm)
            P.cop("vector", lambda q, qtr=qtr: q.tensor_tensor(
                h_sb.ap()[:, 512 * qtr:512 * qtr + 512], h_sb.ap()[:, 512 * qtr:512 * qtr + 512],
                pb[6].ap()[:, 0:512], OP.add))

            for pp in range(4):
                g = 4 * qtr + pp
                j0 = 2 * pp
                P.wait("tensor", "vC", P.val("vC"))
                for hi in range(2):
                    col = 256 * (j0 + hi)
                    bankS = pb[2] if hi == 0 else pb[3]
                    P.top(lambda q, col=col, bankS=bankS: q.matmul(
                        bankS.ap()[:, 0:256], flats.ap()[0:3, col:col + 128],
                        flats.ap()[0:3, 2048 + col:2048 + col + 256], start=True, stop=True))
                    seg_mm = P.top(lambda q, col=col, bankS=bankS: q.matmul(
                        bankS.ap()[:, 256:384], flats.ap()[0:3, col + 128:col + 256],
                        flats.ap()[0:3, 2048 + col + 128:2048 + col + 256], start=True, stop=True), mark=True)
                P.wait("scalar", "tC", seg_mm)
                for hi in range(2):
                    bankS = pb[2] if hi == 0 else pb[3]
                    P.cop("scalar", lambda q, hi=hi, bankS=bankS: q.activation(
                        ST_sb[hi].ap()[:, 0:256], bankS.ap()[:, 0:256], AF.Exp))
                    seg_e_v = P.cop("scalar", lambda q, hi=hi, bankS=bankS: q.activation(
                        ST_sb[hi].ap()[:, 256:384], bankS.ap()[:, 256:384], AF.Exp))
                P.wait("vector", "sC", seg_e_v)
                P.wait("vector", "vC", cbt_v)
                for hi in range(2):
                    st_v = P.cop("vector", lambda q, hi=hi: q.scalar_tensor_tensor(
                        ST_sb[hi].ap()[:, 0:384], ST_sb[hi].ap()[:, 0:384], 1e30,
                        CBTm_sb.ap()[:, 0:384], OP.min, OP.mult), serial=(hi == 0))
                if c == 0 and g == 0 and tap_st is not None:
                    P.wait("sync", "vC", st_v)
                    P.dma("sync", tap_st, ST_sb[0].ap(), "dTap")
                    P.wait("sync", "dTap", P.val("dTap"))
                    P.wait("scalar", "dTap", P.val("dTap"))
                    P.wait("vector", "dTap", P.val("dTap"))

                P.wait("tensor", "vC", hTb_v)
                P.wait("tensor", "sC", P.val("sC"))
                P.wait("tensor", "vC", P.val("vC"))
                P.top(lambda q, g=g: q.matmul(
                    pb[4].ap()[:, 256:512], hT_bf.ap()[:, 128 * g:128 * g + 128], Ccm,
                    start=True, stop=True))
                P.top(lambda q, j0=j0: q.matmul(
                    pb[5].ap()[:, 256:512], flats.ap()[32:34, 0:128],
                    flats.ap()[32:34, 2048 + 256 * j0:2048 + 256 * j0 + 256], start=True, stop=True))
                P.wait("tensor", "vC", st_v)
                for hi in range(2):
                    P.top(lambda q, hi=hi, g=g: q.matmul(
                        pb[4].ap()[64 * hi:64 * hi + 64, 0:256],
                        xT_sb.ap()[:, 64 * (2 * g + hi): 64 * (2 * g + hi) + 64],
                        ST_sb[hi].ap()[:, 0:256],
                        start=True, stop=False, tile_position=(0, 64 * hi),
                        skip_group_check=True))
                for hi in range(2):
                    y_mm = P.top(lambda q, hi=hi, g=g: q.matmul(
                        pb[4].ap()[64 * hi:64 * hi + 64, 128:256],
                        xT_sb.ap()[:, 2048 + 64 * (2 * g + hi): 2048 + 64 * (2 * g + hi) + 64],
                        ST_sb[hi].ap()[:, 256:384],
                        start=False, stop=True, tile_position=(0, 64 * hi),
                        skip_group_check=True), mark=(hi == 1))
                P.wait("scalar", "tC", y_mm)
                P.wait("scalar", "vC", P.val("vC"))
                eb_v = P.cop("scalar", lambda q: q.activation(e_sb.ap(), pb[5].ap()[:, 256:512], AF.Copy))
                P.wait("vector", "sC", eb_v)
                if g == 0:
                    P.wait("vector", "dStF", P.val("dStF"))
                P.cop("vector", lambda q: q.tensor_tensor(ycomb.ap(), pb[4].ap()[:, 256:512], e_sb.ap(), OP.mult))
                P.cop("vector", lambda q: q.tensor_tensor(ycomb.ap(), ycomb.ap(), pb[4].ap()[:, 0:256], OP.add))
                ych_v = P.cop("vector", lambda q, g=g, yb=yb: q.scalar_tensor_tensor(
                    ypre_sb[yb].ap()[:, 256 * g:256 * g + 256],
                    xBC_sb.ap()[:, 259 * g + 3:259 * g + 259],
                    dpB_sb.ap()[:, g:g + 1], ycomb.ap(), OP.mult, OP.add))
        if c == 0 and tap_yp is not None:
            P.wait("sync", "vC", ych_v)
            P.dma("sync", tap_yp, ypre_sb[0].ap(), "dTap")
            P.wait("sync", "dTap", P.val("dTap"))
            P.wait("scalar", "dTap", P.val("dTap"))
            P.wait("vector", "dTap", P.val("dTap"))
        if c == 0 and tap_h is not None:
            P.wait("sync", "vC", P.val("vC"))
            P.dma("sync", tap_h, h_sb.ap(), "dTap")
            P.wait("sync", "dTap", P.val("dTap"))

        P.wait("scalar", "vC", ych_v)
        P.dma("scalar", S_d.ap()[D * half: D * half + D, 256 * (c % 8):256 * (c % 8) + 256]
              .rearrange("(g p) T -> p g T", p=128),
              ypre_sb[yb].ap().rearrange("p (g T) -> p g T", T=256), "dStF")

        prev_te_done = P.val("tC")
        prev_ych_v = ych_v
        if stop_after == f"chunk{c}":
            break

    st_ab.close()
    ab_stash_dma = P.val("dStF")

    ag_v = 0
    if stop_after is None:
        P.wait("gpsimd", "dStF", ab_stash_dma)
        ag_v = P.cop("gpsimd", lambda q: q.collective_compute(
            "AllGather", OP.bypass, replica_groups=[[0, 1], [2, 3], [4, 5], [6, 7]],
            ins=[S_d.ap()], outs=[G_d.ap()]))

    if stop_after is not None:
        zo = nc.alloc_sbuf_tensor("zo", [128, 512], F32)
        zv = P.cop("vector", lambda q: q.memset(zo.ap(), 0.0))
        P.wait("scalar", "vC", zv)
        for mt in range(16):
            for tb in range(4):
                P.dma("scalar", out_d[128 * mt:128 * mt + 128, 512 * tb:512 * tb + 512], zo.ap(), "dOut")
        P.wait("scalar", "dOut", P.val("dOut"))
        P.wait("scalar", "dTap", P.val("dTap"))
        P.wait("scalar", "dStF", P.val("dStF"))
    else:
        def fence():
            tv, vv, sv = P.val("tC"), P.val("vC"), P.val("sC")
            for e in ENGS:
                P.wait(e, "tC", tv)
                P.wait(e, "vC", vv)
                P.wait(e, "sC", sv)

        uTh3 = uTh.rearrange("(kt p) T -> p kt T", p=128)

        def ln_block(cb, dst_sb, dst_col0, rms_adjust, last_u_free, usb, ubf, usq):
            P.wait("sync", "vC", last_u_free)
            P.dma("sync", usb.ap().rearrange("p (kt T) -> p kt T", T=256),
                  uTh3[:, :, 256 * cb: 256 * cb + 256], "dU")
            P.wait("vector", "dU", P.val("dU"))
            ub_v = P.cop("vector", lambda q: q.tensor_copy(ubf.ap(), usb.ap()))
            P.wait("scalar", "vC", ub_v)
            us_v = P.cop("scalar", lambda q: q.activation(usq.ap(), ubf.ap(), AF.Square))
            P.wait("tensor", "vC", ub_v)
            P.wait("tensor", "sC", us_v)
            P.wait("tensor", "vC", P.val("vC"))
            for kt in range(NKT):
                P.top(lambda q, kt=kt: q.matmul(
                    pb[6].ap()[0:1, 0:256], ones_cb.ap(),
                    ubf.ap()[:, 256 * kt:256 * kt + 256],
                    start=(kt == 0), stop=(kt == NKT - 1)))
            for kt in range(NKT):
                smm = P.top(lambda q, kt=kt: q.matmul(
                    pb[6].ap()[0:1, 256:512], ones_cb.ap(),
                    usq.ap()[:, 256 * kt:256 * kt + 256],
                    start=(kt == 0), stop=(kt == NKT - 1)), mark=(kt == NKT - 1))
            P.wait("vector", "tC", smm)
            P.cop("vector", lambda q: q.tensor_scalar_mul(row1.ap()[0:1, 0:256], pb[6].ap()[0:1, 0:256], 1.0 / D))
            P.cop("vector", lambda q: q.tensor_scalar_mul(row2.ap()[0:1, 0:256], pb[6].ap()[0:1, 256:512], 1.0 / D))
            P.cop("vector", lambda q: q.tensor_mul(row3.ap()[0:1, 0:256], row1.ap()[0:1, 0:256], row1.ap()[0:1, 0:256]))
            sv2 = P.cop("vector", lambda q: q.tensor_sub(row2.ap()[0:1, 0:256], row2.ap()[0:1, 0:256], row3.ap()[0:1, 0:256]))
            if rms_adjust:
                sv2 = P.cop("vector", lambda q: q.tensor_scalar(
                    row2.ap()[0:1, 0:256], row2.ap()[0:1, 0:256],
                    1.0 + FRMS_EPS, FRMS_EPS * LN_EPS, OP.mult, OP.add))
                bias = 0.0
            else:
                bias = LN_EPS
            P.wait("scalar", "vC", sv2)
            sq2 = P.cop("scalar", lambda q, bias=bias: q.activation(
                row3.ap()[0:1, 0:256], row2.ap()[0:1, 0:256], AF.Sqrt, bias=bias))
            P.wait("vector", "sC", sq2)
            rs_v = P.cop("vector", lambda q: q.reciprocal(row3.ap()[0:1, 0:256], row3.ap()[0:1, 0:256]))
            P.wait("tensor", "vC", rs_v)
            P.wait("tensor", "sC", P.val("sC"))
            P.top(lambda q: q.matmul(pb[5].ap()[:, 0:256], ones_rf.ap(), row1.ap()[0:1, 0:256], start=True, stop=True))
            bmm = P.top(lambda q: q.matmul(pb[5].ap()[:, 256:512], ones_rf.ap(), row3.ap()[0:1, 0:256], start=True, stop=True), mark=True)
            P.wait("vector", "tC", bmm)
            for kt in range(NKT):
                P.cop("vector", lambda q, kt=kt: q.tensor_sub(
                    usb.ap()[:, 256 * kt:256 * kt + 256], usb.ap()[:, 256 * kt:256 * kt + 256],
                    pb[5].ap()[:, 0:256]), serial=(kt == 0))
            for kt in range(NKT):
                outc = dst_col0 + kt * TOK + 256 * cb
                if rms_adjust:
                    hv = P.cop("vector", lambda q, kt=kt, outc=outc: q.scalar_tensor_tensor(
                        dst_sb.ap()[:, outc:outc + 256], usb.ap()[:, 256 * kt:256 * kt + 256],
                        rmsw_sb.ap()[:, kt:kt + 1], pb[5].ap()[:, 256:512], OP.mult, OP.mult),
                        serial=(kt == 0))
                else:
                    hv = P.cop("vector", lambda q, kt=kt, outc=outc: q.tensor_tensor(
                        dst_sb.ap()[:, outc:outc + 256], usb.ap()[:, 256 * kt:256 * kt + 256],
                        pb[5].ap()[:, 256:512], OP.mult), serial=(kt == 0))
            return hv

        class WStream:

            def __init__(self, name, width, sem, scale_cols=None, nkt=16, queue="sync"):
                self.stage = SBE(f"{name}_st", [128, width], F32)
                self.out = [SBE(f"{name}_o{i}", [128, width], BF16) for i in range(2)]
                self.sem = sem
                self.queue = queue
                self.n = 0
                self.cast_v = [0, 0]

            def load(self, src_ap, cons_tC=None):
                dst = self.stage.ap()
                if src_ap.ndim == 3:
                    dst = dst.rearrange("p (a b) -> p a b", b=src_ap.shape[2])
                i = self.n % 2
                if self.n >= 1:
                    P.wait(self.queue, "sC", self.cast_v[(self.n - 1) % 2])
                P.dma(self.queue, dst, src_ap, self.sem)
                dv = P.val(self.sem)
                P.wait("scalar", self.sem, dv)
                P.wait("scalar", "tC", P.val("tC") if cons_tC is None else cons_tC)
                cv = P.cop("scalar", lambda q, i=i: q.activation(
                    self.out[i].ap(), self.stage.ap(), AF.Copy))
                self.cast_v[i] = cv
                self.n += 1
                return self.out[i], cv

        fence()
        st_e = contextlib.ExitStack()
        SBE = lambda *a: st_e.enter_context(nc.sbuf_tensor(*a))
        h_ffn = SBE("h_ffn", [128, NKT * TOK], BF16)
        dacc = SBE("dacc", [128, 16 * 1024], F32)
        din = SBE("din", [128, 8 * 1024], BF16)
        lg_sb = SBE("lg_sb", [8, TOK], F32)

        st_p0b = contextlib.ExitStack()
        SBP = lambda *a: st_p0b.enter_context(nc.sbuf_tensor(*a))
        u_e = SBP("u_sbE", [128, NKT * 256], F32)
        ubf_e2 = SBP("ubf_e", [128, NKT * 256], BF16)
        usq_e2 = SBP("usq_e", [128, NKT * 256], BF16)
        lgt_sb = SBP("lgt_sb", [128, 16 * 8], F32)
        wn_sb = SBP("wn_sb", [128, 16 * 8], F32)
        rt1 = SBP("rt1", [128, 1], F32)
        rt2 = SBP("rt2", [128, 1], F32)
        rwb_sb = SBP("rwb_sb", [128, 128], BF16)

        hv = 0
        for cb in range(8):
            hv = ln_block(cb, h_ffn, 0, True, hv, u_e, ubf_e2, usq_e2)
        hffn_v = hv

        P.wait("sync", "vC", hv)
        P.dma("sync", u_e.ap()[:, 0:128].rearrange("p (kt e) -> p kt e", e=8),
              rw_d.rearrange("(kt p) e -> p kt e", p=128), "dW1")
        P.wait("vector", "dW1", P.val("dW1"))
        rww_v = P.cop("vector", lambda q, u=u_e: q.tensor_copy(
            rwb_sb.ap(), u.ap()[:, 0:128]))
        P.wait("tensor", "vC", max(rww_v, hffn_v))
        for tb in range(4):
            P.wait("tensor", "sC", P.val("sC"))
            for kt in range(NKT):
                lmm = P.top(lambda q, tb=tb, kt=kt: q.matmul(
                    pb[4].ap()[0:8, 0:512], rwb_sb.ap()[:, 8 * kt:8 * kt + 8],
                    h_ffn.ap()[:, TOK * kt + 512 * tb: TOK * kt + 512 * tb + 512],
                    start=(kt == 0), stop=(kt == NKT - 1)), mark=(kt == NKT - 1))
            P.wait("scalar", "tC", lmm)
            P.cop("scalar", lambda q, tb=tb: q.copy(lg_sb.ap()[:, 512 * tb:512 * tb + 512], pb[4].ap()[0:8, 0:512]))
        lg_v = P.val("sC")
        P.wait("tensor", "sC", lg_v)
        for tt in range(16):
            P.wait("tensor", "vC", P.val("vC"))
            tmm = P.top(lambda q, tt=tt: q.matmul(
                pb[4].ap()[:, 0:8], lg_sb.ap()[0:8, 128 * tt:128 * tt + 128],
                id_f.ap()[0:8, 0:8], start=True, stop=True, is_transpose=True), mark=True)
            P.wait("vector", "tC", tmm)
            P.cop("vector", lambda q, tt=tt: q.tensor_copy(lgt_sb.ap()[:, 8 * tt:8 * tt + 8], pb[4].ap()[:, 0:8]))
        for tt in range(16):
            L = lambda: lgt_sb.ap()[:, 8 * tt:8 * tt + 8]
            P.cop("vector", lambda q, tt=tt: q.reduce_max(rt1.ap(), lgt_sb.ap()[:, 8 * tt:8 * tt + 8], AX.X))
            sub_v = P.cop("vector", lambda q, tt=tt: q.tensor_scalar(
                lgt_sb.ap()[:, 8 * tt:8 * tt + 8], lgt_sb.ap()[:, 8 * tt:8 * tt + 8], rt1.ap()[:, 0:1], None, OP.subtract))
            P.wait("scalar", "vC", sub_v)
            ex_v = P.cop("scalar", lambda q, tt=tt: q.activation(
                lgt_sb.ap()[:, 8 * tt:8 * tt + 8], lgt_sb.ap()[:, 8 * tt:8 * tt + 8], AF.Exp))
            P.wait("vector", "sC", ex_v)
            P.cop("vector", lambda q, tt=tt: q.reduce_sum(rt1.ap(), lgt_sb.ap()[:, 8 * tt:8 * tt + 8], AX.X))
            P.cop("vector", lambda q: q.reciprocal(rt2.ap(), rt1.ap()))
            P.cop("vector", lambda q, tt=tt: q.tensor_scalar(
                lgt_sb.ap()[:, 8 * tt:8 * tt + 8], lgt_sb.ap()[:, 8 * tt:8 * tt + 8], rt2.ap()[:, 0:1], None, OP.mult))
            P.cop("vector", lambda q, tt=tt: q.reduce_max(rt1.ap(), lgt_sb.ap()[:, 8 * tt:8 * tt + 8], AX.X))
            P.cop("vector", lambda q, tt=tt: q.tensor_scalar(
                wn_sb.ap()[:, 8 * tt:8 * tt + 8], lgt_sb.ap()[:, 8 * tt:8 * tt + 8], rt1.ap()[:, 0:1], None, OP.is_ge))
            P.cop("vector", lambda q, tt=tt: q.scalar_tensor_tensor(
                wn_sb.ap()[:, 8 * tt:8 * tt + 8], wn_sb.ap()[:, 8 * tt:8 * tt + 8], -1e9,
                lgt_sb.ap()[:, 8 * tt:8 * tt + 8], OP.mult, OP.add))
            P.cop("vector", lambda q, tt=tt: q.reduce_max(rt2.ap(), wn_sb.ap()[:, 8 * tt:8 * tt + 8], AX.X))
            P.cop("vector", lambda q, tt=tt: q.tensor_scalar(
                wn_sb.ap()[:, 8 * tt:8 * tt + 8], lgt_sb.ap()[:, 8 * tt:8 * tt + 8], rt2.ap()[:, 0:1], None, OP.is_ge))
            P.cop("vector", lambda q, tt=tt: q.tensor_mul(
                wn_sb.ap()[:, 8 * tt:8 * tt + 8], wn_sb.ap()[:, 8 * tt:8 * tt + 8], lgt_sb.ap()[:, 8 * tt:8 * tt + 8]))
            P.cop("vector", lambda q, tt=tt: q.reduce_sum(rt1.ap(), wn_sb.ap()[:, 8 * tt:8 * tt + 8], AX.X))
            P.cop("vector", lambda q: q.reciprocal(rt2.ap(), rt1.ap()))
            wnv = P.cop("vector", lambda q, tt=tt: q.tensor_scalar(
                wn_sb.ap()[:, 8 * tt:8 * tt + 8], wn_sb.ap()[:, 8 * tt:8 * tt + 8], rt2.ap()[:, 0:1], None, OP.mult))
        P.wait("tensor", "vC", wnv)
        for tt in range(16):
            P.wait("tensor", "sC", P.val("sC"))
            tm2 = P.top(lambda q, tt=tt: q.matmul(
                pb[4].ap()[0:8, 0:128], wn_sb.ap()[:, 8 * tt:8 * tt + 8],
                id_f.ap(), start=True, stop=True, is_transpose=True), mark=True)
            P.wait("scalar", "tC", tm2)
            P.cop("scalar", lambda q, tt=tt: q.copy(lg_sb.ap()[:, 128 * tt:128 * tt + 128], pb[4].ap()[0:8, 0:128]))
        wdT_v = P.val("sC")
        fence()
        st_p0b.close()
        wfl = SBE("wfl", [1, 1024], F32)
        wB_sb = SBE("wB_sb", [128, 1024], BF16)
        sgt = SBE("sgt", [128, 512], BF16)
        t1b = SBE("t1b", [128, 512], BF16)
        fstg = SBE("fstg", [128, 1024], BF16)

        wg3 = wg_d.rearrange("e (kt p) f -> e p kt f", p=128)
        wu3 = wu_d.rearrange("e (kt p) f -> e p kt f", p=128)
        wd3 = wd_d.rearrange("e (kf p) d -> e p kf d", p=128)
        swg3 = swg_d.rearrange("(kt p) f -> p kt f", p=128)
        swu3 = swu_d.rearrange("(kt p) f -> p kt f", p=128)
        swd3 = swd_d.rearrange("(kf p) d -> p kf d", p=128)

        ws_gu = WStream("wsgu", 2048, "dW2", rmsw_sb.ap())
        ws_d = WStream("wsd", 1024, "dW4", None, nkt=8)

        for hf in range(2):
            t0 = 1024 * hf
            z0 = P.cop("vector", lambda q: q.memset(dacc.ap(), 0.0))
            for e in range(9):
                shared = (e == 8)
                if not shared:
                    P.wait("sync", "sC", wdT_v)
                    P.wait("sync", "tC", P.val("tC"))
                    P.dma("sync", wfl.ap(), lg_sb.ap()[e:e + 1, t0:t0 + 1024], "dW1")
                    P.wait("tensor", "dW1", P.val("dW1"))
                    P.wait("scalar", "gC", P.val("gC"))
                    for tb2 in range(2):
                        P.wait("tensor", "sC", P.val("sC"))
                        wbm = P.top(lambda q, tb2=tb2: q.matmul(
                            pb[6].ap()[:, 0:512], ones_rf.ap(),
                            wfl.ap()[0:1, 512 * tb2:512 * tb2 + 512], start=True, stop=True), mark=True)
                        P.wait("scalar", "tC", wbm)
                        wb_v = P.cop("scalar", lambda q, tb2=tb2: q.activation(
                            wB_sb.ap()[:, 512 * tb2:512 * tb2 + 512], pb[6].ap()[:, 0:512], AF.Copy))
                for mt in range(8):
                    if shared:
                        gsrc = swg3[:, :, 128 * mt:128 * mt + 128]
                        usrc = swu3[:, :, 128 * mt:128 * mt + 128]
                    else:
                        gsrc = wg3[e, :, :, 128 * mt:128 * mt + 128]
                        usrc = wu3[e, :, :, 128 * mt:128 * mt + 128]
                    wgb, gcv = ws_gu.load(gsrc)
                    wub, ucv = ws_gu.load(usrc)
                    P.wait("tensor", "sC", gcv)
                    P.wait("tensor", "vC", P.val("vC"))
                    P.wait("tensor", "vC", hffn_v)
                    P.wait("tensor", "gC", P.val("gC"))
                    for kt in range(NKT):
                        P.top(lambda q, kt=kt, wgb=wgb, t0=t0: q.matmul(
                            pb[0].ap(), wgb.ap()[:, 128 * kt:128 * kt + 128],
                            h_ffn.ap()[:, TOK * kt + t0: TOK * kt + t0 + 512],
                            start=(kt == 0), stop=(kt == NKT - 1)))
                        P.top(lambda q, kt=kt, wgb=wgb, t0=t0: q.matmul(
                            pb[1].ap(), wgb.ap()[:, 128 * kt:128 * kt + 128],
                            h_ffn.ap()[:, TOK * kt + t0 + 512: TOK * kt + t0 + 1024],
                            start=(kt == 0), stop=(kt == NKT - 1)))
                    P.wait("tensor", "sC", ucv)
                    for kt in range(NKT):
                        P.top(lambda q, kt=kt, wub=wub, t0=t0: q.matmul(
                            pb[2].ap(), wub.ap()[:, 128 * kt:128 * kt + 128],
                            h_ffn.ap()[:, TOK * kt + t0: TOK * kt + t0 + 512],
                            start=(kt == 0), stop=(kt == NKT - 1)))
                        gumm = P.top(lambda q, kt=kt, wub=wub, t0=t0: q.matmul(
                            pb[3].ap(), wub.ap()[:, 128 * kt:128 * kt + 128],
                            h_ffn.ap()[:, TOK * kt + t0 + 512: TOK * kt + t0 + 1024],
                            start=(kt == 0), stop=(kt == NKT - 1)), mark=(kt == NKT - 1))
                    P.wait("scalar", "tC", gumm)
                    for tb2 in range(2):
                        pg = pb[0] if tb2 == 0 else pb[1]
                        pu = pb[2] if tb2 == 0 else pb[3]
                        P.wait("scalar", "gC", P.val("gC"))
                        sg_v = P.cop("scalar", lambda q, pg=pg: q.activation(sgt.ap(), pg.ap(), AF.Silu))
                        P.wait("vector", "sC", sg_v)
                        tt_v = P.cop("vector", lambda q, pu=pu: q.tensor_tensor(t1b.ap(), pu.ap(), sgt.ap(), OP.mult))
                        P.wait("gpsimd", "vC", tt_v)
                        if mt == 0 and tb2 == 0:
                            P.wait("gpsimd", "tC", P.val("tC"))
                        if shared:
                            dv = P.cop("gpsimd", lambda q, mt=mt, tb2=tb2: q.tensor_copy(
                                din.ap()[:, 1024 * mt + 512 * tb2: 1024 * mt + 512 * tb2 + 512], t1b.ap()))
                        else:
                            P.wait("gpsimd", "sC", wb_v)
                            dv = P.cop("gpsimd", lambda q, mt=mt, tb2=tb2: q.tensor_tensor(
                                din.ap()[:, 1024 * mt + 512 * tb2: 1024 * mt + 512 * tb2 + 512],
                                t1b.ap(), wB_sb.ap()[:, 512 * tb2:512 * tb2 + 512], OP.mult))
                din_v = P.val("gC")
                for md in range(16):
                    if shared:
                        dsrc = swd3[:, :, 128 * md:128 * md + 128]
                    else:
                        dsrc = wd3[e, :, :, 128 * md:128 * md + 128]
                    wdb, dcv = ws_d.load(dsrc)
                    P.wait("tensor", "sC", dcv)
                    P.wait("tensor", "gC", din_v)
                    P.wait("tensor", "vC", P.val("vC"))
                    for kf in range(8):
                        P.top(lambda q, kf=kf, wdb=wdb: q.matmul(
                            pb[4].ap(), wdb.ap()[:, 128 * kf:128 * kf + 128],
                            din.ap()[:, 1024 * kf: 1024 * kf + 512],
                            start=(kf == 0), stop=(kf == 7)))
                        dmm = P.top(lambda q, kf=kf, wdb=wdb: q.matmul(
                            pb[5].ap(), wdb.ap()[:, 128 * kf:128 * kf + 128],
                            din.ap()[:, 1024 * kf + 512: 1024 * kf + 1024],
                            start=(kf == 0), stop=(kf == 7)), mark=(kf == 7))
                    P.wait("vector", "tC", dmm)
                    if e == 0:
                        P.wait("vector", "vC", z0)
                    P.cop("vector", lambda q, md=md: q.tensor_tensor(
                        dacc.ap()[:, 1024 * md: 1024 * md + 512],
                        dacc.ap()[:, 1024 * md: 1024 * md + 512], pb[4].ap(), OP.add))
                    P.cop("vector", lambda q, md=md: q.tensor_tensor(
                        dacc.ap()[:, 1024 * md + 512: 1024 * md + 1024],
                        dacc.ap()[:, 1024 * md + 512: 1024 * md + 1024], pb[5].ap(), OP.add))
            for md in range(16):
                P.wait("vector", "dFf", P.val("dFf"))
                fc_v = P.cop("vector", lambda q, md=md: q.tensor_copy(
                    fstg.ap(), dacc.ap()[:, 1024 * md:1024 * md + 1024]))
                P.wait("sync", "vC", fc_v)
                P.dma("sync", ffn_d.ap()[128 * md:128 * md + 128, t0:t0 + 1024], fstg.ap(), "dFf")
                P.wait("sync", "dFf", P.val("dFf"))
        st_e.close()

        dff_final = P.val("dFf")
        fence()
        st_c = contextlib.ExitStack()
        SBE = lambda *a: st_c.enter_context(nc.sbuf_tensor(*a))
        xnh = SBE("xnh", [128, NKT * TOK], BF16)
        u_c = SBE("u_sbC", [128, NKT * 256], F32)
        ubf_c = SBE("ubf_c", [128, NKT * 256], BF16)
        usq_c = SBE("usq_c", [128, NKT * 256], BF16)
        gt = [SBE(f"gt{i}", [128, 1024], BF16) for i in range(4)]
        ysel2 = [SBE(f"ysels{i}", [128, 512], BF16) for i in range(2)]
        szt2 = [SBE(f"szts{i}", [128, 512], BF16) for i in range(2)]
        gatb2 = [SBE(f"gatbs{i}", [128, 512], BF16) for i in range(2)]
        sqt2 = [SBE(f"sqts{i}", [128, 512], BF16) for i in range(2)]
        sums_sb = SBE("sums_sb", [1, TOK], F32)
        ws_z = WStream("wsz", 2048, "dW2", None)

        hv = 0
        for cb in range(8):
            hv = ln_block(cb, xnh, 0, False, hv, u_c, ubf_c, usq_c)
        xnh_v = hv
        zs_v = P.cop("vector", lambda q: q.memset(sums_sb.ap(), 0.0))

        SETS = [[0, 1, 2, 3], [4, 5, 6, 7]]
        Wz3 = Wz_d.rearrange("(kt p) z -> p kt z", p=128)
        NIT = 128

        gld_v = {}
        bld_v = {}
        gat_v = {}
        sqq_v = {}
        dga_v = {}
        qmark = {}
        sumadd = {}
        silu_vv = {}
        zmark = {}

        def issue_g(k):
            mt_, tb_ = divmod(k, 4)
            r_, chp_ = divmod(mt_, 16)
            row = 2 * D * r_ + 128 * chp_
            if k >= 4:
                P.wait("gpsimd", "vC", bld_v[k - 4])
            P.dma("gpsimd", gt[k % 4].ap()[:, 0:512],
                  G_d.ap()[row: row + 128, 512 * tb_: 512 * tb_ + 512], "dG")
            P.dma("gpsimd", gt[k % 4].ap()[:, 512:1024],
                  G_d.ap()[row + D: row + D + 128, 512 * tb_: 512 * tb_ + 512], "dG")
            gld_v[k] = P.val("dG")

        if stop_after is None:
            P.wait("gpsimd", "gC", ag_v)
        for k in range(3):
            issue_g(k)

        wz_cur = ws_z.load(Wz3[:, :, 0:128], cons_tC=0)
        for mt in range(32):
            wzb, zcv = wz_cur
            if mt < 31:
                wz_cur = ws_z.load(Wz3[:, :, 128 * (mt + 1):128 * (mt + 1) + 128],
                                   cons_tC=zmark.get(4 * (mt - 1) + 3, 0))
            S = SETS[mt % 2]
            P.wait("tensor", "sC", zcv)
            if mt == 0:
                P.wait("tensor", "vC", xnh_v)
            for tb in range(4):
                k = 4 * mt + tb
                if mt >= 2:
                    P.wait("tensor", "vC", sumadd[k - 8])
                    P.wait("tensor", "sC", silu_vv[k - 8])
                for kt in range(NKT):
                    zm = P.top(lambda q, kt=kt, tb=tb, wzb=wzb, S=S: q.matmul(
                        pb[S[tb]].ap(), wzb.ap()[:, 128 * kt:128 * kt + 128],
                        xnh.ap()[:, TOK * kt + 512 * tb: TOK * kt + 512 * tb + 512],
                        start=(kt == 0), stop=(kt == NKT - 1)), mark=(kt == NKT - 1))
                zmark[k] = zm
            for tb in range(4):
                k = 4 * mt + tb
                pbank = pb[S[tb]]
                if k + 3 < NIT:
                    issue_g(k + 3)
                P.wait("scalar", "tC", zmark[k])
                if k >= 2:
                    P.wait("scalar", "vC", gat_v[k - 2])
                sz = P.cop("scalar", lambda q, pbank=pbank, sl=szt2[tb % 2]: q.activation(
                    sl.ap(), pbank.ap(), AF.Silu))
                silu_vv[k] = sz
                P.wait("vector", "dG", gld_v[k])
                P.cop("vector", lambda q, kk=k, sl=ysel2[tb % 2]: q.tensor_scalar(
                    sl.ap(), gt[kk % 4].ap()[:, 0:512], selm_sb.ap()[:, 0:1], None, OP.mult))
                bl = P.cop("vector", lambda q, kk=k, sl=ysel2[tb % 2]: q.scalar_tensor_tensor(
                    sl.ap(), gt[kk % 4].ap()[:, 512:1024], selm_sb.ap()[:, 1:2],
                    sl.ap(), OP.mult, OP.add))
                bld_v[k] = bl
                P.wait("vector", "sC", sz)
                if k >= 2:
                    P.wait("vector", "dGa", dga_v[k - 2])
                    P.wait("vector", "sC", sqq_v[k - 2])
                gv2 = P.cop("vector", lambda q, sl=gatb2[tb % 2], ys=ysel2[tb % 2],
                            sz2=szt2[tb % 2]:
                            q.tensor_tensor(sl.ap(), ys.ap(), sz2.ap(), OP.mult))
                gat_v[k] = gv2
                P.wait("scalar", "vC", gv2)
                P.dma("scalar", gat_d.ap()[128 * mt:128 * mt + 128, 512 * tb:512 * tb + 512],
                      gatb2[tb % 2].ap(), "dGa")
                dga_v[k] = P.val("dGa")
                if k >= 2:
                    P.wait("scalar", "tC", qmark[k - 2])
                sq = P.cop("scalar", lambda q, sl=sqt2[tb % 2], g=gatb2[tb % 2]: q.activation(
                    sl.ap(), g.ap(), AF.Square))
                sqq_v[k] = sq
                P.wait("tensor", "sC", sq)
                qm = P.top(lambda q, pbank=pbank, sl=sqt2[tb % 2]: q.matmul(
                    pbank.ap()[0:1, 0:512], ones_cb.ap(), sl.ap(), start=True, stop=True),
                    mark=True)
                qmark[k] = qm
                P.wait("vector", "tC", qm)
                if mt == 0 and tb == 0:
                    P.wait("vector", "vC", zs_v)
                sa = P.cop("vector", lambda q, tb=tb, pbank=pbank: q.tensor_tensor(
                    sums_sb.ap()[0:1, 512 * tb:512 * tb + 512],
                    sums_sb.ap()[0:1, 512 * tb:512 * tb + 512],
                    pbank.ap()[0:1, 0:512], OP.add))
                sumadd[k] = sa
        sums_v = P.val("vC")
        dga_final = P.val("dGa")
        st_c.close()

        fence()
        st_d = contextlib.ExitStack()
        SBE = lambda *a: st_d.enter_context(nc.sbuf_tensor(*a))
        ynorm = SBE("ynorm", [128, 32 * TOK], BF16)
        rstdB = SBE("rstdB", [128, TOK], F32)
        row4 = SBE("row4", [1, TOK], F32)
        gstg = [SBE(f"gstg{i}", [128, TOK], BF16) for i in range(2)]
        ustg = [SBE(f"ustg{i}", [128, 512], F32) for i in range(2)]
        ffstg = [SBE(f"ffstg{i}", [128, 512], BF16) for i in range(2)]
        otb = [SBE(f"otb{i}", [128, 512], F32) for i in range(2)]
        ws_o = WStream("wso", 4096, "dW2", None, nkt=32)

        P.wait("vector", "vC", sums_v)
        P.cop("vector", lambda q: q.tensor_scalar_mul(row4.ap(), sums_sb.ap(), 1.0 / DIN))
        rw4 = P.val("vC")
        P.wait("scalar", "vC", rw4)
        sq4 = P.cop("scalar", lambda q: q.activation(row4.ap(), row4.ap(), AF.Sqrt, bias=MRMS_EPS))
        P.wait("vector", "sC", sq4)
        rc4 = P.cop("vector", lambda q: q.reciprocal(row4.ap(), row4.ap()))
        P.wait("tensor", "vC", rc4)
        for tb in range(4):
            P.wait("tensor", "sC", P.val("sC"))
            bmm = P.top(lambda q, tb=tb: q.matmul(
                pb[4].ap(), ones_rf.ap(), row4.ap()[0:1, 512 * tb:512 * tb + 512],
                start=True, stop=True), mark=True)
            P.wait("scalar", "tC", bmm)
            P.cop("scalar", lambda q, tb=tb: q.activation(
                rstdB.ap()[:, 512 * tb:512 * tb + 512], pb[4].ap(), AF.Copy))
        rstdB_v = P.val("sC")

        P.wait("gpsimd", "dGa", dga_final)
        gld2 = {}
        yv_k = {}

        def issue_gat(kt):
            if kt >= 2:
                P.wait("gpsimd", "vC", yv_k[kt - 2])
            P.dma("gpsimd", gstg[kt % 2].ap(), gat_d.ap()[128 * kt:128 * kt + 128, :], "dGs")
            gld2[kt] = P.val("dGs")

        issue_gat(0)
        issue_gat(1)
        yv = 0
        for kt in range(32):
            P.wait("vector", "dGs", gld2[kt])
            if kt == 0:
                P.wait("vector", "sC", rstdB_v)
            yv = P.cop("vector", lambda q, kt=kt: q.scalar_tensor_tensor(
                ynorm.ap()[:, TOK * kt:TOK * kt + TOK], gstg[kt % 2].ap(),
                nrmw_sb.ap()[:, kt:kt + 1], rstdB.ap(), OP.mult, OP.mult))
            yv_k[kt] = yv
            if kt + 2 < 32:
                issue_gat(kt + 2)
        yn_v = yv

        P.wait("scalar", "dFf", dff_final)
        uTh3b = uTh.rearrange("(mo p) T -> p mo T", p=128)
        uld = {}
        tv1_v = {}
        ffadd_v = {}
        dout_v = {}
        omark = {}

        def issue_uf(k):
            mo_, tb_ = divmod(k, 4)
            if k >= 2:
                P.wait("scalar", "vC", tv1_v[k - 2])
                P.wait("scalar", "gC", ffadd_v[k - 2])
            P.dma("scalar", ustg[k % 2].ap(), uTh3b[:, mo_, 512 * tb_:512 * tb_ + 512], "dU2")
            P.dma("scalar", ffstg[k % 2].ap(),
                  ffn_d.ap()[128 * mo_:128 * mo_ + 128, 512 * tb_:512 * tb_ + 512], "dU2")
            uld[k] = P.val("dU2")

        issue_uf(0)
        issue_uf(1)

        Wo3 = Wo_d.rearrange("(kt p) d -> p kt d", p=128)
        wo_cur = ws_o.load(Wo3[:, :, 0:128], cons_tC=0)
        for mo in range(16):
            wob, ocv = wo_cur
            if mo < 15:
                wo_cur = ws_o.load(Wo3[:, :, 128 * (mo + 1):128 * (mo + 1) + 128],
                                   cons_tC=omark.get(4 * (mo - 1) + 3, 0))
            S = SETS[mo % 2]
            P.wait("tensor", "sC", ocv)
            if mo == 0:
                P.wait("tensor", "vC", yn_v)
            for tb in range(4):
                k = 4 * mo + tb
                if mo >= 2:
                    P.wait("tensor", "vC", tv1_v[k - 8])
                for kt in range(32):
                    om = P.top(lambda q, kt=kt, tb=tb, wob=wob, S=S: q.matmul(
                        pb[S[tb]].ap(), wob.ap()[:, 128 * kt:128 * kt + 128],
                        ynorm.ap()[:, TOK * kt + 512 * tb: TOK * kt + 512 * tb + 512],
                        start=(kt == 0), stop=(kt == 31)), mark=(kt == 31))
                omark[k] = om
            for tb in range(4):
                k = 4 * mo + tb
                pbank = pb[S[tb]]
                P.wait("vector", "tC", omark[k])
                P.wait("vector", "dU2", uld[k])
                if k >= 2:
                    P.wait("vector", "dOut", dout_v[k - 2])
                tv = P.cop("vector", lambda q, pbank=pbank, sl=otb[k % 2], us=ustg[k % 2]:
                           q.tensor_tensor(sl.ap(), pbank.ap(), us.ap(), OP.add))
                tv1_v[k] = tv
                P.wait("gpsimd", "vC", tv)
                ov = P.cop("gpsimd", lambda q, sl=otb[k % 2], fs=ffstg[k % 2]:
                           q.tensor_tensor(sl.ap(), sl.ap(), fs.ap(), OP.add))
                ffadd_v[k] = ov
                P.wait("gpsimd", "gC", ov)
                P.dma("gpsimd", out_d[128 * mo:128 * mo + 128, 512 * tb:512 * tb + 512],
                      otb[k % 2].ap(), "dOut")
                dout_v[k] = P.val("dOut")
                if k + 2 < 64:
                    issue_uf(k + 2)
        st_d.close()
        P.wait("gpsimd", "dOut", P.val("dOut"))
        P.wait("scalar", "dOut", P.val("dOut"))
        P.wait("scalar", "dStF", P.val("dStF"))
        P.wait("scalar", "dGa", P.val("dGa"))
        P.wait("scalar", "dFf", P.val("dFf"))

    with nc.Block() as block:
        @block.sync
        def _(q):
            for f in P.streams["sync"]:
                f(q)

        @block.scalar
        def _(q):
            for f in P.streams["scalar"]:
                f(q)

        @block.vector
        def _(q):
            for f in P.streams["vector"]:
                f(q)

        @block.tensor
        def _(q):
            for f in P.streams["tensor"]:
                f(q)

        @block.gpsimd
        def _(q):
            for f in P.streams["gpsimd"]:
                f(q)

    return nc, dbg_aps


def kernel(**inputs):
    in_maps = host_layout(inputs)
    nc, _ = build_program()
    res = run_bass_kernel_spmd(nc, in_maps, core_ids=list(range(8)))
    out = np.zeros((NB, LQ, D), dtype=np.float32)
    for c in range(8):
        b, s = c // 2, c % 2
        out[b, TOK * s:TOK * s + TOK, :] = res.results[c]["out"].T
    return out


# version 07
# speedup vs baseline: 1.2972x; 1.1022x over previous
import contextlib

import numpy as np
import ml_dtypes
import concourse.bass as bass
import concourse.mybir as mybir
from concourse.bass_utils import run_bass_kernel_spmd

F32 = mybir.dt.float32
BF16 = mybir.dt.bfloat16
AF = mybir.ActivationFunctionType
OP = mybir.AluOpType
AX = mybir.AxisListType

D = 2048
LQ = 4096
NB = 4
DIN = 4096
NHC = 32
HP = 64
DS = 128
CH = 256
NCHUNK = LQ // CH
CONVC = NHC * HP + 2 * DS
NCT = CONVC // 128
WAC = CONVC + NHC
NKT = D // 128
TOK = 2048
NE = 8
FF = 1024
SF = 1024
LN_EPS = 1e-5
MRMS_EPS = 1e-5
FRMS_EPS = 1e-6

ENGS = ("sync", "scalar", "vector", "tensor", "gpsimd")
CHAIN = {"scalar": "sC", "vector": "vC", "gpsimd": "gC"}


class Prog:
    def __init__(self, nc):
        self.nc = nc
        self.streams = {e: [] for e in ENGS}
        self.sems = {}
        self.cnt = {}

    def sem(self, name):
        if name not in self.sems:
            self.sems[name] = self.nc.alloc_semaphore(name)
            self.cnt[name] = 0
        return self.sems[name]

    def emit(self, eng, fn):
        self.streams[eng].append(fn)

    def wait(self, eng, sname, val=None):
        h = self.sem(sname)
        v = self.cnt[sname] if val is None else val
        if v > 0:
            self.emit(eng, lambda q, h=h, v=v: q.wait_ge(h, v))
        return v

    def cop(self, eng, fn, serial=True):
        ch = CHAIN[eng]
        h = self.sem(ch)
        prev = self.cnt[ch]
        if serial and prev > 0:
            self.emit(eng, lambda q, h=h, v=prev: q.wait_ge(h, v))
        self.cnt[ch] = prev + 1
        self.emit(eng, lambda q, fn=fn, h=h: fn(q).then_inc(h))
        return prev + 1

    def top(self, fn, mark=False):
        if not mark:
            self.emit("tensor", fn)
            return self.cnt.get("tC", 0)
        h = self.sem("tC")
        self.cnt["tC"] += 1
        self.emit("tensor", lambda q, fn=fn, h=h: fn(q).then_inc(h))
        return self.cnt["tC"]

    def dma(self, eng, dst, src, inc, slow=False):
        h = self.sem(inc)
        self.cnt[inc] += 16
        self.emit(eng, lambda q, d=dst, s=src, sl=slow, h=h: q.dma_start(
            out=d, in_=s, allow_slow_non_contiguous=sl).then_inc(h, 16))
        return self.cnt[inc]

    def val(self, sname):
        return self.cnt.get(sname, 0)


def host_layout(inputs):
    u = np.asarray(inputs["L0_attn_u"], dtype=np.float32)
    W_in = np.asarray(inputs["L0_attn_W_in"], dtype=np.float32)
    conv_w = np.asarray(inputs["L0_attn_conv_w"], dtype=np.float32)
    conv_b = np.asarray(inputs["L0_attn_conv_b"], dtype=np.float32)
    dt_bias = np.asarray(inputs["L0_attn_dt_bias"], dtype=np.float32)
    A_log = np.asarray(inputs["L0_attn_A_log"], dtype=np.float32)
    Dp = np.asarray(inputs["L0_attn_Dp"], dtype=np.float32)
    norm_w = np.asarray(inputs["L0_attn_norm_w"], dtype=np.float32)
    W_out = np.asarray(inputs["L0_attn_W_out"], dtype=np.float32)
    rms_w = np.asarray(inputs["L0_ffn_rms_weight"], dtype=np.float32)
    router_w = np.asarray(inputs["L0_ffn_router_w"], dtype=np.float32)
    w_gate = np.asarray(inputs["L0_ffn_w_gate"], dtype=np.float32)
    w_up = np.asarray(inputs["L0_ffn_w_up"], dtype=np.float32)
    w_down = np.asarray(inputs["L0_ffn_w_down"], dtype=np.float32)
    sw_gate = np.asarray(inputs["L0_ffn_sw_gate"], dtype=np.float32)
    sw_up = np.asarray(inputs["L0_ffn_sw_up"], dtype=np.float32)
    sw_down = np.asarray(inputs["L0_ffn_sw_down"], dtype=np.float32)

    in_maps = []
    for c in range(8):
        b, s = c // 2, c % 2
        hh = slice(32 * s, 32 * s + 32)
        xch = slice(DIN + 2048 * s, DIN + 2048 * s + 2048)
        WA = np.concatenate([W_in[xch], W_in[8192:8448],
                             W_in[8448 + 32 * s: 8448 + 32 * s + 32]], axis=0)
        cw = np.concatenate([conv_w[2048 * s:2048 * s + 2048], conv_w[4096:4352]], axis=0)
        cb = np.concatenate([conv_b[2048 * s:2048 * s + 2048], conv_b[4096:4352]], axis=0)
        m = {
            "uTf": np.ascontiguousarray(u[b].T),
            "uTh": np.ascontiguousarray(u[b, 2048 * s:2048 * s + 2048].T),
            "WA": np.ascontiguousarray(WA.T),
            "Wz": np.ascontiguousarray(W_in[0:DIN].T),
            "Wo": np.ascontiguousarray(W_out.T),
            "convw": np.ascontiguousarray(cw.reshape(NCT, 128, 4).transpose(1, 0, 2).reshape(128, NCT * 4)),
            "convb": np.ascontiguousarray(cb.reshape(NCT, 128).T),
            "dtb": np.ascontiguousarray(dt_bias[hh].reshape(32, 1)),
            "alog": np.ascontiguousarray(A_log[hh].reshape(32, 1)),
            "dpv": np.ascontiguousarray(Dp[hh].reshape(32, 1)),
            "nrmw": np.ascontiguousarray(norm_w.reshape(32, 128).T),
            "rmsw": np.ascontiguousarray(rms_w.reshape(16, 128).T),
            "rw": router_w,
            "wg": w_gate, "wu": w_up, "wd": w_down,
            "swg": sw_gate, "swu": sw_up, "swd": sw_down,
            "selm": np.ascontiguousarray(
                np.repeat(np.array([[1.0 - s, float(s)]], dtype=np.float32), 128, axis=0)),
        }
        in_maps.append(m)
    return in_maps


def build_program(debug=(), stop_after=None):
    nc = bass.Bass("TRN2", target_bir_lowering=False)
    P = Prog(nc)
    dbg = set(debug)
    dbg_aps = {}

    def mkdbg(name, shape, dtype=F32):
        if name in dbg:
            ap = nc.dram_tensor("dbg_" + name, list(shape), dtype, kind="ExternalOutput").ap()
            dbg_aps[name] = ap
            return ap
        return None

    uTf = nc.dram_tensor("uTf", [D, LQ], F32, kind="ExternalInput").ap()
    uTh = nc.dram_tensor("uTh", [D, TOK], F32, kind="ExternalInput").ap()
    WA_d = nc.dram_tensor("WA", [D, WAC], F32, kind="ExternalInput").ap()
    Wz_d = nc.dram_tensor("Wz", [D, DIN], F32, kind="ExternalInput").ap()
    Wo_d = nc.dram_tensor("Wo", [DIN, D], F32, kind="ExternalInput").ap()
    convw_d = nc.dram_tensor("convw", [128, NCT * 4], F32, kind="ExternalInput").ap()
    convb_d = nc.dram_tensor("convb", [128, NCT], F32, kind="ExternalInput").ap()
    dtb_d = nc.dram_tensor("dtb", [32, 1], F32, kind="ExternalInput").ap()
    alog_d = nc.dram_tensor("alog", [32, 1], F32, kind="ExternalInput").ap()
    dp_d = nc.dram_tensor("dpv", [32, 1], F32, kind="ExternalInput").ap()
    nrmw_d = nc.dram_tensor("nrmw", [128, 32], F32, kind="ExternalInput").ap()
    rmsw_d = nc.dram_tensor("rmsw", [128, 16], F32, kind="ExternalInput").ap()
    rw_d = nc.dram_tensor("rw", [D, NE], F32, kind="ExternalInput").ap()
    wg_d = nc.dram_tensor("wg", [NE, D, FF], F32, kind="ExternalInput").ap()
    wu_d = nc.dram_tensor("wu", [NE, D, FF], F32, kind="ExternalInput").ap()
    wd_d = nc.dram_tensor("wd", [NE, FF, D], F32, kind="ExternalInput").ap()
    swg_d = nc.dram_tensor("swg", [D, SF], F32, kind="ExternalInput").ap()
    swu_d = nc.dram_tensor("swu", [D, SF], F32, kind="ExternalInput").ap()
    swd_d = nc.dram_tensor("swd", [SF, D], F32, kind="ExternalInput").ap()
    selm_d = nc.dram_tensor("selm", [128, 2], F32, kind="ExternalInput").ap()
    out_d = nc.dram_tensor("out", [D, TOK], F32, kind="ExternalOutput").ap()

    S_d = nc.dram_tensor("S", [2 * D, TOK], BF16, kind="Internal")
    G_d = nc.dram_tensor("G", [2 * 2 * D, TOK], BF16, kind="Internal")
    gat_d = nc.dram_tensor("gat", [DIN, TOK], BF16, kind="Internal")
    ffn_d = nc.dram_tensor("ffn", [D, TOK], BF16, kind="Internal")

    id_bf_t = nc.inline_tensor(np.eye(128, dtype=ml_dtypes.bfloat16), name="t_id_bf")
    id_f_t = nc.inline_tensor(np.eye(128, dtype=np.float32), name="t_id_f")
    ones_cb_t = nc.inline_tensor(np.ones((128, 1), dtype=ml_dtypes.bfloat16), name="t_ones_cb")
    ones_rf_t = nc.inline_tensor(np.ones((1, 128), dtype=np.float32), name="t_ones_rf")
    psel = np.zeros((2, 128), dtype=np.float32)
    psel[0, 0:64] = 1.0
    psel[1, 64:128] = 1.0
    psel_t = nc.inline_tensor(psel, name="t_psel")
    mj = (np.arange(128)[:, None] <= np.arange(256)[None, :]).astype(ml_dtypes.bfloat16)
    mask_t = nc.inline_tensor(np.ascontiguousarray(mj), name="t_mask")
    ind = (np.arange(32)[:, None] == (np.arange(2048)[None, :] // 64)).astype(ml_dtypes.bfloat16)
    ind_t = nc.inline_tensor(np.ascontiguousarray(ind), name="t_ind")
    ones32_t = nc.inline_tensor(np.ones((32, 128), dtype=np.float32), name="t_ones32")
    SCALAR_CONSTS = [LN_EPS, 1.0, MRMS_EPS, FRMS_EPS, 0.5, 2.0]
    sconst_t = nc.inline_tensor(
        np.ascontiguousarray(np.repeat(np.array([SCALAR_CONSTS], dtype=np.float32), 128, axis=0)),
        name="t_sconst")

    A = nc.alloc_sbuf_tensor
    id_bf = A("id_bf", [128, 128], BF16)
    id_f = A("id_f", [128, 128], F32)
    ones_cb = A("ones_cb", [128, 1], BF16)
    ones_rf = A("ones_rf", [1, 128], F32)
    mask_sb = A("mask_sb", [128, 256], BF16)
    convw_sb = A("convw_sb", [128, NCT * 4], F32)
    convb_sb = A("convb_sb", [128, NCT], F32)
    dtb_sb = A("dtb_sb", [32, 1], F32)
    alog_sb = A("alog_sb", [32, 1], F32)
    nA_sb = A("nA_sb", [32, 1], F32)
    ones32_sb = A("ones32_sb", [32, 128], F32)
    dpB_sb = A("dpB_sb", [128, 16], F32)
    selm_sb = A("selm_sb", [128, 2], F32)
    nrmw_sb = A("nrmw_sb", [128, 32], F32)
    rmsw_sb = A("rmsw_sb", [128, 16], F32)
    row1 = A("row1", [1, 512], F32)
    row2 = A("row2", [1, 512], F32)
    row3 = A("row3", [1, 512], F32)
    sconst_sb = A("sconst_sb", [128, len(SCALAR_CONSTS)], F32)
    for i, v in enumerate(SCALAR_CONSTS):
        nc.const_aps.register(F32, v, sconst_sb.ap()[:, i:i + 1])

    pb = [nc.alloc_psum_tensor(f"pb{i}", [128, 512], F32) for i in range(8)]

    for dst, src in [(id_bf, id_bf_t), (id_f, id_f_t), (ones_cb, ones_cb_t),
                     (ones_rf, ones_rf_t), (mask_sb, mask_t),
                     (sconst_sb, sconst_t), (ones32_sb, ones32_t)]:
        P.dma("sync", dst.ap(), src.ap(), "dIn")
    P.dma("sync", convw_sb.ap(), convw_d, "dIn")
    P.dma("sync", convb_sb.ap(), convb_d, "dIn")
    P.dma("sync", dtb_sb.ap(), dtb_d, "dIn")
    P.dma("sync", alog_sb.ap(), alog_d, "dIn")
    P.dma("sync", selm_sb.ap(), selm_d, "dIn")
    P.dma("sync", nrmw_sb.ap(), nrmw_d, "dIn")
    P.dma("sync", rmsw_sb.ap(), rmsw_d, "dIn")
    P.dma("gpsimd", dpB_sb.ap()[0:64, :],
          dp_d[0:32:2, 0:1].rearrange("a b -> b a").partition_broadcast(64).squeeze(1), "dGp", slow=True)
    P.dma("gpsimd", dpB_sb.ap()[64:128, :],
          dp_d[1:32:2, 0:1].rearrange("a b -> b a").partition_broadcast(64).squeeze(1), "dGp", slow=True)
    cload = P.val("dIn")

    P.wait("scalar", "dIn", cload)
    P.cop("scalar", lambda q: q.activation(nA_sb.ap(), alog_sb.ap(), AF.Exp))
    P.wait("vector", "sC")
    P.cop("vector", lambda q: q.tensor_scalar_mul(nA_sb.ap(), nA_sb.ap(), -1.0))

    st_ab = contextlib.ExitStack()
    SB = lambda *a: st_ab.enter_context(nc.sbuf_tensor(*a))
    WA_sb = SB("WA_sb", [128, NKT * WAC], BF16)
    xn_sb = [SB(f"xn_sb{i}", [128, NKT * 256], BF16) for i in range(2)]
    xBC_sb = SB("xBC_sb", [128, NCT * 259], BF16)
    convacc = SB("convacc", [128, 512], F32)
    xT_sb = SB("xT_sb", [128, 2 * 2048], BF16)
    xw_sb = SB("xw_sb", [128, 2 * 2048], BF16)
    usq_sb = xw_sb
    Btok_sb = SB("Btok_sb", [128, 2 * 128], BF16)
    wdtT_sb = SB("wdtT_sb", [128, 2 * 32], F32)
    h_sb = SB("h_sb", [128, NHC * HP], F32)
    hT_bf = SB("hT_bf", [128, NHC * HP], BF16)
    smalls = SB("smalls", [32, 2816], F32)
    flats = SB("flats", [128, 4096], F32)
    u_sb = SB("u_sb", [128, NKT * 256], F32)
    ind_sb = SB("ind_sb", [32, 2048], BF16)
    cdr_sb = SB("cdr_sb", [32, 512], F32)

    sm = {}
    off = 0
    for nm, w in [("dtraw", 256), ("dt", 256), ("cs", 256), ("negcs", 256),
                  ("expf", 256), ("decay", 256), ("wdt", 256), ("lndt", 256),
                  ("tmp1", 256), ("tmp2", 256), ("cd", 1), ("cse", 1)]:
        sm[nm] = off
        off += w

    def SM(name, w=256):
        return smalls.ap()[:, sm[name]:sm[name] + w]

    tap_xn = mkdbg("xn0", [128, NKT * 256], BF16)
    tap_xbc = mkdbg("xbc0", [128, NCT * 256], BF16)
    tap_dt = mkdbg("dt0", [32, 256])
    tap_cs = mkdbg("cs0", [32, 256])
    tap_st = mkdbg("st0", [128, 384], BF16)
    tap_h = mkdbg("h0", [128, NHC * HP])
    tap_yp = mkdbg("yp0", [128, 16 * 256], BF16)

    P.wait("vector", "dGp", P.val("dGp"))
    P.wait("vector", "dIn", cload)
    P.wait("gpsimd", "dIn", cload)
    P.cop("vector", lambda q: q.memset(h_sb.ap(), 0.0))
    for t in range(NCT):
        P.cop("vector", lambda q, t=t: q.memset(xBC_sb.ap()[:, 259 * t:259 * t + 3], 0.0), serial=False)
    P.cop("vector", lambda q: q.memset(flats.ap()[0:1, 0:2048], 1.0), serial=False)
    ab_init_v = P.cop("vector", lambda q: q.memset(flats.ap()[0:3, 2048:4096], 1.0), serial=False)
    P.wait("sync", "dIn", cload)
    P.dma("sync", flats.ap()[32:34, 0:128], psel_t.ap(), "dIn")
    P.dma("sync", ind_sb.ap(), ind_t.ap(), "dIn")
    P.wait("tensor", "dIn", P.val("dIn"))

    with nc.sbuf_tensor("wstage0", [128, WAC], F32) as ws0:
        for kt in range(NKT):
            if kt >= 1:
                P.wait("sync", "vC", P.val("vC"))
            P.dma("sync", ws0.ap(), WA_d[128 * kt:128 * kt + 128, :], "dWA")
            P.wait("vector", "dWA", P.val("dWA"))
            P.cop("vector", lambda q, kt=kt: q.tensor_copy(
                WA_sb.ap()[:, WAC * kt: WAC * kt + WAC], ws0.ap()), serial=False)
    wa_ready_v = P.val("vC")

    e_sb = SB("e_sb", [128, 256], F32)
    ycomb = SB("ycomb", [128, 256], F32)
    ypre_sb = [SB(f"ypre{i}", [128, 16 * 256], BF16) for i in range(2)]
    ST_sb = [SB(f"ST{i}", [128, 384], BF16) for i in range(2)]
    CBTm_sb = SB("CBTm", [128, 384], BF16)

    uTf3 = uTf.rearrange("(kt p) T -> p kt T", p=128)
    last_xn_v = 0
    prev_te_done = 0
    prev_ych_v = 0

    if stop_after == "s_init":
        chunk_range = []
    else:
        chunk_range = list(range(NCHUNK))
    for c in chunk_range:
        xb = c % 2
        half = c // 8
        yb = c % 2

        P.wait("sync", "vC", last_xn_v if c > 0 else wa_ready_v)
        P.dma("sync", u_sb.ap().rearrange("p (kt T) -> p kt T", T=256),
              uTf3[:, :, CH * c: CH * c + CH], "dU")
        P.wait("vector", "dU", P.val("dU"))
        P.wait("vector", "tC", prev_te_done)
        ubf_v = P.cop("vector", lambda q, xb=xb, u=u_sb: q.tensor_copy(xn_sb[xb].ap(), u.ap()))
        P.wait("scalar", "vC", ubf_v)
        P.wait("scalar", "tC", prev_te_done)
        usq_v = P.cop("scalar", lambda q, xb=xb: q.activation(usq_sb.ap(), xn_sb[xb].ap(), AF.Square))
        if c == 0 and stop_after == "s_p0a":
            break
        P.wait("tensor", "vC", ubf_v)
        P.wait("tensor", "sC", usq_v)
        P.wait("tensor", "vC", P.val("vC"))
        for kt in range(NKT):
            P.top(lambda q, kt=kt, xb=xb: q.matmul(
                pb[6].ap()[0:1, 0:256], ones_cb.ap(),
                xn_sb[xb].ap()[:, 256 * kt:256 * kt + 256],
                start=(kt == 0), stop=(kt == NKT - 1)))
        for kt in range(NKT):
            sums_mm = P.top(lambda q, kt=kt: q.matmul(
                pb[6].ap()[0:1, 256:512], ones_cb.ap(),
                usq_sb.ap()[:, 256 * kt:256 * kt + 256],
                start=(kt == 0), stop=(kt == NKT - 1)), mark=(kt == NKT - 1))
        if c == 0 and stop_after == "s_p0b":
            break
        P.wait("vector", "tC", sums_mm)
        P.cop("vector", lambda q: q.tensor_scalar_mul(row1.ap()[0:1, 0:256], pb[6].ap()[0:1, 0:256], 1.0 / D))
        P.cop("vector", lambda q: q.tensor_scalar_mul(row2.ap()[0:1, 0:256], pb[6].ap()[0:1, 256:512], 1.0 / D))
        P.cop("vector", lambda q: q.tensor_mul(row3.ap()[0:1, 0:256], row1.ap()[0:1, 0:256], row1.ap()[0:1, 0:256]))
        stats_v = P.cop("vector", lambda q: q.tensor_sub(row2.ap()[0:1, 0:256], row2.ap()[0:1, 0:256], row3.ap()[0:1, 0:256]))
        P.wait("scalar", "vC", stats_v)
        sq_v = P.cop("scalar", lambda q: q.activation(row3.ap()[0:1, 0:256], row2.ap()[0:1, 0:256], AF.Sqrt, bias=LN_EPS))
        P.wait("vector", "sC", sq_v)
        rstd_v = P.cop("vector", lambda q: q.reciprocal(row3.ap()[0:1, 0:256], row3.ap()[0:1, 0:256]))
        if c == 0 and stop_after == "s_p0c":
            break
        P.wait("tensor", "vC", rstd_v)
        P.wait("tensor", "sC", P.val("sC"))
        P.top(lambda q: q.matmul(pb[5].ap()[:, 0:256], ones_rf.ap(), row1.ap()[0:1, 0:256], start=True, stop=True))
        bc_mm = P.top(lambda q: q.matmul(pb[5].ap()[:, 256:512], ones_rf.ap(), row3.ap()[0:1, 0:256], start=True, stop=True), mark=True)
        P.wait("vector", "tC", bc_mm)
        if c == 0 and stop_after == "s_p0d":
            break
        for kt in range(NKT):
            P.cop("vector", lambda q, kt=kt, u=u_sb: q.tensor_sub(
                u.ap()[:, 256 * kt:256 * kt + 256], u.ap()[:, 256 * kt:256 * kt + 256],
                pb[5].ap()[:, 0:256]), serial=(kt == 0))
        if c == 0 and stop_after == "s_p0e":
            break
        for kt in range(NKT):
            xn_v = P.cop("vector", lambda q, kt=kt, xb=xb, u=u_sb: q.tensor_tensor(
                xn_sb[xb].ap()[:, 256 * kt:256 * kt + 256], u.ap()[:, 256 * kt:256 * kt + 256],
                pb[5].ap()[:, 256:512], OP.mult), serial=(kt == 0))
        last_xn_v = xn_v
        if c == 0 and tap_xn is not None:
            P.wait("sync", "vC", xn_v)
            P.dma("sync", tap_xn, xn_sb[0].ap(), "dTap")
            P.wait("sync", "dTap", P.val("dTap"))
            P.wait("vector", "dTap", P.val("dTap"))
            P.wait("tensor", "dTap", P.val("dTap"))


        P.wait("tensor", "vC", xn_v)
        if c == 0:
            P.wait("tensor", "vC", ab_init_v)
        evac_vals = []
        for mt in range(NCT + 1):
            bank = pb[mt % 2]
            mw = 128 if mt < NCT else 32
            if len(evac_vals) >= 2:
                P.wait("tensor", "sC", evac_vals[-2])
            for kt in range(NKT):
                mm_v = P.top(lambda q, mt=mt, kt=kt, bank=bank, mw=mw, xb=xb: q.matmul(
                    bank.ap()[0:mw, 0:256],
                    WA_sb.ap()[:, WAC * kt + 128 * mt: WAC * kt + 128 * mt + mw],
                    xn_sb[xb].ap()[:, 256 * kt:256 * kt + 256],
                    start=(kt == 0), stop=(kt == NKT - 1)), mark=(kt == NKT - 1))
            P.wait("scalar", "tC", mm_v)
            if mt == 0:
                P.wait("scalar", "tC", prev_te_done)
                P.wait("scalar", "vC", prev_ych_v)
            if mt < NCT:
                ev = P.cop("scalar", lambda q, mt=mt, bank=bank: q.copy(
                    xBC_sb.ap()[:, 259 * mt + 3:259 * mt + 259], bank.ap()[:, 0:256]))
            else:
                ev = P.cop("scalar", lambda q, bank=bank: q.copy(
                    SM("dtraw"), bank.ap()[0:32, 0:256]))
            evac_vals.append(ev)
        inproj_evac = evac_vals[-1]

        if c == 0 and stop_after == "s_inproj":
            break
        P.wait("vector", "sC", inproj_evac)
        silu_vals = []
        for t in range(NCT):
            ca = convacc.ap()[:, 256 * (t % 2):256 * (t % 2) + 256]
            if t >= 2:
                P.wait("vector", "sC", silu_vals[t - 2])
            P.cop("vector", lambda q, t=t, ca=ca: q.tensor_scalar(
                ca, xBC_sb.ap()[:, 259 * t + 0:259 * t + 256],
                convw_sb.ap()[:, 4 * t + 0:4 * t + 1], None, OP.mult))
            for k in (1, 2, 3):
                P.cop("vector", lambda q, t=t, k=k, ca=ca: q.scalar_tensor_tensor(
                    ca, xBC_sb.ap()[:, 259 * t + k:259 * t + k + 256],
                    convw_sb.ap()[:, 4 * t + k:4 * t + k + 1], ca, OP.mult, OP.add))
            P.cop("vector", lambda q, t=t: q.tensor_copy(
                xBC_sb.ap()[:, 259 * t + 0:259 * t + 3], xBC_sb.ap()[:, 259 * t + 256:259 * t + 259]))
            vv = P.val("vC")
            P.wait("scalar", "vC", vv)
            sv = P.cop("scalar", lambda q, t=t, ca=ca: q.activation(
                xBC_sb.ap()[:, 259 * t + 3:259 * t + 259], ca,
                AF.Silu, bias=convb_sb.ap()[:, t:t + 1]))
            silu_vals.append(sv)
        conv_done = P.val("sC")
        if c == 0 and tap_xbc is not None:
            P.wait("sync", "sC", conv_done)
            P.dma("sync", tap_xbc.rearrange("p (t T) -> p t T", T=256),
                  xBC_sb.ap().rearrange("p (t T) -> p t T", T=259)[:, :, 3:259], "dTap")
            P.wait("sync", "dTap", P.val("dTap"))
            P.wait("scalar", "dTap", P.val("dTap"))
            P.wait("vector", "dTap", P.val("dTap"))

        if c == 0 and stop_after == "s_conv":
            break
        P.wait("vector", "sC", inproj_evac)
        P.cop("vector", lambda q: q.tensor_scalar(SM("tmp1"), SM("dtraw"), dtb_sb.ap()[0:32, 0:1], None, OP.add))
        P.cop("vector", lambda q: q.tensor_scalar(SM("tmp2"), SM("tmp1"), 0.0, None, OP.max))
        relu_v = P.val("vC")
        P.wait("scalar", "vC", relu_v)
        P.cop("scalar", lambda q: q.activation(SM("dt"), SM("tmp1"), AF.Abs))
        P.cop("scalar", lambda q: q.activation(SM("dt"), SM("dt"), AF.Exp, scale=-1.0))
        ln_v = P.cop("scalar", lambda q: q.activation(SM("dt"), SM("dt"), AF.Ln, bias=1.0))
        P.wait("vector", "sC", ln_v)
        dt_v = P.cop("vector", lambda q: q.tensor_add(SM("dt"), SM("dt"), SM("tmp2")))
        P.wait("scalar", "vC", dt_v)
        P.cop("scalar", lambda q: q.activation(SM("lndt"), SM("dt"), AF.Ln))
        P.cop("vector", lambda q: q.tensor_scalar(SM("tmp1"), SM("dt"), nA_sb.ap()[0:32, 0:1], None, OP.mult))
        P.cop("vector", lambda q: q.tensor_tensor_scan(SM("cs"), SM("tmp1"), SM("tmp1"), 0.0, OP.add, OP.bypass))
        P.cop("vector", lambda q: q.tensor_scalar_mul(SM("negcs"), SM("cs"), -1.0))
        cse_v = P.cop("vector", lambda q: q.tensor_copy(SM("cse", 1), smalls.ap()[:, sm["cs"] + 255:sm["cs"] + 256]))
        P.wait("scalar", "vC", cse_v)
        P.cop("scalar", lambda q: q.activation(SM("expf"), SM("cs"), AF.Exp))
        P.cop("scalar", lambda q: q.activation(SM("decay"), SM("cs"), AF.Exp, scale=-1.0, bias=SM("cse", 1)))
        dexp_v = P.cop("scalar", lambda q: q.activation(SM("cd", 1), SM("cse", 1), AF.Exp))
        P.wait("vector", "sC", dexp_v)
        dtpath_v = P.cop("vector", lambda q: q.tensor_mul(SM("wdt"), SM("decay"), SM("dt")))
        if c == 0 and tap_dt is not None:
            P.wait("sync", "vC", dtpath_v)
            P.dma("sync", tap_dt, SM("dt"), "dTap")
            P.wait("sync", "dTap", P.val("dTap"))
            P.dma("sync", tap_cs, SM("cs"), "dTap")
            P.wait("sync", "dTap", P.val("dTap"))
            P.wait("scalar", "dTap", P.val("dTap"))
            P.wait("vector", "dTap", P.val("dTap"))

        if c == 0 and stop_after == "s_dt":
            break
        P.wait("tensor", "vC", dtpath_v)
        for jt in range(2):
            wdtT_mm = P.top(lambda q, jt=jt: q.matmul(
                pb[2].ap()[0:128, 32 * jt:32 * jt + 32], SM("wdt")[0:32, 128 * jt:128 * jt + 128],
                id_f.ap()[0:32, 0:32], start=True, stop=True, is_transpose=True), mark=(jt == 1))
        P.wait("vector", "tC", wdtT_mm)
        wdtT_v = P.cop("vector", lambda q: q.tensor_copy(wdtT_sb.ap()[:, 0:64], pb[2].ap()[:, 0:64]))
        if c == 0 and stop_after == "s_tr1":
            break
        P.wait("tensor", "sC", conv_done)
        tp_banks = [pb[7].ap().bitcast(BF16)[:, 0:128], pb[3].ap().bitcast(BF16)[:, 0:128]]
        tevac = []
        for t in range(17):
            for jt in range(2):
                tpos = 2 * t + jt
                bk = tp_banks[tpos % 2]
                if tpos >= 2:
                    P.wait("tensor", "vC", tevac[tpos - 2])
                tr_mm = P.top(lambda q, t=t, jt=jt, bk=bk: q.matmul(
                    bk, xBC_sb.ap()[:, 259 * t + 3 + 128 * jt: 259 * t + 3 + 128 * jt + 128],
                    id_bf.ap(), start=True, stop=True, is_transpose=True), mark=True)
                P.wait("vector", "tC", tr_mm)
                if t < 16:
                    ev = P.cop("vector", lambda q, t=t, jt=jt, bk=bk: q.tensor_copy(
                        xT_sb.ap()[:, 2048 * jt + 128 * t: 2048 * jt + 128 * t + 128],
                        bk), serial=False)
                else:
                    ev = P.cop("vector", lambda q, jt=jt, bk=bk: q.tensor_copy(
                        Btok_sb.ap()[:, 128 * jt:128 * jt + 128],
                        bk), serial=False)
                tevac.append(ev)
                if c == 0 and stop_after == "s_tr2" and t == 0:
                    break
            if c == 0 and stop_after == "s_tr2" and t == 0:
                break
        if c == 0 and stop_after == "s_tr2":
            break
        P.wait("vector", "vC", wdtT_v)
        for jt in range(2):
            for h in range(NHC):
                xw_v = P.cop("vector", lambda q, jt=jt, h=h: q.tensor_scalar(
                    xw_sb.ap()[:, 2048 * jt + 64 * h: 2048 * jt + 64 * h + 64],
                    xT_sb.ap()[:, 2048 * jt + 64 * h: 2048 * jt + 64 * h + 64],
                    wdtT_sb.ap()[:, 32 * jt + h:32 * jt + h + 1], None, OP.mult),
                    serial=(jt == 0 and h == 0))

        if c == 0 and stop_after == "s_transp":
            break
        Bcm = xBC_sb.ap()[:, 259 * 16 + 3: 259 * 16 + 259]
        Ccm = xBC_sb.ap()[:, 259 * 17 + 3: 259 * 17 + 259]
        P.wait("tensor", "vC", P.val("vC"))
        P.top(lambda q: q.matmul(pb[2].ap()[:, 0:256], Bcm[:, 0:128], Ccm, start=True, stop=True))
        cbt_mm = P.top(lambda q: q.matmul(pb[3].ap()[:, 0:128], Bcm[:, 128:256], Ccm[:, 128:256], start=True, stop=True), mark=True)
        P.wait("vector", "tC", cbt_mm)
        P.cop("vector", lambda q: q.tensor_tensor(CBTm_sb.ap()[:, 0:256], pb[2].ap()[:, 0:256], mask_sb.ap()[:, 0:256], OP.mult))
        cbt_v = P.cop("vector", lambda q: q.tensor_tensor(CBTm_sb.ap()[:, 256:384], pb[3].ap()[:, 0:128], mask_sb.ap()[:, 0:128], OP.mult))

        hTb_v = P.cop("vector", lambda q: q.tensor_copy(hT_bf.ap(), h_sb.ap()))

        if c == 0 and stop_after == "s_cbt":
            break
        for qtr in range(4):
            P.wait("sync", "vC", dtpath_v)
            P.wait("sync", "tC", P.val("tC"))
            P.dma("sync", flats.ap()[1:2, 0:2048], smalls.ap()[8 * qtr:8 * qtr + 8, sm["negcs"]:sm["negcs"] + 256], "dFl")
            P.dma("sync", flats.ap()[2:3, 0:2048], smalls.ap()[8 * qtr:8 * qtr + 8, sm["lndt"]:sm["lndt"] + 256], "dFl")
            P.dma("sync", flats.ap()[0:1, 2048:4096], smalls.ap()[8 * qtr:8 * qtr + 8, sm["cs"]:sm["cs"] + 256], "dFl")
            P.dma("sync", flats.ap()[32:33, 2048:4096], smalls.ap()[8 * qtr:8 * qtr + 8, sm["expf"]:sm["expf"] + 256], "dFl")
            if qtr < 3:
                P.dma("sync", flats.ap()[33:34, 2048:4096],
                      smalls.ap()[8 * qtr + 1:8 * qtr + 9, sm["expf"]:sm["expf"] + 256], "dFl")
            else:
                P.dma("sync", flats.ap()[33:34, 2048:3840],
                      smalls.ap()[8 * qtr + 1:8 * qtr + 8, sm["expf"]:sm["expf"] + 256], "dFl")
            fl_v = P.val("dFl")
            P.wait("tensor", "dFl", fl_v)

            P.wait("vector", "sC", dexp_v)
            cdr_v = P.cop("vector", lambda q, qtr=qtr: q.tensor_scalar(
                cdr_sb.ap(), ind_sb.ap()[:, 512 * qtr:512 * qtr + 512], SM("cd", 1), None, OP.mult))
            P.wait("tensor", "vC", cdr_v)
            cdb_mm = P.top(lambda q: q.matmul(
                pb[6].ap()[:, 0:512], ones32_sb.ap(), cdr_sb.ap(), start=True, stop=True), mark=True)
            P.wait("vector", "tC", cdb_mm)
            P.wait("vector", "vC", hTb_v)
            hsc_v = P.cop("vector", lambda q, qtr=qtr: q.tensor_tensor(
                h_sb.ap()[:, 512 * qtr:512 * qtr + 512], h_sb.ap()[:, 512 * qtr:512 * qtr + 512],
                pb[6].ap()[:, 0:512], OP.mult))
            P.wait("tensor", "vC", hsc_v)
            for jt in range(2):
                st_mm = P.top(lambda q, jt=jt, qtr=qtr: q.matmul(
                    pb[6].ap()[:, 0:512], Btok_sb.ap()[:, 128 * jt:128 * jt + 128],
                    xw_sb.ap()[:, 2048 * jt + 512 * qtr: 2048 * jt + 512 * qtr + 512],
                    start=(jt == 0), stop=(jt == 1)), mark=(jt == 1))
            P.wait("vector", "tC", st_mm)
            P.cop("vector", lambda q, qtr=qtr: q.tensor_tensor(
                h_sb.ap()[:, 512 * qtr:512 * qtr + 512], h_sb.ap()[:, 512 * qtr:512 * qtr + 512],
                pb[6].ap()[:, 0:512], OP.add))

            for pp in range(4):
                g = 4 * qtr + pp
                j0 = 2 * pp
                P.wait("tensor", "vC", P.val("vC"))
                for hi in range(2):
                    col = 256 * (j0 + hi)
                    bankS = pb[2] if hi == 0 else pb[3]
                    P.top(lambda q, col=col, bankS=bankS: q.matmul(
                        bankS.ap()[:, 0:256], flats.ap()[0:3, col:col + 128],
                        flats.ap()[0:3, 2048 + col:2048 + col + 256], start=True, stop=True))
                    seg_mm = P.top(lambda q, col=col, bankS=bankS: q.matmul(
                        bankS.ap()[:, 256:384], flats.ap()[0:3, col + 128:col + 256],
                        flats.ap()[0:3, 2048 + col + 128:2048 + col + 256], start=True, stop=True), mark=True)
                P.wait("scalar", "tC", seg_mm)
                for hi in range(2):
                    bankS = pb[2] if hi == 0 else pb[3]
                    P.cop("scalar", lambda q, hi=hi, bankS=bankS: q.activation(
                        ST_sb[hi].ap()[:, 0:256], bankS.ap()[:, 0:256], AF.Exp))
                    seg_e_v = P.cop("scalar", lambda q, hi=hi, bankS=bankS: q.activation(
                        ST_sb[hi].ap()[:, 256:384], bankS.ap()[:, 256:384], AF.Exp))
                P.wait("vector", "sC", seg_e_v)
                P.wait("vector", "vC", cbt_v)
                for hi in range(2):
                    st_v = P.cop("vector", lambda q, hi=hi: q.scalar_tensor_tensor(
                        ST_sb[hi].ap()[:, 0:384], ST_sb[hi].ap()[:, 0:384], 1e30,
                        CBTm_sb.ap()[:, 0:384], OP.min, OP.mult), serial=(hi == 0))
                if c == 0 and g == 0 and tap_st is not None:
                    P.wait("sync", "vC", st_v)
                    P.dma("sync", tap_st, ST_sb[0].ap(), "dTap")
                    P.wait("sync", "dTap", P.val("dTap"))
                    P.wait("scalar", "dTap", P.val("dTap"))
                    P.wait("vector", "dTap", P.val("dTap"))

                P.wait("tensor", "vC", hTb_v)
                P.wait("tensor", "sC", P.val("sC"))
                P.wait("tensor", "vC", P.val("vC"))
                P.top(lambda q, g=g: q.matmul(
                    pb[4].ap()[:, 256:512], hT_bf.ap()[:, 128 * g:128 * g + 128], Ccm,
                    start=True, stop=True))
                P.top(lambda q, j0=j0: q.matmul(
                    pb[5].ap()[:, 256:512], flats.ap()[32:34, 0:128],
                    flats.ap()[32:34, 2048 + 256 * j0:2048 + 256 * j0 + 256], start=True, stop=True))
                P.wait("tensor", "vC", st_v)
                for hi in range(2):
                    P.top(lambda q, hi=hi, g=g: q.matmul(
                        pb[4].ap()[64 * hi:64 * hi + 64, 0:256],
                        xT_sb.ap()[:, 64 * (2 * g + hi): 64 * (2 * g + hi) + 64],
                        ST_sb[hi].ap()[:, 0:256],
                        start=True, stop=False, tile_position=(0, 64 * hi),
                        skip_group_check=True))
                for hi in range(2):
                    y_mm = P.top(lambda q, hi=hi, g=g: q.matmul(
                        pb[4].ap()[64 * hi:64 * hi + 64, 128:256],
                        xT_sb.ap()[:, 2048 + 64 * (2 * g + hi): 2048 + 64 * (2 * g + hi) + 64],
                        ST_sb[hi].ap()[:, 256:384],
                        start=False, stop=True, tile_position=(0, 64 * hi),
                        skip_group_check=True), mark=(hi == 1))
                P.wait("scalar", "tC", y_mm)
                P.wait("scalar", "vC", P.val("vC"))
                eb_v = P.cop("scalar", lambda q: q.activation(e_sb.ap(), pb[5].ap()[:, 256:512], AF.Copy))
                P.wait("vector", "sC", eb_v)
                if g == 0:
                    P.wait("vector", "dStF", P.val("dStF"))
                P.cop("vector", lambda q: q.tensor_tensor(ycomb.ap(), pb[4].ap()[:, 256:512], e_sb.ap(), OP.mult))
                P.cop("vector", lambda q: q.tensor_tensor(ycomb.ap(), ycomb.ap(), pb[4].ap()[:, 0:256], OP.add))
                ych_v = P.cop("vector", lambda q, g=g, yb=yb: q.scalar_tensor_tensor(
                    ypre_sb[yb].ap()[:, 256 * g:256 * g + 256],
                    xBC_sb.ap()[:, 259 * g + 3:259 * g + 259],
                    dpB_sb.ap()[:, g:g + 1], ycomb.ap(), OP.mult, OP.add))
        if c == 0 and tap_yp is not None:
            P.wait("sync", "vC", ych_v)
            P.dma("sync", tap_yp, ypre_sb[0].ap(), "dTap")
            P.wait("sync", "dTap", P.val("dTap"))
            P.wait("scalar", "dTap", P.val("dTap"))
            P.wait("vector", "dTap", P.val("dTap"))
        if c == 0 and tap_h is not None:
            P.wait("sync", "vC", P.val("vC"))
            P.dma("sync", tap_h, h_sb.ap(), "dTap")
            P.wait("sync", "dTap", P.val("dTap"))

        P.wait("scalar", "vC", ych_v)
        P.dma("scalar", S_d.ap()[D * half: D * half + D, 256 * (c % 8):256 * (c % 8) + 256]
              .rearrange("(g p) T -> p g T", p=128),
              ypre_sb[yb].ap().rearrange("p (g T) -> p g T", T=256), "dStF")

        prev_te_done = P.val("tC")
        prev_ych_v = ych_v
        if stop_after == f"chunk{c}":
            break

    st_ab.close()
    ab_stash_dma = P.val("dStF")

    ag_v = 0
    if stop_after is None:
        P.wait("gpsimd", "dStF", ab_stash_dma)
        ag_v = P.cop("gpsimd", lambda q: q.collective_compute(
            "AllGather", OP.bypass, replica_groups=[[0, 1], [2, 3], [4, 5], [6, 7]],
            ins=[S_d.ap()], outs=[G_d.ap()]))

    if stop_after is not None:
        zo = nc.alloc_sbuf_tensor("zo", [128, 512], F32)
        zv = P.cop("vector", lambda q: q.memset(zo.ap(), 0.0))
        P.wait("scalar", "vC", zv)
        for mt in range(16):
            for tb in range(4):
                P.dma("scalar", out_d[128 * mt:128 * mt + 128, 512 * tb:512 * tb + 512], zo.ap(), "dOut")
        P.wait("scalar", "dOut", P.val("dOut"))
        P.wait("scalar", "dTap", P.val("dTap"))
        P.wait("scalar", "dStF", P.val("dStF"))
    else:
        def fence():
            tv, vv, sv = P.val("tC"), P.val("vC"), P.val("sC")
            for e in ENGS:
                P.wait(e, "tC", tv)
                P.wait(e, "vC", vv)
                P.wait(e, "sC", sv)

        uTh3 = uTh.rearrange("(kt p) T -> p kt T", p=128)

        def ln_block(cb, dst_sb, dst_col0, rms_adjust, last_u_free, usb, ubf, usq):
            P.wait("sync", "vC", last_u_free)
            P.dma("sync", usb.ap().rearrange("p (kt T) -> p kt T", T=256),
                  uTh3[:, :, 256 * cb: 256 * cb + 256], "dU")
            P.wait("vector", "dU", P.val("dU"))
            ub_v = P.cop("vector", lambda q: q.tensor_copy(ubf.ap(), usb.ap()))
            P.wait("scalar", "vC", ub_v)
            us_v = P.cop("scalar", lambda q: q.activation(usq.ap(), ubf.ap(), AF.Square))
            P.wait("tensor", "vC", ub_v)
            P.wait("tensor", "sC", us_v)
            P.wait("tensor", "vC", P.val("vC"))
            for kt in range(NKT):
                P.top(lambda q, kt=kt: q.matmul(
                    pb[6].ap()[0:1, 0:256], ones_cb.ap(),
                    ubf.ap()[:, 256 * kt:256 * kt + 256],
                    start=(kt == 0), stop=(kt == NKT - 1)))
            for kt in range(NKT):
                smm = P.top(lambda q, kt=kt: q.matmul(
                    pb[6].ap()[0:1, 256:512], ones_cb.ap(),
                    usq.ap()[:, 256 * kt:256 * kt + 256],
                    start=(kt == 0), stop=(kt == NKT - 1)), mark=(kt == NKT - 1))
            P.wait("vector", "tC", smm)
            P.cop("vector", lambda q: q.tensor_scalar_mul(row1.ap()[0:1, 0:256], pb[6].ap()[0:1, 0:256], 1.0 / D))
            P.cop("vector", lambda q: q.tensor_scalar_mul(row2.ap()[0:1, 0:256], pb[6].ap()[0:1, 256:512], 1.0 / D))
            P.cop("vector", lambda q: q.tensor_mul(row3.ap()[0:1, 0:256], row1.ap()[0:1, 0:256], row1.ap()[0:1, 0:256]))
            sv2 = P.cop("vector", lambda q: q.tensor_sub(row2.ap()[0:1, 0:256], row2.ap()[0:1, 0:256], row3.ap()[0:1, 0:256]))
            if rms_adjust:
                sv2 = P.cop("vector", lambda q: q.tensor_scalar(
                    row2.ap()[0:1, 0:256], row2.ap()[0:1, 0:256],
                    1.0 + FRMS_EPS, FRMS_EPS * LN_EPS, OP.mult, OP.add))
                bias = 0.0
            else:
                bias = LN_EPS
            P.wait("scalar", "vC", sv2)
            sq2 = P.cop("scalar", lambda q, bias=bias: q.activation(
                row3.ap()[0:1, 0:256], row2.ap()[0:1, 0:256], AF.Sqrt, bias=bias))
            P.wait("vector", "sC", sq2)
            rs_v = P.cop("vector", lambda q: q.reciprocal(row3.ap()[0:1, 0:256], row3.ap()[0:1, 0:256]))
            P.wait("tensor", "vC", rs_v)
            P.wait("tensor", "sC", P.val("sC"))
            P.top(lambda q: q.matmul(pb[5].ap()[:, 0:256], ones_rf.ap(), row1.ap()[0:1, 0:256], start=True, stop=True))
            bmm = P.top(lambda q: q.matmul(pb[5].ap()[:, 256:512], ones_rf.ap(), row3.ap()[0:1, 0:256], start=True, stop=True), mark=True)
            P.wait("vector", "tC", bmm)
            for kt in range(NKT):
                P.cop("vector", lambda q, kt=kt: q.tensor_sub(
                    usb.ap()[:, 256 * kt:256 * kt + 256], usb.ap()[:, 256 * kt:256 * kt + 256],
                    pb[5].ap()[:, 0:256]), serial=(kt == 0))
            for kt in range(NKT):
                outc = dst_col0 + kt * TOK + 256 * cb
                if rms_adjust:
                    hv = P.cop("vector", lambda q, kt=kt, outc=outc: q.scalar_tensor_tensor(
                        dst_sb.ap()[:, outc:outc + 256], usb.ap()[:, 256 * kt:256 * kt + 256],
                        rmsw_sb.ap()[:, kt:kt + 1], pb[5].ap()[:, 256:512], OP.mult, OP.mult),
                        serial=(kt == 0))
                else:
                    hv = P.cop("vector", lambda q, kt=kt, outc=outc: q.tensor_tensor(
                        dst_sb.ap()[:, outc:outc + 256], usb.ap()[:, 256 * kt:256 * kt + 256],
                        pb[5].ap()[:, 256:512], OP.mult), serial=(kt == 0))
            return hv

        class WStream:

            def __init__(self, name, width, sem, scale_cols=None, nkt=16, queue="sync",
                         nbuf=2):
                self.stage = SBE(f"{name}_st", [128, width], F32)
                self.out = [SBE(f"{name}_o{i}", [128, width], BF16) for i in range(nbuf)]
                self.sem = sem
                self.queue = queue
                self.nbuf = nbuf
                self.n = 0
                self.cast_v = [0] * nbuf

            def load(self, src_ap, cons_tC=None):
                dst = self.stage.ap()
                if src_ap.ndim == 3:
                    dst = dst.rearrange("p (a b) -> p a b", b=src_ap.shape[2])
                i = self.n % self.nbuf
                if self.n >= 1:
                    P.wait(self.queue, "sC", self.cast_v[(self.n - 1) % self.nbuf])
                P.dma(self.queue, dst, src_ap, self.sem)
                dv = P.val(self.sem)
                P.wait("scalar", self.sem, dv)
                P.wait("scalar", "tC", P.val("tC") if cons_tC is None else cons_tC)
                cv = P.cop("scalar", lambda q, i=i: q.activation(
                    self.out[i].ap(), self.stage.ap(), AF.Copy))
                self.cast_v[i] = cv
                self.n += 1
                return self.out[i], cv

        fence()
        st_e = contextlib.ExitStack()
        SBE = lambda *a: st_e.enter_context(nc.sbuf_tensor(*a))
        h_ffn = SBE("h_ffn", [128, NKT * TOK], BF16)
        dacc = SBE("dacc", [128, 16 * 1024], F32)
        din = SBE("din", [128, 8 * 1024], BF16)
        lg_sb = SBE("lg_sb", [8, TOK], F32)

        st_p0b = contextlib.ExitStack()
        SBP = lambda *a: st_p0b.enter_context(nc.sbuf_tensor(*a))
        u_e = SBP("u_sbE", [128, NKT * 256], F32)
        ubf_e2 = SBP("ubf_e", [128, NKT * 256], BF16)
        usq_e2 = SBP("usq_e", [128, NKT * 256], BF16)
        lgt_sb = SBP("lgt_sb", [128, 16 * 8], F32)
        wn_sb = SBP("wn_sb", [128, 16 * 8], F32)
        rt1 = SBP("rt1", [128, 1], F32)
        rt2 = SBP("rt2", [128, 1], F32)
        rwb_sb = SBP("rwb_sb", [128, 128], BF16)

        hv = 0
        for cb in range(8):
            hv = ln_block(cb, h_ffn, 0, True, hv, u_e, ubf_e2, usq_e2)
        hffn_v = hv

        P.wait("sync", "vC", hv)
        P.dma("sync", u_e.ap()[:, 0:128].rearrange("p (kt e) -> p kt e", e=8),
              rw_d.rearrange("(kt p) e -> p kt e", p=128), "dW1")
        P.wait("vector", "dW1", P.val("dW1"))
        rww_v = P.cop("vector", lambda q, u=u_e: q.tensor_copy(
            rwb_sb.ap(), u.ap()[:, 0:128]))
        P.wait("tensor", "vC", max(rww_v, hffn_v))
        for tb in range(4):
            P.wait("tensor", "sC", P.val("sC"))
            for kt in range(NKT):
                lmm = P.top(lambda q, tb=tb, kt=kt: q.matmul(
                    pb[4].ap()[0:8, 0:512], rwb_sb.ap()[:, 8 * kt:8 * kt + 8],
                    h_ffn.ap()[:, TOK * kt + 512 * tb: TOK * kt + 512 * tb + 512],
                    start=(kt == 0), stop=(kt == NKT - 1)), mark=(kt == NKT - 1))
            P.wait("scalar", "tC", lmm)
            P.cop("scalar", lambda q, tb=tb: q.copy(lg_sb.ap()[:, 512 * tb:512 * tb + 512], pb[4].ap()[0:8, 0:512]))
        lg_v = P.val("sC")
        P.wait("tensor", "sC", lg_v)
        for tt in range(16):
            P.wait("tensor", "vC", P.val("vC"))
            tmm = P.top(lambda q, tt=tt: q.matmul(
                pb[4].ap()[:, 0:8], lg_sb.ap()[0:8, 128 * tt:128 * tt + 128],
                id_f.ap()[0:8, 0:8], start=True, stop=True, is_transpose=True), mark=True)
            P.wait("vector", "tC", tmm)
            P.cop("vector", lambda q, tt=tt: q.tensor_copy(lgt_sb.ap()[:, 8 * tt:8 * tt + 8], pb[4].ap()[:, 0:8]))
        for tt in range(16):
            L = lambda: lgt_sb.ap()[:, 8 * tt:8 * tt + 8]
            P.cop("vector", lambda q, tt=tt: q.reduce_max(rt1.ap(), lgt_sb.ap()[:, 8 * tt:8 * tt + 8], AX.X))
            sub_v = P.cop("vector", lambda q, tt=tt: q.tensor_scalar(
                lgt_sb.ap()[:, 8 * tt:8 * tt + 8], lgt_sb.ap()[:, 8 * tt:8 * tt + 8], rt1.ap()[:, 0:1], None, OP.subtract))
            P.wait("scalar", "vC", sub_v)
            ex_v = P.cop("scalar", lambda q, tt=tt: q.activation(
                lgt_sb.ap()[:, 8 * tt:8 * tt + 8], lgt_sb.ap()[:, 8 * tt:8 * tt + 8], AF.Exp))
            P.wait("vector", "sC", ex_v)
            P.cop("vector", lambda q, tt=tt: q.reduce_sum(rt1.ap(), lgt_sb.ap()[:, 8 * tt:8 * tt + 8], AX.X))
            P.cop("vector", lambda q: q.reciprocal(rt2.ap(), rt1.ap()))
            P.cop("vector", lambda q, tt=tt: q.tensor_scalar(
                lgt_sb.ap()[:, 8 * tt:8 * tt + 8], lgt_sb.ap()[:, 8 * tt:8 * tt + 8], rt2.ap()[:, 0:1], None, OP.mult))
            P.cop("vector", lambda q, tt=tt: q.reduce_max(rt1.ap(), lgt_sb.ap()[:, 8 * tt:8 * tt + 8], AX.X))
            P.cop("vector", lambda q, tt=tt: q.tensor_scalar(
                wn_sb.ap()[:, 8 * tt:8 * tt + 8], lgt_sb.ap()[:, 8 * tt:8 * tt + 8], rt1.ap()[:, 0:1], None, OP.is_ge))
            P.cop("vector", lambda q, tt=tt: q.scalar_tensor_tensor(
                wn_sb.ap()[:, 8 * tt:8 * tt + 8], wn_sb.ap()[:, 8 * tt:8 * tt + 8], -1e9,
                lgt_sb.ap()[:, 8 * tt:8 * tt + 8], OP.mult, OP.add))
            P.cop("vector", lambda q, tt=tt: q.reduce_max(rt2.ap(), wn_sb.ap()[:, 8 * tt:8 * tt + 8], AX.X))
            P.cop("vector", lambda q, tt=tt: q.tensor_scalar(
                wn_sb.ap()[:, 8 * tt:8 * tt + 8], lgt_sb.ap()[:, 8 * tt:8 * tt + 8], rt2.ap()[:, 0:1], None, OP.is_ge))
            P.cop("vector", lambda q, tt=tt: q.tensor_mul(
                wn_sb.ap()[:, 8 * tt:8 * tt + 8], wn_sb.ap()[:, 8 * tt:8 * tt + 8], lgt_sb.ap()[:, 8 * tt:8 * tt + 8]))
            P.cop("vector", lambda q, tt=tt: q.reduce_sum(rt1.ap(), wn_sb.ap()[:, 8 * tt:8 * tt + 8], AX.X))
            P.cop("vector", lambda q: q.reciprocal(rt2.ap(), rt1.ap()))
            wnv = P.cop("vector", lambda q, tt=tt: q.tensor_scalar(
                wn_sb.ap()[:, 8 * tt:8 * tt + 8], wn_sb.ap()[:, 8 * tt:8 * tt + 8], rt2.ap()[:, 0:1], None, OP.mult))
        P.wait("tensor", "vC", wnv)
        for tt in range(16):
            P.wait("tensor", "sC", P.val("sC"))
            tm2 = P.top(lambda q, tt=tt: q.matmul(
                pb[4].ap()[0:8, 0:128], wn_sb.ap()[:, 8 * tt:8 * tt + 8],
                id_f.ap(), start=True, stop=True, is_transpose=True), mark=True)
            P.wait("scalar", "tC", tm2)
            P.cop("scalar", lambda q, tt=tt: q.copy(lg_sb.ap()[:, 128 * tt:128 * tt + 128], pb[4].ap()[0:8, 0:128]))
        wdT_v = P.val("sC")
        fence()
        st_p0b.close()
        wfl = SBE("wfl", [1, 1024], F32)
        wB_sb = SBE("wB_sb", [128, 1024], BF16)
        sgt2 = [SBE(f"sgt{i}", [128, 512], BF16) for i in range(2)]
        t1b2 = [SBE(f"t1b{i}", [128, 512], BF16) for i in range(2)]
        fstg = SBE("fstg", [128, 1024], BF16)

        wg3 = wg_d.rearrange("e (kt p) f -> e p kt f", p=128)
        wu3 = wu_d.rearrange("e (kt p) f -> e p kt f", p=128)
        wd3 = wd_d.rearrange("e (kf p) d -> e p kf d", p=128)
        swg3 = swg_d.rearrange("(kt p) f -> p kt f", p=128)
        swu3 = swu_d.rearrange("(kt p) f -> p kt f", p=128)
        swd3 = swd_d.rearrange("(kf p) d -> p kf d", p=128)

        ws_gu = WStream("wsgu", 2048, "dW2", nbuf=4)
        ws_d = WStream("wsd", 1024, "dW4", nbuf=2)

        def gu_src(e, mt):
            if e == 8:
                return (swg3[:, :, 128 * mt:128 * mt + 128],
                        swu3[:, :, 128 * mt:128 * mt + 128])
            return (wg3[e, :, :, 128 * mt:128 * mt + 128],
                    wu3[e, :, :, 128 * mt:128 * mt + 128])

        def d_src(e, md):
            if e == 8:
                return swd3[:, :, 128 * md:128 * md + 128]
            return wd3[e, :, :, 128 * md:128 * md + 128]

        SETS = [[0, 1, 2, 3], [4, 5, 6, 7]]
        gu_seq = [(hf, e, mt) for hf in range(2) for e in range(9) for mt in range(8)]
        d_seq = [(hf, e, md) for hf in range(2) for e in range(9) for md in range(16)]
        g_mark = {}
        u_mark = {}
        tt_v = {}
        sg_v = {}
        din_g = {}
        dmark = {}
        dadd_v = {}

        def gu_load(i):
            _, e_, mt_ = gu_seq[i]
            gs, us = gu_src(e_, mt_)
            gb = ws_gu.load(gs, cons_tC=g_mark.get(i - 2, 0))
            ub = ws_gu.load(us, cons_tC=u_mark.get(i - 2, 0))
            return gb, ub

        def d_load(j):
            _, e_, md_ = d_seq[j]
            return ws_d.load(d_src(e_, md_), cons_tC=dmark.get(j - 2, 0))

        gu_next = gu_load(0)
        d_next = d_load(0)
        wfl_mark = 0
        sb_gate_free = wdT_v
        din_v = 0
        prev_down_mark = 0
        last_dadd = 0
        last_tt = 0
        for hf in range(2):
            t0 = 1024 * hf
            for e in range(9):
                shared = (e == 8)
                ebase = 72 * hf + 8 * e
                if not shared:
                    P.wait("sync", "tC", wfl_mark)
                    P.dma("sync", wfl.ap(), lg_sb.ap()[e:e + 1, t0:t0 + 1024], "dW1")
                    wflv = P.val("dW1")
                    P.wait("tensor", "dW1", wflv)
                    P.wait("tensor", "sC", sb_gate_free)
                    P.top(lambda q: q.matmul(
                        pb[4].ap(), ones_rf.ap(), wfl.ap()[0:1, 0:512],
                        start=True, stop=True))
                    wfl_mark = P.top(lambda q: q.matmul(
                        pb[5].ap(), ones_rf.ap(), wfl.ap()[0:1, 512:1024],
                        start=True, stop=True), mark=True)
                    P.wait("scalar", "tC", wfl_mark)
                    P.wait("scalar", "gC", din_v)
                    P.cop("scalar", lambda q: q.activation(
                        wB_sb.ap()[:, 0:512], pb[4].ap(), AF.Copy))
                    wb_v = P.cop("scalar", lambda q: q.activation(
                        wB_sb.ap()[:, 512:1024], pb[5].ap(), AF.Copy))
                for mt in range(8):
                    i = ebase + mt
                    (wgb, gcv), (wub, ucv) = gu_next
                    if i + 1 < len(gu_seq):
                        gu_next = gu_load(i + 1)
                    S = SETS[mt % 2]
                    if mt >= 2:
                        P.wait("tensor", "vC", tt_v[(i - 2, 1)])
                    elif mt == 0:
                        P.wait("tensor", "vC", last_dadd)
                    else:
                        P.wait("tensor", "vC", last_tt)
                    P.wait("tensor", "sC", gcv)
                    for tb2 in range(2):
                        for kt in range(NKT):
                            gm = P.top(lambda q, kt=kt, tb2=tb2, wgb=wgb, t0=t0, S=S: q.matmul(
                                pb[S[tb2]].ap(), wgb.ap()[:, 128 * kt:128 * kt + 128],
                                h_ffn.ap()[:, TOK * kt + t0 + 512 * tb2:
                                           TOK * kt + t0 + 512 * tb2 + 512],
                                start=(kt == 0), stop=(kt == NKT - 1)), mark=(kt == NKT - 1))
                        g_mark[i] = gm
                    P.wait("tensor", "sC", ucv)
                    for tb2 in range(2):
                        for kt in range(NKT):
                            um = P.top(lambda q, kt=kt, tb2=tb2, wub=wub, t0=t0, S=S: q.matmul(
                                pb[S[2 + tb2]].ap(), wub.ap()[:, 128 * kt:128 * kt + 128],
                                h_ffn.ap()[:, TOK * kt + t0 + 512 * tb2:
                                           TOK * kt + t0 + 512 * tb2 + 512],
                                start=(kt == 0), stop=(kt == NKT - 1)), mark=(kt == NKT - 1))
                        u_mark[i] = um
                    for tb2 in range(2):
                        P.wait("scalar", "tC", g_mark[i])
                        if i >= 1:
                            P.wait("scalar", "vC", tt_v.get((i - 1, tb2), 0))
                        sg = P.cop("scalar", lambda q, S=S, tb2=tb2, sl=sgt2[tb2]: q.activation(
                            sl.ap(), pb[S[tb2]].ap(), AF.Silu))
                        sg_v[(i, tb2)] = sg
                        P.wait("vector", "tC", u_mark[i])
                        P.wait("vector", "sC", sg)
                        if i >= 1:
                            P.wait("vector", "gC", din_g.get((i - 1, tb2), 0))
                        tt = P.cop("vector", lambda q, S=S, tb2=tb2, sl=t1b2[tb2]: q.tensor_tensor(
                            sl.ap(), pb[S[2 + tb2]].ap(), sgt2[tb2].ap(), OP.mult))
                        tt_v[(i, tb2)] = tt
                        P.wait("gpsimd", "vC", tt)
                        if mt == 0 and tb2 == 0:
                            P.wait("gpsimd", "tC", prev_down_mark)
                            if not shared:
                                P.wait("gpsimd", "sC", wb_v)
                        if shared:
                            dg = P.cop("gpsimd", lambda q, mt=mt, tb2=tb2: q.tensor_copy(
                                din.ap()[:, 1024 * mt + 512 * tb2: 1024 * mt + 512 * tb2 + 512],
                                t1b2[tb2].ap()))
                        else:
                            dg = P.cop("gpsimd", lambda q, mt=mt, tb2=tb2: q.tensor_tensor(
                                din.ap()[:, 1024 * mt + 512 * tb2: 1024 * mt + 512 * tb2 + 512],
                                t1b2[tb2].ap(), wB_sb.ap()[:, 512 * tb2:512 * tb2 + 512],
                                OP.mult))
                        din_g[(i, tb2)] = dg
                    if mt == 7:
                        sb_gate_free = sg_v[(i, 1)]
                        last_tt = tt_v[(i, 1)]
                din_v = P.val("gC")
                for md in range(16):
                    j = 144 * hf + 16 * e + md
                    wdb, dcv = d_next
                    if j + 1 < len(d_seq):
                        d_next = d_load(j + 1)
                    PA = [0, 1] if md % 2 == 0 else [2, 3]
                    P.wait("tensor", "sC", dcv)
                    if md == 0:
                        P.wait("tensor", "gC", din_v)
                        P.wait("tensor", "vC", last_tt)
                        P.wait("tensor", "sC", sb_gate_free)
                    elif md >= 2:
                        P.wait("tensor", "vC", dadd_v[j - 2])
                    for b2 in range(2):
                        for kf in range(8):
                            dm = P.top(lambda q, kf=kf, b2=b2, wdb=wdb, PA=PA: q.matmul(
                                pb[PA[b2]].ap(), wdb.ap()[:, 128 * kf:128 * kf + 128],
                                din.ap()[:, 1024 * kf + 512 * b2: 1024 * kf + 512 * b2 + 512],
                                start=(kf == 0), stop=(kf == 7)), mark=(kf == 7))
                        dmark[j] = dm
                    P.wait("vector", "tC", dmark[j])
                    for b2 in range(2):
                        sl = (1024 * md + 512 * b2, 1024 * md + 512 * b2 + 512)
                        if e == 0:
                            da = P.cop("vector", lambda q, sl=sl, PA=PA, b2=b2: q.tensor_copy(
                                dacc.ap()[:, sl[0]:sl[1]], pb[PA[b2]].ap()))
                        else:
                            da = P.cop("vector", lambda q, sl=sl, PA=PA, b2=b2: q.tensor_tensor(
                                dacc.ap()[:, sl[0]:sl[1]], dacc.ap()[:, sl[0]:sl[1]],
                                pb[PA[b2]].ap(), OP.add))
                    dadd_v[j] = da
                    prev_down_mark = dmark[j]
                last_dadd = dadd_v[144 * hf + 16 * e + 15]
            for md in range(16):
                P.wait("vector", "dFf", P.val("dFf"))
                fc_v = P.cop("vector", lambda q, md=md: q.tensor_copy(
                    fstg.ap(), dacc.ap()[:, 1024 * md:1024 * md + 1024]))
                P.wait("sync", "vC", fc_v)
                P.dma("sync", ffn_d.ap()[128 * md:128 * md + 128, t0:t0 + 1024], fstg.ap(), "dFf")
                P.wait("sync", "dFf", P.val("dFf"))
        st_e.close()

        dff_final = P.val("dFf")
        fence()
        st_c = contextlib.ExitStack()
        SBE = lambda *a: st_c.enter_context(nc.sbuf_tensor(*a))
        xnh = SBE("xnh", [128, NKT * TOK], BF16)
        u_c = SBE("u_sbC", [128, NKT * 256], F32)
        ubf_c = SBE("ubf_c", [128, NKT * 256], BF16)
        usq_c = SBE("usq_c", [128, NKT * 256], BF16)
        gt = [SBE(f"gt{i}", [128, 1024], BF16) for i in range(4)]
        ysel2 = [SBE(f"ysels{i}", [128, 512], BF16) for i in range(2)]
        szt2 = [SBE(f"szts{i}", [128, 512], BF16) for i in range(2)]
        gatb2 = [SBE(f"gatbs{i}", [128, 512], BF16) for i in range(2)]
        sqt2 = [SBE(f"sqts{i}", [128, 512], BF16) for i in range(2)]
        sums_sb = SBE("sums_sb", [1, TOK], F32)
        ws_z = WStream("wsz", 2048, "dW2", None)

        hv = 0
        for cb in range(8):
            hv = ln_block(cb, xnh, 0, False, hv, u_c, ubf_c, usq_c)
        xnh_v = hv
        zs_v = P.cop("vector", lambda q: q.memset(sums_sb.ap(), 0.0))

        SETS = [[0, 1, 2, 3], [4, 5, 6, 7]]
        Wz3 = Wz_d.rearrange("(kt p) z -> p kt z", p=128)
        NIT = 128

        gld_v = {}
        bld_v = {}
        gat_v = {}
        sqq_v = {}
        dga_v = {}
        qmark = {}
        sumadd = {}
        silu_vv = {}
        zmark = {}

        def issue_g(k):
            mt_, tb_ = divmod(k, 4)
            r_, chp_ = divmod(mt_, 16)
            row = 2 * D * r_ + 128 * chp_
            if k >= 4:
                P.wait("gpsimd", "vC", bld_v[k - 4])
            P.dma("gpsimd", gt[k % 4].ap()[:, 0:512],
                  G_d.ap()[row: row + 128, 512 * tb_: 512 * tb_ + 512], "dG")
            P.dma("gpsimd", gt[k % 4].ap()[:, 512:1024],
                  G_d.ap()[row + D: row + D + 128, 512 * tb_: 512 * tb_ + 512], "dG")
            gld_v[k] = P.val("dG")

        if stop_after is None:
            P.wait("gpsimd", "gC", ag_v)
        for k in range(3):
            issue_g(k)

        wz_cur = ws_z.load(Wz3[:, :, 0:128], cons_tC=0)
        for mt in range(32):
            wzb, zcv = wz_cur
            if mt < 31:
                wz_cur = ws_z.load(Wz3[:, :, 128 * (mt + 1):128 * (mt + 1) + 128],
                                   cons_tC=zmark.get(4 * (mt - 1) + 3, 0))
            S = SETS[mt % 2]
            P.wait("tensor", "sC", zcv)
            if mt == 0:
                P.wait("tensor", "vC", xnh_v)
            for tb in range(4):
                k = 4 * mt + tb
                if mt >= 2:
                    P.wait("tensor", "vC", sumadd[k - 8])
                    P.wait("tensor", "sC", silu_vv[k - 8])
                for kt in range(NKT):
                    zm = P.top(lambda q, kt=kt, tb=tb, wzb=wzb, S=S: q.matmul(
                        pb[S[tb]].ap(), wzb.ap()[:, 128 * kt:128 * kt + 128],
                        xnh.ap()[:, TOK * kt + 512 * tb: TOK * kt + 512 * tb + 512],
                        start=(kt == 0), stop=(kt == NKT - 1)), mark=(kt == NKT - 1))
                zmark[k] = zm
            for tb in range(4):
                k = 4 * mt + tb
                pbank = pb[S[tb]]
                if k + 3 < NIT:
                    issue_g(k + 3)
                P.wait("scalar", "tC", zmark[k])
                if k >= 2:
                    P.wait("scalar", "vC", gat_v[k - 2])
                sz = P.cop("scalar", lambda q, pbank=pbank, sl=szt2[tb % 2]: q.activation(
                    sl.ap(), pbank.ap(), AF.Silu))
                silu_vv[k] = sz
                P.wait("vector", "dG", gld_v[k])
                P.cop("vector", lambda q, kk=k, sl=ysel2[tb % 2]: q.tensor_scalar(
                    sl.ap(), gt[kk % 4].ap()[:, 0:512], selm_sb.ap()[:, 0:1], None, OP.mult))
                bl = P.cop("vector", lambda q, kk=k, sl=ysel2[tb % 2]: q.scalar_tensor_tensor(
                    sl.ap(), gt[kk % 4].ap()[:, 512:1024], selm_sb.ap()[:, 1:2],
                    sl.ap(), OP.mult, OP.add))
                bld_v[k] = bl
                P.wait("vector", "sC", sz)
                if k >= 2:
                    P.wait("vector", "dGa", dga_v[k - 2])
                    P.wait("vector", "sC", sqq_v[k - 2])
                gv2 = P.cop("vector", lambda q, sl=gatb2[tb % 2], ys=ysel2[tb % 2],
                            sz2=szt2[tb % 2]:
                            q.tensor_tensor(sl.ap(), ys.ap(), sz2.ap(), OP.mult))
                gat_v[k] = gv2
                P.wait("scalar", "vC", gv2)
                P.dma("scalar", gat_d.ap()[128 * mt:128 * mt + 128, 512 * tb:512 * tb + 512],
                      gatb2[tb % 2].ap(), "dGa")
                dga_v[k] = P.val("dGa")
                if k >= 2:
                    P.wait("scalar", "tC", qmark[k - 2])
                sq = P.cop("scalar", lambda q, sl=sqt2[tb % 2], g=gatb2[tb % 2]: q.activation(
                    sl.ap(), g.ap(), AF.Square))
                sqq_v[k] = sq
                P.wait("tensor", "sC", sq)
                qm = P.top(lambda q, pbank=pbank, sl=sqt2[tb % 2]: q.matmul(
                    pbank.ap()[0:1, 0:512], ones_cb.ap(), sl.ap(), start=True, stop=True),
                    mark=True)
                qmark[k] = qm
                P.wait("vector", "tC", qm)
                if mt == 0 and tb == 0:
                    P.wait("vector", "vC", zs_v)
                sa = P.cop("vector", lambda q, tb=tb, pbank=pbank: q.tensor_tensor(
                    sums_sb.ap()[0:1, 512 * tb:512 * tb + 512],
                    sums_sb.ap()[0:1, 512 * tb:512 * tb + 512],
                    pbank.ap()[0:1, 0:512], OP.add))
                sumadd[k] = sa
        sums_v = P.val("vC")
        dga_final = P.val("dGa")
        st_c.close()

        fence()
        st_d = contextlib.ExitStack()
        SBE = lambda *a: st_d.enter_context(nc.sbuf_tensor(*a))
        ynorm = SBE("ynorm", [128, 32 * TOK], BF16)
        rstdB = SBE("rstdB", [128, TOK], F32)
        row4 = SBE("row4", [1, TOK], F32)
        gstg = [SBE(f"gstg{i}", [128, TOK], BF16) for i in range(2)]
        ustg = [SBE(f"ustg{i}", [128, 512], F32) for i in range(2)]
        ffstg = [SBE(f"ffstg{i}", [128, 512], BF16) for i in range(2)]
        otb = [SBE(f"otb{i}", [128, 512], F32) for i in range(2)]
        ws_o = WStream("wso", 4096, "dW2", None, nkt=32)

        P.wait("vector", "vC", sums_v)
        P.cop("vector", lambda q: q.tensor_scalar_mul(row4.ap(), sums_sb.ap(), 1.0 / DIN))
        rw4 = P.val("vC")
        P.wait("scalar", "vC", rw4)
        sq4 = P.cop("scalar", lambda q: q.activation(row4.ap(), row4.ap(), AF.Sqrt, bias=MRMS_EPS))
        P.wait("vector", "sC", sq4)
        rc4 = P.cop("vector", lambda q: q.reciprocal(row4.ap(), row4.ap()))
        P.wait("tensor", "vC", rc4)
        for tb in range(4):
            P.wait("tensor", "sC", P.val("sC"))
            bmm = P.top(lambda q, tb=tb: q.matmul(
                pb[4].ap(), ones_rf.ap(), row4.ap()[0:1, 512 * tb:512 * tb + 512],
                start=True, stop=True), mark=True)
            P.wait("scalar", "tC", bmm)
            P.cop("scalar", lambda q, tb=tb: q.activation(
                rstdB.ap()[:, 512 * tb:512 * tb + 512], pb[4].ap(), AF.Copy))
        rstdB_v = P.val("sC")

        P.wait("gpsimd", "dGa", dga_final)
        gld2 = {}
        yv_k = {}

        def issue_gat(kt):
            if kt >= 2:
                P.wait("gpsimd", "vC", yv_k[kt - 2])
            P.dma("gpsimd", gstg[kt % 2].ap(), gat_d.ap()[128 * kt:128 * kt + 128, :], "dGs")
            gld2[kt] = P.val("dGs")

        issue_gat(0)
        issue_gat(1)
        yv = 0
        for kt in range(32):
            P.wait("vector", "dGs", gld2[kt])
            if kt == 0:
                P.wait("vector", "sC", rstdB_v)
            yv = P.cop("vector", lambda q, kt=kt: q.scalar_tensor_tensor(
                ynorm.ap()[:, TOK * kt:TOK * kt + TOK], gstg[kt % 2].ap(),
                nrmw_sb.ap()[:, kt:kt + 1], rstdB.ap(), OP.mult, OP.mult))
            yv_k[kt] = yv
            if kt + 2 < 32:
                issue_gat(kt + 2)
        yn_v = yv

        P.wait("scalar", "dFf", dff_final)
        uTh3b = uTh.rearrange("(mo p) T -> p mo T", p=128)
        uld = {}
        tv1_v = {}
        ffadd_v = {}
        dout_v = {}
        omark = {}

        def issue_uf(k):
            mo_, tb_ = divmod(k, 4)
            if k >= 2:
                P.wait("scalar", "vC", tv1_v[k - 2])
                P.wait("scalar", "gC", ffadd_v[k - 2])
            P.dma("scalar", ustg[k % 2].ap(), uTh3b[:, mo_, 512 * tb_:512 * tb_ + 512], "dU2")
            P.dma("scalar", ffstg[k % 2].ap(),
                  ffn_d.ap()[128 * mo_:128 * mo_ + 128, 512 * tb_:512 * tb_ + 512], "dU2")
            uld[k] = P.val("dU2")

        issue_uf(0)
        issue_uf(1)

        Wo3 = Wo_d.rearrange("(kt p) d -> p kt d", p=128)
        wo_cur = ws_o.load(Wo3[:, :, 0:128], cons_tC=0)
        for mo in range(16):
            wob, ocv = wo_cur
            if mo < 15:
                wo_cur = ws_o.load(Wo3[:, :, 128 * (mo + 1):128 * (mo + 1) + 128],
                                   cons_tC=omark.get(4 * (mo - 1) + 3, 0))
            S = SETS[mo % 2]
            P.wait("tensor", "sC", ocv)
            if mo == 0:
                P.wait("tensor", "vC", yn_v)
            for tb in range(4):
                k = 4 * mo + tb
                if mo >= 2:
                    P.wait("tensor", "vC", tv1_v[k - 8])
                for kt in range(32):
                    om = P.top(lambda q, kt=kt, tb=tb, wob=wob, S=S: q.matmul(
                        pb[S[tb]].ap(), wob.ap()[:, 128 * kt:128 * kt + 128],
                        ynorm.ap()[:, TOK * kt + 512 * tb: TOK * kt + 512 * tb + 512],
                        start=(kt == 0), stop=(kt == 31)), mark=(kt == 31))
                omark[k] = om
            for tb in range(4):
                k = 4 * mo + tb
                pbank = pb[S[tb]]
                P.wait("vector", "tC", omark[k])
                P.wait("vector", "dU2", uld[k])
                if k >= 2:
                    P.wait("vector", "dOut", dout_v[k - 2])
                tv = P.cop("vector", lambda q, pbank=pbank, sl=otb[k % 2], us=ustg[k % 2]:
                           q.tensor_tensor(sl.ap(), pbank.ap(), us.ap(), OP.add))
                tv1_v[k] = tv
                P.wait("gpsimd", "vC", tv)
                ov = P.cop("gpsimd", lambda q, sl=otb[k % 2], fs=ffstg[k % 2]:
                           q.tensor_tensor(sl.ap(), sl.ap(), fs.ap(), OP.add))
                ffadd_v[k] = ov
                P.wait("gpsimd", "gC", ov)
                P.dma("gpsimd", out_d[128 * mo:128 * mo + 128, 512 * tb:512 * tb + 512],
                      otb[k % 2].ap(), "dOut")
                dout_v[k] = P.val("dOut")
                if k + 2 < 64:
                    issue_uf(k + 2)
        st_d.close()
        P.wait("gpsimd", "dOut", P.val("dOut"))
        P.wait("scalar", "dOut", P.val("dOut"))
        P.wait("scalar", "dStF", P.val("dStF"))
        P.wait("scalar", "dGa", P.val("dGa"))
        P.wait("scalar", "dFf", P.val("dFf"))

    with nc.Block() as block:
        @block.sync
        def _(q):
            for f in P.streams["sync"]:
                f(q)

        @block.scalar
        def _(q):
            for f in P.streams["scalar"]:
                f(q)

        @block.vector
        def _(q):
            for f in P.streams["vector"]:
                f(q)

        @block.tensor
        def _(q):
            for f in P.streams["tensor"]:
                f(q)

        @block.gpsimd
        def _(q):
            for f in P.streams["gpsimd"]:
                f(q)

    return nc, dbg_aps


def kernel(**inputs):
    in_maps = host_layout(inputs)
    nc, _ = build_program()
    res = run_bass_kernel_spmd(nc, in_maps, core_ids=list(range(8)))
    out = np.zeros((NB, LQ, D), dtype=np.float32)
    for c in range(8):
        b, s = c // 2, c % 2
        out[b, TOK * s:TOK * s + TOK, :] = res.results[c]["out"].T
    return out


# version 09 (final)
# speedup vs baseline: 1.5185x; 1.1706x over previous
import contextlib

import numpy as np
import ml_dtypes
import concourse.bass as bass
import concourse.mybir as mybir
from concourse.bass_utils import run_bass_kernel_spmd

F32 = mybir.dt.float32
BF16 = mybir.dt.bfloat16
AF = mybir.ActivationFunctionType
OP = mybir.AluOpType
AX = mybir.AxisListType

D = 2048
LQ = 4096
NB = 4
DIN = 4096
NHC = 32
HP = 64
DS = 128
CH = 256
NCHUNK = LQ // CH
CONVC = NHC * HP + 2 * DS
NCT = CONVC // 128
WAC = CONVC + NHC
NKT = D // 128
TOK = 2048
NE = 8
FF = 1024
SF = 1024
LN_EPS = 1e-5
MRMS_EPS = 1e-5
FRMS_EPS = 1e-6

ENGS = ("sync", "scalar", "vector", "tensor", "gpsimd")
CHAIN = {"scalar": "sC", "vector": "vC", "gpsimd": "gC"}


class Prog:
    def __init__(self, nc):
        self.nc = nc
        self.streams = {e: [] for e in ENGS}
        self.sems = {}
        self.cnt = {}

    def sem(self, name):
        if name not in self.sems:
            self.sems[name] = self.nc.alloc_semaphore(name)
            self.cnt[name] = 0
        return self.sems[name]

    def emit(self, eng, fn):
        self.streams[eng].append(fn)

    def wait(self, eng, sname, val=None):
        h = self.sem(sname)
        v = self.cnt[sname] if val is None else val
        if v > 0:
            self.emit(eng, lambda q, h=h, v=v: q.wait_ge(h, v))
        return v

    def cop(self, eng, fn, serial=True):
        ch = CHAIN[eng]
        h = self.sem(ch)
        prev = self.cnt[ch]
        if serial and prev > 0:
            self.emit(eng, lambda q, h=h, v=prev: q.wait_ge(h, v))
        self.cnt[ch] = prev + 1
        self.emit(eng, lambda q, fn=fn, h=h: fn(q).then_inc(h))
        return prev + 1

    def top(self, fn, mark=False):
        if not mark:
            self.emit("tensor", fn)
            return self.cnt.get("tC", 0)
        h = self.sem("tC")
        self.cnt["tC"] += 1
        self.emit("tensor", lambda q, fn=fn, h=h: fn(q).then_inc(h))
        return self.cnt["tC"]

    def dma(self, eng, dst, src, inc, slow=False):
        h = self.sem(inc)
        self.cnt[inc] += 16
        self.emit(eng, lambda q, d=dst, s=src, sl=slow, h=h: q.dma_start(
            out=d, in_=s, allow_slow_non_contiguous=sl).then_inc(h, 16))
        return self.cnt[inc]

    def val(self, sname):
        return self.cnt.get(sname, 0)


def host_layout(inputs):
    u = np.asarray(inputs["L0_attn_u"], dtype=np.float32)
    W_in = np.asarray(inputs["L0_attn_W_in"], dtype=np.float32)
    conv_w = np.asarray(inputs["L0_attn_conv_w"], dtype=np.float32)
    conv_b = np.asarray(inputs["L0_attn_conv_b"], dtype=np.float32)
    dt_bias = np.asarray(inputs["L0_attn_dt_bias"], dtype=np.float32)
    A_log = np.asarray(inputs["L0_attn_A_log"], dtype=np.float32)
    Dp = np.asarray(inputs["L0_attn_Dp"], dtype=np.float32)
    norm_w = np.asarray(inputs["L0_attn_norm_w"], dtype=np.float32)
    W_out = np.asarray(inputs["L0_attn_W_out"], dtype=np.float32)
    rms_w = np.asarray(inputs["L0_ffn_rms_weight"], dtype=np.float32)
    router_w = np.asarray(inputs["L0_ffn_router_w"], dtype=np.float32)
    w_gate = np.asarray(inputs["L0_ffn_w_gate"], dtype=np.float32)
    w_up = np.asarray(inputs["L0_ffn_w_up"], dtype=np.float32)
    w_down = np.asarray(inputs["L0_ffn_w_down"], dtype=np.float32)
    sw_gate = np.asarray(inputs["L0_ffn_sw_gate"], dtype=np.float32)
    sw_up = np.asarray(inputs["L0_ffn_sw_up"], dtype=np.float32)
    sw_down = np.asarray(inputs["L0_ffn_sw_down"], dtype=np.float32)

    in_maps = []
    for c in range(8):
        b, s = c // 2, c % 2
        hh = slice(32 * s, 32 * s + 32)
        xch = slice(DIN + 2048 * s, DIN + 2048 * s + 2048)
        WA = np.concatenate([W_in[xch], W_in[8192:8448],
                             W_in[8448 + 32 * s: 8448 + 32 * s + 32]], axis=0)
        cw = np.concatenate([conv_w[2048 * s:2048 * s + 2048], conv_w[4096:4352]], axis=0)
        cb = np.concatenate([conv_b[2048 * s:2048 * s + 2048], conv_b[4096:4352]], axis=0)
        m = {
            "uTf": np.ascontiguousarray(u[b].T),
            "uTh": np.ascontiguousarray(u[b, 2048 * s:2048 * s + 2048].T),
            "WA": np.ascontiguousarray(WA.T),
            "Wz": np.ascontiguousarray(W_in[0:DIN].T),
            "Wo": np.ascontiguousarray(W_out.T),
            "convw": np.ascontiguousarray(cw.reshape(NCT, 128, 4).transpose(1, 0, 2).reshape(128, NCT * 4)),
            "convb": np.ascontiguousarray(cb.reshape(NCT, 128).T),
            "dtb": np.ascontiguousarray(dt_bias[hh].reshape(32, 1)),
            "alog": np.ascontiguousarray(A_log[hh].reshape(32, 1)),
            "dpv": np.ascontiguousarray(Dp[hh].reshape(32, 1)),
            "nrmw": np.ascontiguousarray(norm_w.reshape(32, 128).T),
            "rmsw": np.ascontiguousarray(rms_w.reshape(16, 128).T),
            "rw": router_w,
            "wg": w_gate, "wu": w_up, "wd": w_down,
            "swg": sw_gate, "swu": sw_up, "swd": sw_down,
            "selm": np.ascontiguousarray(
                np.repeat(np.array([[1.0 - s, float(s)]], dtype=np.float32), 128, axis=0)),
        }
        in_maps.append(m)
    return in_maps


def build_program(debug=(), stop_after=None):
    nc = bass.Bass("TRN2", target_bir_lowering=False)
    P = Prog(nc)
    dbg = set(debug)
    dbg_aps = {}

    def mkdbg(name, shape, dtype=F32):
        if name in dbg:
            ap = nc.dram_tensor("dbg_" + name, list(shape), dtype, kind="ExternalOutput").ap()
            dbg_aps[name] = ap
            return ap
        return None

    uTf = nc.dram_tensor("uTf", [D, LQ], F32, kind="ExternalInput").ap()
    uTh = nc.dram_tensor("uTh", [D, TOK], F32, kind="ExternalInput").ap()
    WA_d = nc.dram_tensor("WA", [D, WAC], F32, kind="ExternalInput").ap()
    Wz_d = nc.dram_tensor("Wz", [D, DIN], F32, kind="ExternalInput").ap()
    Wo_d = nc.dram_tensor("Wo", [DIN, D], F32, kind="ExternalInput").ap()
    convw_d = nc.dram_tensor("convw", [128, NCT * 4], F32, kind="ExternalInput").ap()
    convb_d = nc.dram_tensor("convb", [128, NCT], F32, kind="ExternalInput").ap()
    dtb_d = nc.dram_tensor("dtb", [32, 1], F32, kind="ExternalInput").ap()
    alog_d = nc.dram_tensor("alog", [32, 1], F32, kind="ExternalInput").ap()
    dp_d = nc.dram_tensor("dpv", [32, 1], F32, kind="ExternalInput").ap()
    nrmw_d = nc.dram_tensor("nrmw", [128, 32], F32, kind="ExternalInput").ap()
    rmsw_d = nc.dram_tensor("rmsw", [128, 16], F32, kind="ExternalInput").ap()
    rw_d = nc.dram_tensor("rw", [D, NE], F32, kind="ExternalInput").ap()
    wg_d = nc.dram_tensor("wg", [NE, D, FF], F32, kind="ExternalInput").ap()
    wu_d = nc.dram_tensor("wu", [NE, D, FF], F32, kind="ExternalInput").ap()
    wd_d = nc.dram_tensor("wd", [NE, FF, D], F32, kind="ExternalInput").ap()
    swg_d = nc.dram_tensor("swg", [D, SF], F32, kind="ExternalInput").ap()
    swu_d = nc.dram_tensor("swu", [D, SF], F32, kind="ExternalInput").ap()
    swd_d = nc.dram_tensor("swd", [SF, D], F32, kind="ExternalInput").ap()
    selm_d = nc.dram_tensor("selm", [128, 2], F32, kind="ExternalInput").ap()
    out_d = nc.dram_tensor("out", [D, TOK], F32, kind="ExternalOutput").ap()

    S_d = nc.dram_tensor("S", [2 * D, TOK], BF16, kind="Internal")
    G_d = nc.dram_tensor("G", [2 * 2 * D, TOK], BF16, kind="Internal")
    gat_d = nc.dram_tensor("gat", [DIN, TOK], BF16, kind="Internal")
    ffn_d = nc.dram_tensor("ffn", [D, TOK], BF16, kind="Internal")

    id_bf_t = nc.inline_tensor(np.eye(128, dtype=ml_dtypes.bfloat16), name="t_id_bf")
    id_f_t = nc.inline_tensor(np.eye(128, dtype=np.float32), name="t_id_f")
    ones_cb_t = nc.inline_tensor(np.ones((128, 1), dtype=ml_dtypes.bfloat16), name="t_ones_cb")
    ones_rf_t = nc.inline_tensor(np.ones((1, 128), dtype=np.float32), name="t_ones_rf")
    psel = np.zeros((2, 128), dtype=np.float32)
    psel[0, 0:64] = 1.0
    psel[1, 64:128] = 1.0
    psel_t = nc.inline_tensor(psel, name="t_psel")
    mj = (np.arange(128)[:, None] <= np.arange(256)[None, :]).astype(ml_dtypes.bfloat16)
    mask_t = nc.inline_tensor(np.ascontiguousarray(mj), name="t_mask")
    ind = (np.arange(32)[:, None] == (np.arange(2048)[None, :] // 64)).astype(ml_dtypes.bfloat16)
    ind_t = nc.inline_tensor(np.ascontiguousarray(ind), name="t_ind")
    ones32_t = nc.inline_tensor(np.ones((32, 128), dtype=np.float32), name="t_ones32")
    SCALAR_CONSTS = [LN_EPS, 1.0, MRMS_EPS, FRMS_EPS, 0.5, 2.0]
    sconst_t = nc.inline_tensor(
        np.ascontiguousarray(np.repeat(np.array([SCALAR_CONSTS], dtype=np.float32), 128, axis=0)),
        name="t_sconst")

    A = nc.alloc_sbuf_tensor
    id_bf = A("id_bf", [128, 128], BF16)
    id_f = A("id_f", [128, 128], F32)
    ones_cb = A("ones_cb", [128, 1], BF16)
    ones_rf = A("ones_rf", [1, 128], F32)
    mask_sb = A("mask_sb", [128, 256], BF16)
    convw_sb = A("convw_sb", [128, NCT * 4], F32)
    convb_sb = A("convb_sb", [128, NCT], F32)
    dtb_sb = A("dtb_sb", [32, 1], F32)
    alog_sb = A("alog_sb", [32, 1], F32)
    nA_sb = A("nA_sb", [32, 1], F32)
    ones32_sb = A("ones32_sb", [32, 128], F32)
    dpB_sb = A("dpB_sb", [128, 16], F32)
    selm_sb = A("selm_sb", [128, 2], F32)
    nrmw_sb = A("nrmw_sb", [128, 32], F32)
    rmsw_sb = A("rmsw_sb", [128, 16], F32)
    row1 = A("row1", [1, 512], F32)
    row2 = A("row2", [1, 512], F32)
    row3 = A("row3", [1, 512], F32)
    sconst_sb = A("sconst_sb", [128, len(SCALAR_CONSTS)], F32)
    for i, v in enumerate(SCALAR_CONSTS):
        nc.const_aps.register(F32, v, sconst_sb.ap()[:, i:i + 1])

    pb = [nc.alloc_psum_tensor(f"pb{i}", [128, 512], F32) for i in range(8)]

    for dst, src in [(id_bf, id_bf_t), (id_f, id_f_t), (ones_cb, ones_cb_t),
                     (ones_rf, ones_rf_t), (mask_sb, mask_t),
                     (sconst_sb, sconst_t), (ones32_sb, ones32_t)]:
        P.dma("sync", dst.ap(), src.ap(), "dIn")
    P.dma("sync", convw_sb.ap(), convw_d, "dIn")
    P.dma("sync", convb_sb.ap(), convb_d, "dIn")
    P.dma("sync", dtb_sb.ap(), dtb_d, "dIn")
    P.dma("sync", alog_sb.ap(), alog_d, "dIn")
    P.dma("sync", selm_sb.ap(), selm_d, "dIn")
    P.dma("sync", nrmw_sb.ap(), nrmw_d, "dIn")
    P.dma("sync", rmsw_sb.ap(), rmsw_d, "dIn")
    P.dma("gpsimd", dpB_sb.ap()[0:64, :],
          dp_d[0:32:2, 0:1].rearrange("a b -> b a").partition_broadcast(64).squeeze(1), "dGp", slow=True)
    P.dma("gpsimd", dpB_sb.ap()[64:128, :],
          dp_d[1:32:2, 0:1].rearrange("a b -> b a").partition_broadcast(64).squeeze(1), "dGp", slow=True)
    cload = P.val("dIn")

    P.wait("scalar", "dIn", cload)
    P.cop("scalar", lambda q: q.activation(nA_sb.ap(), alog_sb.ap(), AF.Exp))
    P.wait("vector", "sC")
    P.cop("vector", lambda q: q.tensor_scalar_mul(nA_sb.ap(), nA_sb.ap(), -1.0))

    st_ab = contextlib.ExitStack()
    SB = lambda *a: st_ab.enter_context(nc.sbuf_tensor(*a))
    WA_sb = SB("WA_sb", [128, NKT * WAC], BF16)
    xn_sb = [SB(f"xn_sb{i}", [128, NKT * 256], BF16) for i in range(2)]
    xBC_sb = SB("xBC_sb", [128, NCT * 259], BF16)
    convacc = SB("convacc", [128, 512], F32)
    xT_sb = SB("xT_sb", [128, 2 * 2048], BF16)
    xw_sb = SB("xw_sb", [128, 2 * 2048], BF16)
    usq_sb = xw_sb
    Btok_sb = SB("Btok_sb", [128, 2 * 128], BF16)
    wdtT_sb = SB("wdtT_sb", [128, 2 * 32], F32)
    h_sb = SB("h_sb", [128, NHC * HP], F32)
    hT_bf = SB("hT_bf", [128, NHC * HP], BF16)
    smalls = SB("smalls", [32, 2816], F32)
    flats = SB("flats", [128, 4096], F32)
    u_sb = SB("u_sb", [128, NKT * 256], F32)
    ind_sb = SB("ind_sb", [32, 2048], BF16)
    cdr_sb = SB("cdr_sb", [32, 512], F32)

    sm = {}
    off = 0
    for nm, w in [("dtraw", 256), ("dt", 256), ("cs", 256), ("negcs", 256),
                  ("expf", 256), ("decay", 256), ("wdt", 256), ("lndt", 256),
                  ("tmp1", 256), ("tmp2", 256), ("cd", 1), ("cse", 1)]:
        sm[nm] = off
        off += w

    def SM(name, w=256):
        return smalls.ap()[:, sm[name]:sm[name] + w]

    tap_xn = mkdbg("xn0", [128, NKT * 256], BF16)
    tap_xbc = mkdbg("xbc0", [128, NCT * 256], BF16)
    tap_dt = mkdbg("dt0", [32, 256])
    tap_cs = mkdbg("cs0", [32, 256])
    tap_st = mkdbg("st0", [128, 384], BF16)
    tap_h = mkdbg("h0", [128, NHC * HP])
    tap_yp = mkdbg("yp0", [128, 16 * 256], BF16)

    P.wait("vector", "dGp", P.val("dGp"))
    P.wait("vector", "dIn", cload)
    P.wait("gpsimd", "dIn", cload)
    P.cop("vector", lambda q: q.memset(h_sb.ap(), 0.0))
    for t in range(NCT):
        P.cop("vector", lambda q, t=t: q.memset(xBC_sb.ap()[:, 259 * t:259 * t + 3], 0.0), serial=False)
    P.cop("vector", lambda q: q.memset(flats.ap()[0:1, 0:2048], 1.0), serial=False)
    ab_init_v = P.cop("vector", lambda q: q.memset(flats.ap()[0:3, 2048:4096], 1.0), serial=False)
    P.wait("sync", "dIn", cload)
    P.dma("sync", flats.ap()[32:34, 0:128], psel_t.ap(), "dIn")
    P.dma("sync", ind_sb.ap(), ind_t.ap(), "dIn")
    P.wait("tensor", "dIn", P.val("dIn"))

    with nc.sbuf_tensor("wstage0", [128, WAC], F32) as ws0:
        for kt in range(NKT):
            if kt >= 1:
                P.wait("sync", "vC", P.val("vC"))
            P.dma("sync", ws0.ap(), WA_d[128 * kt:128 * kt + 128, :], "dWA")
            P.wait("vector", "dWA", P.val("dWA"))
            P.cop("vector", lambda q, kt=kt: q.tensor_copy(
                WA_sb.ap()[:, WAC * kt: WAC * kt + WAC], ws0.ap()), serial=False)
    wa_ready_v = P.val("vC")

    e_sb = SB("e_sb", [128, 256], F32)
    ycomb = SB("ycomb", [128, 256], F32)
    ypre_sb = [SB(f"ypre{i}", [128, 16 * 256], BF16) for i in range(2)]
    ST4 = [[SB(f"ST0{i}", [128, 384], BF16) for i in range(2)], None]
    CBTm_sb = SB("CBTm", [128, 384], BF16)

    class _APW:
        def __init__(self, a):
            self._a = a

        def ap(self):
            return self._a

    _cab = convacc.ap().bitcast(BF16)
    ST4[1] = [_APW(_cab[:, 0:384]), _APW(_cab[:, 384:768])]

    uTf3 = uTf.rearrange("(kt p) T -> p kt T", p=128)
    last_xn_v = 0
    prev_te_done = 0
    prev_ych_v = 0
    exp_v = {}
    stst_v = {}
    y_mark = {}
    eb_v = {}
    ycomb_v = {}
    seg_mark = {}

    if stop_after == "s_init":
        chunk_range = []
    else:
        chunk_range = list(range(NCHUNK))
    for c in chunk_range:
        xb = c % 2
        half = c // 8
        yb = c % 2

        P.wait("sync", "vC", last_xn_v if c > 0 else wa_ready_v)
        P.dma("sync", u_sb.ap().rearrange("p (kt T) -> p kt T", T=256),
              uTf3[:, :, CH * c: CH * c + CH], "dU")
        P.wait("vector", "dU", P.val("dU"))
        P.wait("vector", "tC", prev_te_done)
        ubf_v = P.cop("vector", lambda q, xb=xb, u=u_sb: q.tensor_copy(xn_sb[xb].ap(), u.ap()))
        P.wait("scalar", "vC", ubf_v)
        P.wait("scalar", "tC", prev_te_done)
        usq_v = P.cop("scalar", lambda q, xb=xb: q.activation(usq_sb.ap(), xn_sb[xb].ap(), AF.Square))
        if c == 0 and stop_after == "s_p0a":
            break
        P.wait("tensor", "vC", ubf_v)
        P.wait("tensor", "sC", usq_v)
        P.wait("tensor", "vC", P.val("vC"))
        for kt in range(NKT):
            P.top(lambda q, kt=kt, xb=xb: q.matmul(
                pb[6].ap()[0:1, 0:256], ones_cb.ap(),
                xn_sb[xb].ap()[:, 256 * kt:256 * kt + 256],
                start=(kt == 0), stop=(kt == NKT - 1)))
        for kt in range(NKT):
            sums_mm = P.top(lambda q, kt=kt: q.matmul(
                pb[6].ap()[0:1, 256:512], ones_cb.ap(),
                usq_sb.ap()[:, 256 * kt:256 * kt + 256],
                start=(kt == 0), stop=(kt == NKT - 1)), mark=(kt == NKT - 1))
        if c == 0 and stop_after == "s_p0b":
            break
        P.wait("vector", "tC", sums_mm)
        P.cop("vector", lambda q: q.tensor_scalar_mul(row1.ap()[0:1, 0:256], pb[6].ap()[0:1, 0:256], 1.0 / D))
        P.cop("vector", lambda q: q.tensor_scalar_mul(row2.ap()[0:1, 0:256], pb[6].ap()[0:1, 256:512], 1.0 / D))
        P.cop("vector", lambda q: q.tensor_mul(row3.ap()[0:1, 0:256], row1.ap()[0:1, 0:256], row1.ap()[0:1, 0:256]))
        stats_v = P.cop("vector", lambda q: q.tensor_sub(row2.ap()[0:1, 0:256], row2.ap()[0:1, 0:256], row3.ap()[0:1, 0:256]))
        P.wait("scalar", "vC", stats_v)
        sq_v = P.cop("scalar", lambda q: q.activation(row3.ap()[0:1, 0:256], row2.ap()[0:1, 0:256], AF.Sqrt, bias=LN_EPS))
        P.wait("vector", "sC", sq_v)
        rstd_v = P.cop("vector", lambda q: q.reciprocal(row3.ap()[0:1, 0:256], row3.ap()[0:1, 0:256]))
        if c == 0 and stop_after == "s_p0c":
            break
        P.wait("tensor", "vC", rstd_v)
        P.wait("tensor", "sC", P.val("sC"))
        P.top(lambda q: q.matmul(pb[5].ap()[:, 0:256], ones_rf.ap(), row1.ap()[0:1, 0:256], start=True, stop=True))
        bc_mm = P.top(lambda q: q.matmul(pb[5].ap()[:, 256:512], ones_rf.ap(), row3.ap()[0:1, 0:256], start=True, stop=True), mark=True)
        P.wait("vector", "tC", bc_mm)
        if c == 0 and stop_after == "s_p0d":
            break
        for kt in range(NKT):
            P.cop("vector", lambda q, kt=kt, u=u_sb: q.tensor_sub(
                u.ap()[:, 256 * kt:256 * kt + 256], u.ap()[:, 256 * kt:256 * kt + 256],
                pb[5].ap()[:, 0:256]), serial=(kt == 0))
        if c == 0 and stop_after == "s_p0e":
            break
        for kt in range(NKT):
            xn_v = P.cop("vector", lambda q, kt=kt, xb=xb, u=u_sb: q.tensor_tensor(
                xn_sb[xb].ap()[:, 256 * kt:256 * kt + 256], u.ap()[:, 256 * kt:256 * kt + 256],
                pb[5].ap()[:, 256:512], OP.mult), serial=(kt == 0))
        last_xn_v = xn_v
        if c == 0 and tap_xn is not None:
            P.wait("sync", "vC", xn_v)
            P.dma("sync", tap_xn, xn_sb[0].ap(), "dTap")
            P.wait("sync", "dTap", P.val("dTap"))
            P.wait("vector", "dTap", P.val("dTap"))
            P.wait("tensor", "dTap", P.val("dTap"))


        P.wait("tensor", "vC", xn_v)
        if c == 0:
            P.wait("tensor", "vC", ab_init_v)
        evac_vals = []
        for mt in range(NCT + 1):
            bank = pb[mt % 2]
            mw = 128 if mt < NCT else 32
            if len(evac_vals) >= 2:
                P.wait("tensor", "sC", evac_vals[-2])
            for kt in range(NKT):
                mm_v = P.top(lambda q, mt=mt, kt=kt, bank=bank, mw=mw, xb=xb: q.matmul(
                    bank.ap()[0:mw, 0:256],
                    WA_sb.ap()[:, WAC * kt + 128 * mt: WAC * kt + 128 * mt + mw],
                    xn_sb[xb].ap()[:, 256 * kt:256 * kt + 256],
                    start=(kt == 0), stop=(kt == NKT - 1)), mark=(kt == NKT - 1))
            P.wait("scalar", "tC", mm_v)
            if mt == 0:
                P.wait("scalar", "tC", prev_te_done)
                P.wait("scalar", "vC", prev_ych_v)
            if mt < NCT:
                ev = P.cop("scalar", lambda q, mt=mt, bank=bank: q.copy(
                    xBC_sb.ap()[:, 259 * mt + 3:259 * mt + 259], bank.ap()[:, 0:256]))
            else:
                ev = P.cop("scalar", lambda q, bank=bank: q.copy(
                    SM("dtraw"), bank.ap()[0:32, 0:256]))
            evac_vals.append(ev)
        inproj_evac = evac_vals[-1]

        if c == 0 and stop_after == "s_inproj":
            break
        silu_vals = []
        for t in range(NCT):
            ca = convacc.ap()[:, 256 * (t % 2):256 * (t % 2) + 256]
            P.wait("vector", "sC", evac_vals[t])
            if t >= 2:
                P.wait("vector", "sC", silu_vals[t - 2])
            P.cop("vector", lambda q, t=t, ca=ca: q.tensor_scalar(
                ca, xBC_sb.ap()[:, 259 * t + 0:259 * t + 256],
                convw_sb.ap()[:, 4 * t + 0:4 * t + 1], None, OP.mult))
            for k in (1, 2, 3):
                P.cop("vector", lambda q, t=t, k=k, ca=ca: q.scalar_tensor_tensor(
                    ca, xBC_sb.ap()[:, 259 * t + k:259 * t + k + 256],
                    convw_sb.ap()[:, 4 * t + k:4 * t + k + 1], ca, OP.mult, OP.add))
            P.cop("vector", lambda q, t=t: q.tensor_copy(
                xBC_sb.ap()[:, 259 * t + 0:259 * t + 3], xBC_sb.ap()[:, 259 * t + 256:259 * t + 259]))
            vv = P.val("vC")
            P.wait("scalar", "vC", vv)
            sv = P.cop("scalar", lambda q, t=t, ca=ca: q.activation(
                xBC_sb.ap()[:, 259 * t + 3:259 * t + 259], ca,
                AF.Silu, bias=convb_sb.ap()[:, t:t + 1]))
            silu_vals.append(sv)
        conv_done = P.val("sC")
        if c == 0 and tap_xbc is not None:
            P.wait("sync", "sC", conv_done)
            P.dma("sync", tap_xbc.rearrange("p (t T) -> p t T", T=256),
                  xBC_sb.ap().rearrange("p (t T) -> p t T", T=259)[:, :, 3:259], "dTap")
            P.wait("sync", "dTap", P.val("dTap"))
            P.wait("scalar", "dTap", P.val("dTap"))
            P.wait("vector", "dTap", P.val("dTap"))

        if c == 0 and stop_after == "s_conv":
            break
        P.wait("vector", "sC", inproj_evac)
        P.cop("vector", lambda q: q.tensor_scalar(SM("tmp1"), SM("dtraw"), dtb_sb.ap()[0:32, 0:1], None, OP.add))
        P.cop("vector", lambda q: q.tensor_scalar(SM("tmp2"), SM("tmp1"), 0.0, None, OP.max))
        relu_v = P.val("vC")
        P.wait("scalar", "vC", relu_v)
        P.cop("scalar", lambda q: q.activation(SM("dt"), SM("tmp1"), AF.Abs))
        P.cop("scalar", lambda q: q.activation(SM("dt"), SM("dt"), AF.Exp, scale=-1.0))
        ln_v = P.cop("scalar", lambda q: q.activation(SM("dt"), SM("dt"), AF.Ln, bias=1.0))
        P.wait("vector", "sC", ln_v)
        dt_v = P.cop("vector", lambda q: q.tensor_add(SM("dt"), SM("dt"), SM("tmp2")))
        P.wait("scalar", "vC", dt_v)
        P.cop("scalar", lambda q: q.activation(SM("lndt"), SM("dt"), AF.Ln))
        P.cop("vector", lambda q: q.tensor_scalar(SM("tmp1"), SM("dt"), nA_sb.ap()[0:32, 0:1], None, OP.mult))
        P.cop("vector", lambda q: q.tensor_tensor_scan(SM("cs"), SM("tmp1"), SM("tmp1"), 0.0, OP.add, OP.bypass))
        P.cop("vector", lambda q: q.tensor_scalar_mul(SM("negcs"), SM("cs"), -1.0))
        cse_v = P.cop("vector", lambda q: q.tensor_copy(SM("cse", 1), smalls.ap()[:, sm["cs"] + 255:sm["cs"] + 256]))
        P.wait("scalar", "vC", cse_v)
        P.cop("scalar", lambda q: q.activation(SM("expf"), SM("cs"), AF.Exp))
        P.cop("scalar", lambda q: q.activation(SM("decay"), SM("cs"), AF.Exp, scale=-1.0, bias=SM("cse", 1)))
        dexp_v = P.cop("scalar", lambda q: q.activation(SM("cd", 1), SM("cse", 1), AF.Exp))
        P.wait("vector", "sC", dexp_v)
        dtpath_v = P.cop("vector", lambda q: q.tensor_mul(SM("wdt"), SM("decay"), SM("dt")))
        if c == 0 and tap_dt is not None:
            P.wait("sync", "vC", dtpath_v)
            P.dma("sync", tap_dt, SM("dt"), "dTap")
            P.wait("sync", "dTap", P.val("dTap"))
            P.dma("sync", tap_cs, SM("cs"), "dTap")
            P.wait("sync", "dTap", P.val("dTap"))
            P.wait("scalar", "dTap", P.val("dTap"))
            P.wait("vector", "dTap", P.val("dTap"))

        if c == 0 and stop_after == "s_dt":
            break
        P.wait("tensor", "vC", dtpath_v)
        for jt in range(2):
            wdtT_mm = P.top(lambda q, jt=jt: q.matmul(
                pb[2].ap()[0:128, 32 * jt:32 * jt + 32], SM("wdt")[0:32, 128 * jt:128 * jt + 128],
                id_f.ap()[0:32, 0:32], start=True, stop=True, is_transpose=True), mark=(jt == 1))
        P.wait("vector", "tC", wdtT_mm)
        wdtT_v = P.cop("vector", lambda q: q.tensor_copy(wdtT_sb.ap()[:, 0:64], pb[2].ap()[:, 0:64]))
        if c == 0 and stop_after == "s_tr1":
            break
        P.wait("tensor", "sC", conv_done)
        tp_banks = [pb[7].ap().bitcast(BF16)[:, 0:128], pb[3].ap().bitcast(BF16)[:, 0:128]]
        tevac = []
        for t in range(17):
            for jt in range(2):
                tpos = 2 * t + jt
                bk = tp_banks[tpos % 2]
                if tpos >= 2:
                    P.wait("tensor", "vC", tevac[tpos - 2])
                tr_mm = P.top(lambda q, t=t, jt=jt, bk=bk: q.matmul(
                    bk, xBC_sb.ap()[:, 259 * t + 3 + 128 * jt: 259 * t + 3 + 128 * jt + 128],
                    id_bf.ap(), start=True, stop=True, is_transpose=True), mark=True)
                P.wait("vector", "tC", tr_mm)
                if t < 16:
                    ev = P.cop("vector", lambda q, t=t, jt=jt, bk=bk: q.tensor_copy(
                        xT_sb.ap()[:, 2048 * jt + 128 * t: 2048 * jt + 128 * t + 128],
                        bk), serial=False)
                else:
                    ev = P.cop("vector", lambda q, jt=jt, bk=bk: q.tensor_copy(
                        Btok_sb.ap()[:, 128 * jt:128 * jt + 128],
                        bk), serial=False)
                tevac.append(ev)
                if c == 0 and stop_after == "s_tr2" and t == 0:
                    break
            if c == 0 and stop_after == "s_tr2" and t == 0:
                break
        if c == 0 and stop_after == "s_tr2":
            break
        P.wait("vector", "vC", wdtT_v)
        for jt in range(2):
            for h in range(NHC):
                xw_v = P.cop("vector", lambda q, jt=jt, h=h: q.tensor_scalar(
                    xw_sb.ap()[:, 2048 * jt + 64 * h: 2048 * jt + 64 * h + 64],
                    xT_sb.ap()[:, 2048 * jt + 64 * h: 2048 * jt + 64 * h + 64],
                    wdtT_sb.ap()[:, 32 * jt + h:32 * jt + h + 1], None, OP.mult),
                    serial=(jt == 0 and h == 0))

        if c == 0 and stop_after == "s_transp":
            break
        Bcm = xBC_sb.ap()[:, 259 * 16 + 3: 259 * 16 + 259]
        Ccm = xBC_sb.ap()[:, 259 * 17 + 3: 259 * 17 + 259]
        P.wait("tensor", "vC", P.val("vC"))
        P.top(lambda q: q.matmul(pb[2].ap()[:, 0:256], Bcm[:, 0:128], Ccm, start=True, stop=True))
        cbt_mm = P.top(lambda q: q.matmul(pb[3].ap()[:, 0:128], Bcm[:, 128:256], Ccm[:, 128:256], start=True, stop=True), mark=True)
        P.wait("vector", "tC", cbt_mm)
        P.cop("vector", lambda q: q.tensor_tensor(CBTm_sb.ap()[:, 0:256], pb[2].ap()[:, 0:256], mask_sb.ap()[:, 0:256], OP.mult))
        cbt_v = P.cop("vector", lambda q: q.tensor_tensor(CBTm_sb.ap()[:, 256:384], pb[3].ap()[:, 0:128], mask_sb.ap()[:, 0:128], OP.mult))

        hTb_v = P.cop("vector", lambda q: q.tensor_copy(hT_bf.ap(), h_sb.ap()))

        if c == 0 and stop_after == "s_cbt":
            break
        for qtr in range(4):
            P.wait("sync", "vC", dtpath_v)
            P.wait("sync", "tC", P.val("tC"))
            P.dma("sync", flats.ap()[1:2, 0:2048], smalls.ap()[8 * qtr:8 * qtr + 8, sm["negcs"]:sm["negcs"] + 256], "dFl")
            P.dma("sync", flats.ap()[2:3, 0:2048], smalls.ap()[8 * qtr:8 * qtr + 8, sm["lndt"]:sm["lndt"] + 256], "dFl")
            P.dma("sync", flats.ap()[0:1, 2048:4096], smalls.ap()[8 * qtr:8 * qtr + 8, sm["cs"]:sm["cs"] + 256], "dFl")
            P.dma("sync", flats.ap()[32:33, 2048:4096], smalls.ap()[8 * qtr:8 * qtr + 8, sm["expf"]:sm["expf"] + 256], "dFl")
            if qtr < 3:
                P.dma("sync", flats.ap()[33:34, 2048:4096],
                      smalls.ap()[8 * qtr + 1:8 * qtr + 9, sm["expf"]:sm["expf"] + 256], "dFl")
            else:
                P.dma("sync", flats.ap()[33:34, 2048:3840],
                      smalls.ap()[8 * qtr + 1:8 * qtr + 8, sm["expf"]:sm["expf"] + 256], "dFl")
            fl_v = P.val("dFl")
            P.wait("tensor", "dFl", fl_v)

            P.wait("vector", "sC", dexp_v)
            cdr_v = P.cop("vector", lambda q, qtr=qtr: q.tensor_scalar(
                cdr_sb.ap(), ind_sb.ap()[:, 512 * qtr:512 * qtr + 512], SM("cd", 1), None, OP.mult))
            P.wait("tensor", "vC", cdr_v)
            cdb_mm = P.top(lambda q: q.matmul(
                pb[6].ap()[:, 0:512], ones32_sb.ap(), cdr_sb.ap(), start=True, stop=True), mark=True)
            P.wait("vector", "tC", cdb_mm)
            P.wait("vector", "vC", hTb_v)
            hsc_v = P.cop("vector", lambda q, qtr=qtr: q.tensor_tensor(
                h_sb.ap()[:, 512 * qtr:512 * qtr + 512], h_sb.ap()[:, 512 * qtr:512 * qtr + 512],
                pb[6].ap()[:, 0:512], OP.mult))
            P.wait("tensor", "vC", hsc_v)
            for jt in range(2):
                st_mm = P.top(lambda q, jt=jt, qtr=qtr: q.matmul(
                    pb[6].ap()[:, 0:512], Btok_sb.ap()[:, 128 * jt:128 * jt + 128],
                    xw_sb.ap()[:, 2048 * jt + 512 * qtr: 2048 * jt + 512 * qtr + 512],
                    start=(jt == 0), stop=(jt == 1)), mark=(jt == 1))
            P.wait("vector", "tC", st_mm)
            P.cop("vector", lambda q, qtr=qtr: q.tensor_tensor(
                h_sb.ap()[:, 512 * qtr:512 * qtr + 512], h_sb.ap()[:, 512 * qtr:512 * qtr + 512],
                pb[6].ap()[:, 0:512], OP.add))

            def emit_seg(g):
                gidx = 16 * c + g
                par = g % 2
                segb = [pb[2], pb[3]] if par == 0 else [pb[0], pb[1]]
                j0 = 2 * (g % 4)
                P.wait("tensor", "sC", exp_v.get(gidx - 2, 0))
                if g == 0:
                    P.wait("tensor", "vC", cbt_v)
                if g == 1:
                    P.wait("tensor", "sC", inproj_evac)
                for hi in range(2):
                    col = 256 * (j0 + hi)
                    bankS = segb[hi]
                    P.top(lambda q, col=col, bankS=bankS: q.matmul(
                        bankS.ap()[:, 0:256], flats.ap()[0:3, col:col + 128],
                        flats.ap()[0:3, 2048 + col:2048 + col + 256], start=True, stop=True))
                    smm = P.top(lambda q, col=col, bankS=bankS: q.matmul(
                        bankS.ap()[:, 256:384], flats.ap()[0:3, col + 128:col + 256],
                        flats.ap()[0:3, 2048 + col + 128:2048 + col + 256], start=True, stop=True), mark=True)
                seg_mark[gidx] = smm
                P.wait("scalar", "tC", smm)
                for hi in range(2):
                    bankS = segb[hi]
                    P.cop("scalar", lambda q, hi=hi, bankS=bankS, par=par: q.activation(
                        ST4[par][hi].ap()[:, 0:256], bankS.ap()[:, 0:256], AF.Exp))
                    ev = P.cop("scalar", lambda q, hi=hi, bankS=bankS, par=par: q.activation(
                        ST4[par][hi].ap()[:, 256:384], bankS.ap()[:, 256:384], AF.Exp))
                exp_v[gidx] = ev
                P.wait("vector", "sC", ev)
                P.wait("vector", "vC", cbt_v)
                for hi in range(2):
                    sv = P.cop("vector", lambda q, hi=hi, par=par: q.scalar_tensor_tensor(
                        ST4[par][hi].ap()[:, 0:384], ST4[par][hi].ap()[:, 0:384], 1e30,
                        CBTm_sb.ap()[:, 0:384], OP.min, OP.mult), serial=(hi == 0))
                stst_v[gidx] = sv
                if c == 0 and g == 0 and tap_st is not None:
                    P.wait("sync", "vC", sv)
                    P.dma("sync", tap_st, ST4[0][0].ap(), "dTap")
                    P.wait("sync", "dTap", P.val("dTap"))
                    P.wait("scalar", "dTap", P.val("dTap"))
                    P.wait("vector", "dTap", P.val("dTap"))

            def emit_y(g):
                gidx = 16 * c + g
                par = g % 2
                ybank = pb[4] if par == 0 else pb[6]
                ebank = pb[5] if par == 0 else pb[7]
                j0 = 2 * (g % 4)
                P.wait("tensor", "vC", stst_v[gidx])
                P.wait("tensor", "sC", eb_v.get(gidx - 2, 0))
                if g <= 1:
                    P.wait("tensor", "sC", conv_done)
                    P.wait("tensor", "vC", hTb_v)
                P.top(lambda q, g=g, ybank=ybank: q.matmul(
                    ybank.ap()[:, 256:512], hT_bf.ap()[:, 128 * g:128 * g + 128], Ccm,
                    start=True, stop=True))
                P.top(lambda q, j0=j0, ebank=ebank: q.matmul(
                    ebank.ap()[:, 256:512], flats.ap()[32:34, 0:128],
                    flats.ap()[32:34, 2048 + 256 * j0:2048 + 256 * j0 + 256], start=True, stop=True))
                for hi in range(2):
                    P.top(lambda q, hi=hi, g=g, par=par, ybank=ybank: q.matmul(
                        ybank.ap()[64 * hi:64 * hi + 64, 0:256],
                        xT_sb.ap()[:, 64 * (2 * g + hi): 64 * (2 * g + hi) + 64],
                        ST4[par][hi].ap()[:, 0:256],
                        start=True, stop=False, tile_position=(0, 64 * hi),
                        skip_group_check=True))
                for hi in range(2):
                    ym = P.top(lambda q, hi=hi, g=g, par=par, ybank=ybank: q.matmul(
                        ybank.ap()[64 * hi:64 * hi + 64, 128:256],
                        xT_sb.ap()[:, 2048 + 64 * (2 * g + hi): 2048 + 64 * (2 * g + hi) + 64],
                        ST4[par][hi].ap()[:, 256:384],
                        start=False, stop=True, tile_position=(0, 64 * hi),
                        skip_group_check=True), mark=(hi == 1))
                y_mark[gidx] = ym
                P.wait("scalar", "tC", ym)
                P.wait("scalar", "vC", ycomb_v.get(gidx - 1, 0))
                ebv = P.cop("scalar", lambda q, ebank=ebank: q.activation(
                    e_sb.ap(), ebank.ap()[:, 256:512], AF.Copy))
                eb_v[gidx] = ebv
                P.wait("vector", "sC", ebv)
                if g == 0:
                    P.wait("vector", "dStF", P.val("dStF"))
                ycomb_v[gidx] = P.cop("vector", lambda q, ybank=ybank: q.tensor_tensor(
                    ycomb.ap(), ybank.ap()[:, 256:512], e_sb.ap(), OP.mult))
                P.cop("vector", lambda q, ybank=ybank: q.tensor_tensor(
                    ycomb.ap(), ycomb.ap(), ybank.ap()[:, 0:256], OP.add))
                yv = P.cop("vector", lambda q, g=g, yb=yb: q.scalar_tensor_tensor(
                    ypre_sb[yb].ap()[:, 256 * g:256 * g + 256],
                    xBC_sb.ap()[:, 259 * g + 3:259 * g + 259],
                    dpB_sb.ap()[:, g:g + 1], ycomb.ap(), OP.mult, OP.add))
                return yv

            seq = [(0, "s"), (1, "s"), (0, "y"), (2, "s"), (1, "y"), (3, "s"), (2, "y"), (3, "y")]
            for pp, kind in seq:
                g = 4 * qtr + pp
                if kind == "s":
                    emit_seg(g)
                else:
                    ych_v = emit_y(g)
        if c == 0 and tap_yp is not None:
            P.wait("sync", "vC", ych_v)
            P.dma("sync", tap_yp, ypre_sb[0].ap(), "dTap")
            P.wait("sync", "dTap", P.val("dTap"))
            P.wait("scalar", "dTap", P.val("dTap"))
            P.wait("vector", "dTap", P.val("dTap"))
        if c == 0 and tap_h is not None:
            P.wait("sync", "vC", P.val("vC"))
            P.dma("sync", tap_h, h_sb.ap(), "dTap")
            P.wait("sync", "dTap", P.val("dTap"))

        P.wait("scalar", "vC", ych_v)
        P.dma("scalar", S_d.ap()[D * half: D * half + D, 256 * (c % 8):256 * (c % 8) + 256]
              .rearrange("(g p) T -> p g T", p=128),
              ypre_sb[yb].ap().rearrange("p (g T) -> p g T", T=256), "dStF")

        prev_te_done = P.val("tC")
        prev_ych_v = ych_v
        if stop_after == f"chunk{c}":
            break

    st_ab.close()
    ab_stash_dma = P.val("dStF")

    ag_v = 0
    if stop_after is None:
        P.wait("gpsimd", "dStF", ab_stash_dma)
        ag_v = P.cop("gpsimd", lambda q: q.collective_compute(
            "AllGather", OP.bypass, replica_groups=[[0, 1], [2, 3], [4, 5], [6, 7]],
            ins=[S_d.ap()], outs=[G_d.ap()]))

    if stop_after is not None:
        zo = nc.alloc_sbuf_tensor("zo", [128, 512], F32)
        zv = P.cop("vector", lambda q: q.memset(zo.ap(), 0.0))
        P.wait("scalar", "vC", zv)
        for mt in range(16):
            for tb in range(4):
                P.dma("scalar", out_d[128 * mt:128 * mt + 128, 512 * tb:512 * tb + 512], zo.ap(), "dOut")
        P.wait("scalar", "dOut", P.val("dOut"))
        P.wait("scalar", "dTap", P.val("dTap"))
        P.wait("scalar", "dStF", P.val("dStF"))
    else:
        def fence():
            tv, vv, sv = P.val("tC"), P.val("vC"), P.val("sC")
            for e in ENGS:
                P.wait(e, "tC", tv)
                P.wait(e, "vC", vv)
                P.wait(e, "sC", sv)

        uTh3 = uTh.rearrange("(kt p) T -> p kt T", p=128)

        def ln_block(cb, dst_sb, dst_col0, rms_adjust, last_u_free, usb, ubf, usq):
            P.wait("sync", "vC", last_u_free)
            P.dma("sync", usb.ap().rearrange("p (kt T) -> p kt T", T=256),
                  uTh3[:, :, 256 * cb: 256 * cb + 256], "dU")
            P.wait("vector", "dU", P.val("dU"))
            ub_v = P.cop("vector", lambda q: q.tensor_copy(ubf.ap(), usb.ap()))
            P.wait("scalar", "vC", ub_v)
            us_v = P.cop("scalar", lambda q: q.activation(usq.ap(), ubf.ap(), AF.Square))
            P.wait("tensor", "vC", ub_v)
            P.wait("tensor", "sC", us_v)
            P.wait("tensor", "vC", P.val("vC"))
            for kt in range(NKT):
                P.top(lambda q, kt=kt: q.matmul(
                    pb[6].ap()[0:1, 0:256], ones_cb.ap(),
                    ubf.ap()[:, 256 * kt:256 * kt + 256],
                    start=(kt == 0), stop=(kt == NKT - 1)))
            for kt in range(NKT):
                smm = P.top(lambda q, kt=kt: q.matmul(
                    pb[6].ap()[0:1, 256:512], ones_cb.ap(),
                    usq.ap()[:, 256 * kt:256 * kt + 256],
                    start=(kt == 0), stop=(kt == NKT - 1)), mark=(kt == NKT - 1))
            P.wait("vector", "tC", smm)
            P.cop("vector", lambda q: q.tensor_scalar_mul(row1.ap()[0:1, 0:256], pb[6].ap()[0:1, 0:256], 1.0 / D))
            P.cop("vector", lambda q: q.tensor_scalar_mul(row2.ap()[0:1, 0:256], pb[6].ap()[0:1, 256:512], 1.0 / D))
            P.cop("vector", lambda q: q.tensor_mul(row3.ap()[0:1, 0:256], row1.ap()[0:1, 0:256], row1.ap()[0:1, 0:256]))
            sv2 = P.cop("vector", lambda q: q.tensor_sub(row2.ap()[0:1, 0:256], row2.ap()[0:1, 0:256], row3.ap()[0:1, 0:256]))
            if rms_adjust:
                sv2 = P.cop("vector", lambda q: q.tensor_scalar(
                    row2.ap()[0:1, 0:256], row2.ap()[0:1, 0:256],
                    1.0 + FRMS_EPS, FRMS_EPS * LN_EPS, OP.mult, OP.add))
                bias = 0.0
            else:
                bias = LN_EPS
            P.wait("scalar", "vC", sv2)
            sq2 = P.cop("scalar", lambda q, bias=bias: q.activation(
                row3.ap()[0:1, 0:256], row2.ap()[0:1, 0:256], AF.Sqrt, bias=bias))
            P.wait("vector", "sC", sq2)
            rs_v = P.cop("vector", lambda q: q.reciprocal(row3.ap()[0:1, 0:256], row3.ap()[0:1, 0:256]))
            P.wait("tensor", "vC", rs_v)
            P.wait("tensor", "sC", P.val("sC"))
            P.top(lambda q: q.matmul(pb[5].ap()[:, 0:256], ones_rf.ap(), row1.ap()[0:1, 0:256], start=True, stop=True))
            bmm = P.top(lambda q: q.matmul(pb[5].ap()[:, 256:512], ones_rf.ap(), row3.ap()[0:1, 0:256], start=True, stop=True), mark=True)
            P.wait("vector", "tC", bmm)
            for kt in range(NKT):
                P.cop("vector", lambda q, kt=kt: q.tensor_sub(
                    usb.ap()[:, 256 * kt:256 * kt + 256], usb.ap()[:, 256 * kt:256 * kt + 256],
                    pb[5].ap()[:, 0:256]), serial=(kt == 0))
            for kt in range(NKT):
                outc = dst_col0 + kt * TOK + 256 * cb
                if rms_adjust:
                    hv = P.cop("vector", lambda q, kt=kt, outc=outc: q.scalar_tensor_tensor(
                        dst_sb.ap()[:, outc:outc + 256], usb.ap()[:, 256 * kt:256 * kt + 256],
                        rmsw_sb.ap()[:, kt:kt + 1], pb[5].ap()[:, 256:512], OP.mult, OP.mult),
                        serial=(kt == 0))
                else:
                    hv = P.cop("vector", lambda q, kt=kt, outc=outc: q.tensor_tensor(
                        dst_sb.ap()[:, outc:outc + 256], usb.ap()[:, 256 * kt:256 * kt + 256],
                        pb[5].ap()[:, 256:512], OP.mult), serial=(kt == 0))
            return hv

        class WStream:

            def __init__(self, name, width, sem, scale_cols=None, nkt=16, queue="sync",
                         nbuf=2):
                self.stage = SBE(f"{name}_st", [128, width], F32)
                self.out = [SBE(f"{name}_o{i}", [128, width], BF16) for i in range(nbuf)]
                self.sem = sem
                self.queue = queue
                self.nbuf = nbuf
                self.n = 0
                self.cast_v = [0] * nbuf

            def load(self, src_ap, cons_tC=None):
                dst = self.stage.ap()
                if src_ap.ndim == 3:
                    dst = dst.rearrange("p (a b) -> p a b", b=src_ap.shape[2])
                i = self.n % self.nbuf
                if self.n >= 1:
                    P.wait(self.queue, "sC", self.cast_v[(self.n - 1) % self.nbuf])
                P.dma(self.queue, dst, src_ap, self.sem)
                dv = P.val(self.sem)
                P.wait("scalar", self.sem, dv)
                P.wait("scalar", "tC", P.val("tC") if cons_tC is None else cons_tC)
                cv = P.cop("scalar", lambda q, i=i: q.activation(
                    self.out[i].ap(), self.stage.ap(), AF.Copy))
                self.cast_v[i] = cv
                self.n += 1
                return self.out[i], cv

        fence()
        st_e = contextlib.ExitStack()
        SBE = lambda *a: st_e.enter_context(nc.sbuf_tensor(*a))
        h_ffn = SBE("h_ffn", [128, NKT * TOK], BF16)
        dacc = SBE("dacc", [128, 16 * 1024], F32)
        din = SBE("din", [128, 8 * 1024], BF16)
        lg_sb = SBE("lg_sb", [8, TOK], F32)

        st_p0b = contextlib.ExitStack()
        SBP = lambda *a: st_p0b.enter_context(nc.sbuf_tensor(*a))
        u_e = SBP("u_sbE", [128, NKT * 256], F32)
        ubf_e2 = SBP("ubf_e", [128, NKT * 256], BF16)
        usq_e2 = SBP("usq_e", [128, NKT * 256], BF16)
        lgt_sb = SBP("lgt_sb", [128, 16 * 8], F32)
        wn_sb = SBP("wn_sb", [128, 16 * 8], F32)
        rt1 = SBP("rt1", [128, 1], F32)
        rt2 = SBP("rt2", [128, 1], F32)
        rwb_sb = SBP("rwb_sb", [128, 128], BF16)

        hv = 0
        for cb in range(8):
            hv = ln_block(cb, h_ffn, 0, True, hv, u_e, ubf_e2, usq_e2)
        hffn_v = hv

        P.wait("sync", "vC", hv)
        P.dma("sync", u_e.ap()[:, 0:128].rearrange("p (kt e) -> p kt e", e=8),
              rw_d.rearrange("(kt p) e -> p kt e", p=128), "dW1")
        P.wait("vector", "dW1", P.val("dW1"))
        rww_v = P.cop("vector", lambda q, u=u_e: q.tensor_copy(
            rwb_sb.ap(), u.ap()[:, 0:128]))
        P.wait("tensor", "vC", max(rww_v, hffn_v))
        for tb in range(4):
            P.wait("tensor", "sC", P.val("sC"))
            for kt in range(NKT):
                lmm = P.top(lambda q, tb=tb, kt=kt: q.matmul(
                    pb[4].ap()[0:8, 0:512], rwb_sb.ap()[:, 8 * kt:8 * kt + 8],
                    h_ffn.ap()[:, TOK * kt + 512 * tb: TOK * kt + 512 * tb + 512],
                    start=(kt == 0), stop=(kt == NKT - 1)), mark=(kt == NKT - 1))
            P.wait("scalar", "tC", lmm)
            P.cop("scalar", lambda q, tb=tb: q.copy(lg_sb.ap()[:, 512 * tb:512 * tb + 512], pb[4].ap()[0:8, 0:512]))
        lg_v = P.val("sC")
        P.wait("tensor", "sC", lg_v)
        for tt in range(16):
            P.wait("tensor", "vC", P.val("vC"))
            tmm = P.top(lambda q, tt=tt: q.matmul(
                pb[4].ap()[:, 0:8], lg_sb.ap()[0:8, 128 * tt:128 * tt + 128],
                id_f.ap()[0:8, 0:8], start=True, stop=True, is_transpose=True), mark=True)
            P.wait("vector", "tC", tmm)
            P.cop("vector", lambda q, tt=tt: q.tensor_copy(lgt_sb.ap()[:, 8 * tt:8 * tt + 8], pb[4].ap()[:, 0:8]))
        for tt in range(16):
            L = lambda: lgt_sb.ap()[:, 8 * tt:8 * tt + 8]
            P.cop("vector", lambda q, tt=tt: q.reduce_max(rt1.ap(), lgt_sb.ap()[:, 8 * tt:8 * tt + 8], AX.X))
            sub_v = P.cop("vector", lambda q, tt=tt: q.tensor_scalar(
                lgt_sb.ap()[:, 8 * tt:8 * tt + 8], lgt_sb.ap()[:, 8 * tt:8 * tt + 8], rt1.ap()[:, 0:1], None, OP.subtract))
            P.wait("scalar", "vC", sub_v)
            ex_v = P.cop("scalar", lambda q, tt=tt: q.activation(
                lgt_sb.ap()[:, 8 * tt:8 * tt + 8], lgt_sb.ap()[:, 8 * tt:8 * tt + 8], AF.Exp))
            P.wait("vector", "sC", ex_v)
            P.cop("vector", lambda q, tt=tt: q.reduce_sum(rt1.ap(), lgt_sb.ap()[:, 8 * tt:8 * tt + 8], AX.X))
            P.cop("vector", lambda q: q.reciprocal(rt2.ap(), rt1.ap()))
            P.cop("vector", lambda q, tt=tt: q.tensor_scalar(
                lgt_sb.ap()[:, 8 * tt:8 * tt + 8], lgt_sb.ap()[:, 8 * tt:8 * tt + 8], rt2.ap()[:, 0:1], None, OP.mult))
            P.cop("vector", lambda q, tt=tt: q.reduce_max(rt1.ap(), lgt_sb.ap()[:, 8 * tt:8 * tt + 8], AX.X))
            P.cop("vector", lambda q, tt=tt: q.tensor_scalar(
                wn_sb.ap()[:, 8 * tt:8 * tt + 8], lgt_sb.ap()[:, 8 * tt:8 * tt + 8], rt1.ap()[:, 0:1], None, OP.is_ge))
            P.cop("vector", lambda q, tt=tt: q.scalar_tensor_tensor(
                wn_sb.ap()[:, 8 * tt:8 * tt + 8], wn_sb.ap()[:, 8 * tt:8 * tt + 8], -1e9,
                lgt_sb.ap()[:, 8 * tt:8 * tt + 8], OP.mult, OP.add))
            P.cop("vector", lambda q, tt=tt: q.reduce_max(rt2.ap(), wn_sb.ap()[:, 8 * tt:8 * tt + 8], AX.X))
            P.cop("vector", lambda q, tt=tt: q.tensor_scalar(
                wn_sb.ap()[:, 8 * tt:8 * tt + 8], lgt_sb.ap()[:, 8 * tt:8 * tt + 8], rt2.ap()[:, 0:1], None, OP.is_ge))
            P.cop("vector", lambda q, tt=tt: q.tensor_mul(
                wn_sb.ap()[:, 8 * tt:8 * tt + 8], wn_sb.ap()[:, 8 * tt:8 * tt + 8], lgt_sb.ap()[:, 8 * tt:8 * tt + 8]))
            P.cop("vector", lambda q, tt=tt: q.reduce_sum(rt1.ap(), wn_sb.ap()[:, 8 * tt:8 * tt + 8], AX.X))
            P.cop("vector", lambda q: q.reciprocal(rt2.ap(), rt1.ap()))
            wnv = P.cop("vector", lambda q, tt=tt: q.tensor_scalar(
                wn_sb.ap()[:, 8 * tt:8 * tt + 8], wn_sb.ap()[:, 8 * tt:8 * tt + 8], rt2.ap()[:, 0:1], None, OP.mult))
        P.wait("tensor", "vC", wnv)
        for tt in range(16):
            P.wait("tensor", "sC", P.val("sC"))
            tm2 = P.top(lambda q, tt=tt: q.matmul(
                pb[4].ap()[0:8, 0:128], wn_sb.ap()[:, 8 * tt:8 * tt + 8],
                id_f.ap(), start=True, stop=True, is_transpose=True), mark=True)
            P.wait("scalar", "tC", tm2)
            P.cop("scalar", lambda q, tt=tt: q.copy(lg_sb.ap()[:, 128 * tt:128 * tt + 128], pb[4].ap()[0:8, 0:128]))
        wdT_v = P.val("sC")
        fence()
        st_p0b.close()
        wfl = SBE("wfl", [1, 1024], F32)
        wB_sb = SBE("wB_sb", [128, 1024], BF16)
        sgt2 = [SBE(f"sgt{i}", [128, 512], BF16) for i in range(2)]
        t1b2 = [SBE(f"t1b{i}", [128, 512], BF16) for i in range(2)]
        fstg = SBE("fstg", [128, 1024], BF16)

        wg3 = wg_d.rearrange("e (kt p) f -> e p kt f", p=128)
        wu3 = wu_d.rearrange("e (kt p) f -> e p kt f", p=128)
        wd3 = wd_d.rearrange("e (kf p) d -> e p kf d", p=128)
        swg3 = swg_d.rearrange("(kt p) f -> p kt f", p=128)
        swu3 = swu_d.rearrange("(kt p) f -> p kt f", p=128)
        swd3 = swd_d.rearrange("(kf p) d -> p kf d", p=128)

        ws_gu = WStream("wsgu", 2048, "dW2", nbuf=4)
        ws_d = WStream("wsd", 1024, "dW4", nbuf=2)

        def gu_src(e, mt):
            if e == 8:
                return (swg3[:, :, 128 * mt:128 * mt + 128],
                        swu3[:, :, 128 * mt:128 * mt + 128])
            return (wg3[e, :, :, 128 * mt:128 * mt + 128],
                    wu3[e, :, :, 128 * mt:128 * mt + 128])

        def d_src(e, md):
            if e == 8:
                return swd3[:, :, 128 * md:128 * md + 128]
            return wd3[e, :, :, 128 * md:128 * md + 128]

        SETS = [[0, 1, 2, 3], [4, 5, 6, 7]]
        gu_seq = [(hf, e, mt) for hf in range(2) for e in range(9) for mt in range(8)]
        d_seq = [(hf, e, md) for hf in range(2) for e in range(9) for md in range(16)]
        g_mark = {}
        u_mark = {}
        tt_v = {}
        sg_v = {}
        din_g = {}
        dmark = {}
        dadd_v = {}

        def gu_load(i):
            _, e_, mt_ = gu_seq[i]
            gs, us = gu_src(e_, mt_)
            gb = ws_gu.load(gs, cons_tC=g_mark.get(i - 2, 0))
            ub = ws_gu.load(us, cons_tC=u_mark.get(i - 2, 0))
            return gb, ub

        def d_load(j):
            _, e_, md_ = d_seq[j]
            return ws_d.load(d_src(e_, md_), cons_tC=dmark.get(j - 2, 0))

        gu_next = gu_load(0)
        d_next = d_load(0)
        wfl_mark = 0
        sb_gate_free = wdT_v
        din_v = 0
        prev_down_mark = 0
        last_dadd = [0, 0]
        last_tt = 0
        for hf in range(2):
            t0 = 1024 * hf
            for e in range(9):
                shared = (e == 8)
                ebase = 72 * hf + 8 * e
                for mt in range(8):
                    i = ebase + mt
                    (wgb, gcv), (wub, ucv) = gu_next
                    if i + 1 < len(gu_seq):
                        gu_next = gu_load(i + 1)
                    S = SETS[mt % 2]
                    if mt >= 2:
                        P.wait("tensor", "vC", tt_v[(i - 2, 1)])
                    elif mt == 0:
                        P.wait("tensor", "vC", last_dadd[0])
                    else:
                        P.wait("tensor", "vC", last_tt)
                    if mt == 1 and not shared:
                        P.wait("tensor", "sC", wb_v)
                    P.wait("tensor", "sC", gcv)
                    for tb2 in range(2):
                        for kt in range(NKT):
                            gm = P.top(lambda q, kt=kt, tb2=tb2, wgb=wgb, t0=t0, S=S: q.matmul(
                                pb[S[tb2]].ap(), wgb.ap()[:, 128 * kt:128 * kt + 128],
                                h_ffn.ap()[:, TOK * kt + t0 + 512 * tb2:
                                           TOK * kt + t0 + 512 * tb2 + 512],
                                start=(kt == 0), stop=(kt == NKT - 1)), mark=(kt == NKT - 1))
                        g_mark[i] = gm
                    if mt == 0:
                        P.wait("tensor", "vC", last_dadd[1])
                    P.wait("tensor", "sC", ucv)
                    for tb2 in range(2):
                        for kt in range(NKT):
                            um = P.top(lambda q, kt=kt, tb2=tb2, wub=wub, t0=t0, S=S: q.matmul(
                                pb[S[2 + tb2]].ap(), wub.ap()[:, 128 * kt:128 * kt + 128],
                                h_ffn.ap()[:, TOK * kt + t0 + 512 * tb2:
                                           TOK * kt + t0 + 512 * tb2 + 512],
                                start=(kt == 0), stop=(kt == NKT - 1)), mark=(kt == NKT - 1))
                        u_mark[i] = um
                    if mt == 0 and not shared:
                        P.wait("sync", "tC", wfl_mark)
                        P.dma("sync", wfl.ap(), lg_sb.ap()[e:e + 1, t0:t0 + 1024], "dW1")
                        wflv = P.val("dW1")
                        P.wait("tensor", "dW1", wflv)
                        P.wait("tensor", "sC", sb_gate_free)
                        P.wait("tensor", "vC", last_tt)
                        P.top(lambda q: q.matmul(
                            pb[4].ap(), ones_rf.ap(), wfl.ap()[0:1, 0:512],
                            start=True, stop=True))
                        wfl_mark = P.top(lambda q: q.matmul(
                            pb[5].ap(), ones_rf.ap(), wfl.ap()[0:1, 512:1024],
                            start=True, stop=True), mark=True)
                        P.wait("scalar", "tC", wfl_mark)
                        P.wait("scalar", "gC", din_v)
                        P.cop("scalar", lambda q: q.activation(
                            wB_sb.ap()[:, 0:512], pb[4].ap(), AF.Copy))
                        wb_v = P.cop("scalar", lambda q: q.activation(
                            wB_sb.ap()[:, 512:1024], pb[5].ap(), AF.Copy))
                    for tb2 in range(2):
                        P.wait("scalar", "tC", g_mark[i])
                        if i >= 1:
                            P.wait("scalar", "vC", tt_v.get((i - 1, tb2), 0))
                        sg = P.cop("scalar", lambda q, S=S, tb2=tb2, sl=sgt2[tb2]: q.activation(
                            sl.ap(), pb[S[tb2]].ap(), AF.Silu))
                        sg_v[(i, tb2)] = sg
                        P.wait("vector", "tC", u_mark[i])
                        P.wait("vector", "sC", sg)
                        if i >= 1:
                            P.wait("vector", "gC", din_g.get((i - 1, tb2), 0))
                        tt = P.cop("vector", lambda q, S=S, tb2=tb2, sl=t1b2[tb2]: q.tensor_tensor(
                            sl.ap(), pb[S[2 + tb2]].ap(), sgt2[tb2].ap(), OP.mult))
                        tt_v[(i, tb2)] = tt
                        P.wait("gpsimd", "vC", tt)
                        if mt == 0 and tb2 == 0:
                            P.wait("gpsimd", "tC", prev_down_mark)
                            if not shared:
                                P.wait("gpsimd", "sC", wb_v)
                        if shared:
                            dg = P.cop("gpsimd", lambda q, mt=mt, tb2=tb2: q.tensor_copy(
                                din.ap()[:, 1024 * mt + 512 * tb2: 1024 * mt + 512 * tb2 + 512],
                                t1b2[tb2].ap()))
                        else:
                            dg = P.cop("gpsimd", lambda q, mt=mt, tb2=tb2: q.tensor_tensor(
                                din.ap()[:, 1024 * mt + 512 * tb2: 1024 * mt + 512 * tb2 + 512],
                                t1b2[tb2].ap(), wB_sb.ap()[:, 512 * tb2:512 * tb2 + 512],
                                OP.mult))
                        din_g[(i, tb2)] = dg
                    if mt == 7:
                        sb_gate_free = sg_v[(i, 1)]
                        last_tt = tt_v[(i, 1)]
                din_v = P.val("gC")
                for md in range(16):
                    j = 144 * hf + 16 * e + md
                    wdb, dcv = d_next
                    if j + 1 < len(d_seq):
                        d_next = d_load(j + 1)
                    PA = [0, 1] if md % 2 == 0 else [2, 3]
                    P.wait("tensor", "sC", dcv)
                    if md == 0:
                        P.wait("tensor", "gC", din_v)
                        P.wait("tensor", "vC", last_tt)
                        P.wait("tensor", "sC", sb_gate_free)
                    elif md >= 2:
                        P.wait("tensor", "vC", dadd_v[j - 2])
                    for b2 in range(2):
                        for kf in range(8):
                            dm = P.top(lambda q, kf=kf, b2=b2, wdb=wdb, PA=PA: q.matmul(
                                pb[PA[b2]].ap(), wdb.ap()[:, 128 * kf:128 * kf + 128],
                                din.ap()[:, 1024 * kf + 512 * b2: 1024 * kf + 512 * b2 + 512],
                                start=(kf == 0), stop=(kf == 7)), mark=(kf == 7))
                        dmark[j] = dm
                    P.wait("vector", "tC", dmark[j])
                    for b2 in range(2):
                        sl = (1024 * md + 512 * b2, 1024 * md + 512 * b2 + 512)
                        if e == 0:
                            da = P.cop("vector", lambda q, sl=sl, PA=PA, b2=b2: q.tensor_copy(
                                dacc.ap()[:, sl[0]:sl[1]], pb[PA[b2]].ap()))
                        else:
                            da = P.cop("vector", lambda q, sl=sl, PA=PA, b2=b2: q.tensor_tensor(
                                dacc.ap()[:, sl[0]:sl[1]], dacc.ap()[:, sl[0]:sl[1]],
                                pb[PA[b2]].ap(), OP.add))
                    dadd_v[j] = da
                    prev_down_mark = dmark[j]
                jb = 144 * hf + 16 * e
                last_dadd = [dadd_v[jb + 14], dadd_v[jb + 15]]
            for md in range(16):
                P.wait("vector", "dFf", P.val("dFf"))
                fc_v = P.cop("vector", lambda q, md=md: q.tensor_copy(
                    fstg.ap(), dacc.ap()[:, 1024 * md:1024 * md + 1024]))
                P.wait("sync", "vC", fc_v)
                P.dma("sync", ffn_d.ap()[128 * md:128 * md + 128, t0:t0 + 1024], fstg.ap(), "dFf")
                P.wait("sync", "dFf", P.val("dFf"))
        st_e.close()

        dff_final = P.val("dFf")
        fence()
        st_c = contextlib.ExitStack()
        SBE = lambda *a: st_c.enter_context(nc.sbuf_tensor(*a))
        xnh = SBE("xnh", [128, NKT * TOK], BF16)
        u_c = SBE("u_sbC", [128, NKT * 256], F32)
        ubf_c = SBE("ubf_c", [128, NKT * 256], BF16)
        usq_c = SBE("usq_c", [128, NKT * 256], BF16)
        gt = [SBE(f"gt{i}", [128, 1024], BF16) for i in range(4)]
        ysel2 = [SBE(f"ysels{i}", [128, 512], BF16) for i in range(2)]
        szt2 = [SBE(f"szts{i}", [128, 512], BF16) for i in range(2)]
        gatb2 = [SBE(f"gatbs{i}", [128, 512], BF16) for i in range(2)]
        sqt4 = [SBE(f"sqts{i}", [128, 512], BF16) for i in range(4)]
        sums_sb = SBE("sums_sb", [1, TOK], F32)
        ws_z = WStream("wsz", 2048, "dW2", None)

        hv = 0
        for cb in range(8):
            hv = ln_block(cb, xnh, 0, False, hv, u_c, ubf_c, usq_c)
        xnh_v = hv
        zs_v = P.cop("vector", lambda q: q.memset(sums_sb.ap(), 0.0))

        SETS = [[0, 1, 2, 3], [4, 5, 6, 7]]
        Wz3 = Wz_d.rearrange("(kt p) z -> p kt z", p=128)
        NIT = 128

        gld_v = {}
        bld_v = {}
        gat_v = {}
        sqq_v = {}
        dga_v = {}
        qmark = {}
        sumadd = {}
        silu_vv = {}
        zmark = {}

        def issue_g(k):
            mt_, tb_ = divmod(k, 4)
            r_, chp_ = divmod(mt_, 16)
            row = 2 * D * r_ + 128 * chp_
            if k >= 4:
                P.wait("gpsimd", "vC", bld_v[k - 4])
            P.dma("gpsimd", gt[k % 4].ap()[:, 0:512],
                  G_d.ap()[row: row + 128, 512 * tb_: 512 * tb_ + 512], "dG")
            P.dma("gpsimd", gt[k % 4].ap()[:, 512:1024],
                  G_d.ap()[row + D: row + D + 128, 512 * tb_: 512 * tb_ + 512], "dG")
            gld_v[k] = P.val("dG")

        if stop_after is None:
            P.wait("gpsimd", "gC", ag_v)
        for k in range(3):
            issue_g(k)

        wz_cur = ws_z.load(Wz3[:, :, 0:128], cons_tC=0)
        for mt in range(32):
            wzb, zcv = wz_cur
            if mt < 31:
                wz_cur = ws_z.load(Wz3[:, :, 128 * (mt + 1):128 * (mt + 1) + 128],
                                   cons_tC=zmark.get(4 * (mt - 1) + 3, 0))
            S = SETS[mt % 2]
            P.wait("tensor", "sC", zcv)
            if mt == 0:
                P.wait("tensor", "vC", xnh_v)
            for tb in range(4):
                k = 4 * mt + tb
                if mt >= 2:
                    P.wait("tensor", "vC", sumadd[k - 8])
                    P.wait("tensor", "sC", silu_vv[k - 8])
                for kt in range(NKT):
                    zm = P.top(lambda q, kt=kt, tb=tb, wzb=wzb, S=S: q.matmul(
                        pb[S[tb]].ap(), wzb.ap()[:, 128 * kt:128 * kt + 128],
                        xnh.ap()[:, TOK * kt + 512 * tb: TOK * kt + 512 * tb + 512],
                        start=(kt == 0), stop=(kt == NKT - 1)), mark=(kt == NKT - 1))
                zmark[k] = zm
            def flush_q(kq):
                mtq, tbq = divmod(kq, 4)
                pbq = pb[SETS[mtq % 2][tbq]]
                P.wait("tensor", "sC", sqq_v[kq])
                qm = P.top(lambda q, pbq=pbq, sl=sqt4[kq % 4]: q.matmul(
                    pbq.ap()[0:1, 0:512], ones_cb.ap(), sl.ap(), start=True, stop=True),
                    mark=True)
                qmark[kq] = qm
                P.wait("vector", "tC", qm)
                if kq == 0:
                    P.wait("vector", "vC", zs_v)
                sumadd[kq] = P.cop("vector", lambda q, tbq=tbq, pbq=pbq: q.tensor_tensor(
                    sums_sb.ap()[0:1, 512 * tbq:512 * tbq + 512],
                    sums_sb.ap()[0:1, 512 * tbq:512 * tbq + 512],
                    pbq.ap()[0:1, 0:512], OP.add))

            for tb in range(4):
                k = 4 * mt + tb
                pbank = pb[S[tb]]
                if k >= 4:
                    flush_q(k - 4)
                if k + 3 < NIT:
                    issue_g(k + 3)
                P.wait("scalar", "tC", zmark[k])
                if k >= 2:
                    P.wait("scalar", "vC", gat_v[k - 2])
                sz = P.cop("scalar", lambda q, pbank=pbank, sl=szt2[tb % 2]: q.activation(
                    sl.ap(), pbank.ap(), AF.Silu))
                silu_vv[k] = sz
                P.wait("vector", "dG", gld_v[k])
                P.cop("vector", lambda q, kk=k, sl=ysel2[tb % 2]: q.tensor_scalar(
                    sl.ap(), gt[kk % 4].ap()[:, 0:512], selm_sb.ap()[:, 0:1], None, OP.mult))
                bl = P.cop("vector", lambda q, kk=k, sl=ysel2[tb % 2]: q.scalar_tensor_tensor(
                    sl.ap(), gt[kk % 4].ap()[:, 512:1024], selm_sb.ap()[:, 1:2],
                    sl.ap(), OP.mult, OP.add))
                bld_v[k] = bl
                P.wait("vector", "sC", sz)
                if k >= 2:
                    P.wait("vector", "dGa", dga_v[k - 2])
                    P.wait("vector", "sC", sqq_v[k - 2])
                gv2 = P.cop("vector", lambda q, sl=gatb2[tb % 2], ys=ysel2[tb % 2],
                            sz2=szt2[tb % 2]:
                            q.tensor_tensor(sl.ap(), ys.ap(), sz2.ap(), OP.mult))
                gat_v[k] = gv2
                P.wait("scalar", "vC", gv2)
                P.dma("scalar", gat_d.ap()[128 * mt:128 * mt + 128, 512 * tb:512 * tb + 512],
                      gatb2[tb % 2].ap(), "dGa")
                dga_v[k] = P.val("dGa")
                if k >= 4:
                    P.wait("scalar", "tC", qmark[k - 4])
                sq = P.cop("scalar", lambda q, sl=sqt4[k % 4], g=gatb2[tb % 2]: q.activation(
                    sl.ap(), g.ap(), AF.Square))
                sqq_v[k] = sq
        for kq in range(NIT - 4, NIT):
            flush_q(kq)
        sums_v = P.val("vC")
        dga_final = P.val("dGa")
        st_c.close()

        fence()
        st_d = contextlib.ExitStack()
        SBE = lambda *a: st_d.enter_context(nc.sbuf_tensor(*a))
        ynorm = SBE("ynorm", [128, 32 * TOK], BF16)
        rstdB = SBE("rstdB", [128, TOK], F32)
        row4 = SBE("row4", [1, TOK], F32)
        gstg = [SBE(f"gstg{i}", [128, TOK], BF16) for i in range(2)]
        ustg = [SBE(f"ustg{i}", [128, 512], F32) for i in range(2)]
        ffstg = [SBE(f"ffstg{i}", [128, 512], BF16) for i in range(2)]
        otb = [SBE(f"otb{i}", [128, 512], F32) for i in range(2)]
        ws_o = WStream("wso", 4096, "dW2", None, nkt=32)

        P.wait("vector", "vC", sums_v)
        P.cop("vector", lambda q: q.tensor_scalar_mul(row4.ap(), sums_sb.ap(), 1.0 / DIN))
        rw4 = P.val("vC")
        P.wait("scalar", "vC", rw4)
        sq4 = P.cop("scalar", lambda q: q.activation(row4.ap(), row4.ap(), AF.Sqrt, bias=MRMS_EPS))
        P.wait("vector", "sC", sq4)
        rc4 = P.cop("vector", lambda q: q.reciprocal(row4.ap(), row4.ap()))
        P.wait("tensor", "vC", rc4)
        for tb in range(4):
            P.wait("tensor", "sC", P.val("sC"))
            bmm = P.top(lambda q, tb=tb: q.matmul(
                pb[4].ap(), ones_rf.ap(), row4.ap()[0:1, 512 * tb:512 * tb + 512],
                start=True, stop=True), mark=True)
            P.wait("scalar", "tC", bmm)
            P.cop("scalar", lambda q, tb=tb: q.activation(
                rstdB.ap()[:, 512 * tb:512 * tb + 512], pb[4].ap(), AF.Copy))
        rstdB_v = P.val("sC")

        P.wait("gpsimd", "dGa", dga_final)
        gld2 = {}
        yv_k = {}

        def issue_gat(kt):
            if kt >= 2:
                P.wait("gpsimd", "vC", yv_k[kt - 2])
            P.dma("gpsimd", gstg[kt % 2].ap(), gat_d.ap()[128 * kt:128 * kt + 128, :], "dGs")
            gld2[kt] = P.val("dGs")

        issue_gat(0)
        issue_gat(1)
        yv = 0
        for kt in range(32):
            P.wait("vector", "dGs", gld2[kt])
            if kt == 0:
                P.wait("vector", "sC", rstdB_v)
            yv = P.cop("vector", lambda q, kt=kt: q.scalar_tensor_tensor(
                ynorm.ap()[:, TOK * kt:TOK * kt + TOK], gstg[kt % 2].ap(),
                nrmw_sb.ap()[:, kt:kt + 1], rstdB.ap(), OP.mult, OP.mult))
            yv_k[kt] = yv
            if kt + 2 < 32:
                issue_gat(kt + 2)
        yn_v = yv

        P.wait("scalar", "dFf", dff_final)
        uTh3b = uTh.rearrange("(mo p) T -> p mo T", p=128)
        uld = {}
        tv1_v = {}
        ffadd_v = {}
        dout_v = {}
        omark = {}

        def issue_uf(k):
            mo_, tb_ = divmod(k, 4)
            if k >= 2:
                P.wait("scalar", "vC", tv1_v[k - 2])
                P.wait("scalar", "gC", ffadd_v[k - 2])
            P.dma("scalar", ustg[k % 2].ap(), uTh3b[:, mo_, 512 * tb_:512 * tb_ + 512], "dU2")
            P.dma("scalar", ffstg[k % 2].ap(),
                  ffn_d.ap()[128 * mo_:128 * mo_ + 128, 512 * tb_:512 * tb_ + 512], "dU2")
            uld[k] = P.val("dU2")

        issue_uf(0)
        issue_uf(1)

        Wo3 = Wo_d.rearrange("(kt p) d -> p kt d", p=128)
        wo_cur = ws_o.load(Wo3[:, :, 0:128], cons_tC=0)
        for mo in range(16):
            wob, ocv = wo_cur
            if mo < 15:
                wo_cur = ws_o.load(Wo3[:, :, 128 * (mo + 1):128 * (mo + 1) + 128],
                                   cons_tC=omark.get(4 * (mo - 1) + 3, 0))
            S = SETS[mo % 2]
            P.wait("tensor", "sC", ocv)
            if mo == 0:
                P.wait("tensor", "vC", yn_v)
            for tb in range(4):
                k = 4 * mo + tb
                if mo >= 2:
                    P.wait("tensor", "vC", tv1_v[k - 8])
                for kt in range(32):
                    om = P.top(lambda q, kt=kt, tb=tb, wob=wob, S=S: q.matmul(
                        pb[S[tb]].ap(), wob.ap()[:, 128 * kt:128 * kt + 128],
                        ynorm.ap()[:, TOK * kt + 512 * tb: TOK * kt + 512 * tb + 512],
                        start=(kt == 0), stop=(kt == 31)), mark=(kt == 31))
                omark[k] = om
            for tb in range(4):
                k = 4 * mo + tb
                pbank = pb[S[tb]]
                P.wait("vector", "tC", omark[k])
                P.wait("vector", "dU2", uld[k])
                if k >= 2:
                    P.wait("vector", "dOut", dout_v[k - 2])
                tv = P.cop("vector", lambda q, pbank=pbank, sl=otb[k % 2], us=ustg[k % 2]:
                           q.tensor_tensor(sl.ap(), pbank.ap(), us.ap(), OP.add))
                tv1_v[k] = tv
                P.wait("gpsimd", "vC", tv)
                ov = P.cop("gpsimd", lambda q, sl=otb[k % 2], fs=ffstg[k % 2]:
                           q.tensor_tensor(sl.ap(), sl.ap(), fs.ap(), OP.add))
                ffadd_v[k] = ov
                P.wait("gpsimd", "gC", ov)
                P.dma("gpsimd", out_d[128 * mo:128 * mo + 128, 512 * tb:512 * tb + 512],
                      otb[k % 2].ap(), "dOut")
                dout_v[k] = P.val("dOut")
                if k + 2 < 64:
                    issue_uf(k + 2)
        st_d.close()
        P.wait("gpsimd", "dOut", P.val("dOut"))
        P.wait("scalar", "dOut", P.val("dOut"))
        P.wait("scalar", "dStF", P.val("dStF"))
        P.wait("scalar", "dGa", P.val("dGa"))
        P.wait("scalar", "dFf", P.val("dFf"))

    with nc.Block() as block:
        @block.sync
        def _(q):
            for f in P.streams["sync"]:
                f(q)

        @block.scalar
        def _(q):
            for f in P.streams["scalar"]:
                f(q)

        @block.vector
        def _(q):
            for f in P.streams["vector"]:
                f(q)

        @block.tensor
        def _(q):
            for f in P.streams["tensor"]:
                f(q)

        @block.gpsimd
        def _(q):
            for f in P.streams["gpsimd"]:
                f(q)

    return nc, dbg_aps


def kernel(**inputs):
    in_maps = host_layout(inputs)
    nc, _ = build_program()
    res = run_bass_kernel_spmd(nc, in_maps, core_ids=list(range(8)))
    out = np.zeros((NB, LQ, D), dtype=np.float32)
    for c in range(8):
        b, s = c // 2, c % 2
        out[b, TOK * s:TOK * s + TOK, :] = res.results[c]["out"].T
    return out
